# version 01
# baseline (speedup 1.0000x reference)
import os as _os

import ml_dtypes
import numpy as np

import concourse.bass as bass
import concourse.mybir as mybir
from concourse.bass_utils import run_bass_kernel_spmd

F32 = mybir.dt.float32
BF16 = mybir.dt.bfloat16
AF = mybir.ActivationFunctionType
ALU = mybir.AluOpType
AX = mybir.AxisListType.X

T = 4096
E = 1024
I = 2816
NP = 8
NTT = 32
NTC = 8
NICH = 22
MAGIC = 1.5 * 2 ** 23
QMAX = 127.0
EPS = 1e-5
LN2 = float(np.log(2.0))

STAGE = 3
_NOMM = _os.environ.get("K_NOMM", "0") == "1"
_NOSILU = _os.environ.get("K_NOSILU", "0") == "1"
_NOTR = _os.environ.get("K_NOTR", "0") == "1"
_MMONLY = _os.environ.get("K_MMONLY", "0") == "1"
_DUMP = _os.environ.get("K_DUMP", "0") == "1"
_RACE = _os.environ.get("K_RACE", "0") == "1"


class Sem:
    def __init__(self, nc, name):
        self.h = nc.alloc_semaphore(name)
        self.n = 0


class Prog:
    def __init__(self, nc):
        self.nc = nc
        self.items = {k: [] for k in "tvsgy"}
        self.last = {}

    def flush(self):
        nc, items = self.nc, self.items
        self.items = {k: [] for k in "tvsgy"}
        with nc.Block() as blk:
            for key, reg in (("y", blk.sync), ("g", blk.gpsimd),
                             ("t", blk.tensor), ("v", blk.vector),
                             ("s", blk.scalar)):
                if items[key]:
                    def make(fs):
                        def run(eng):
                            for f in fs:
                                f(eng)
                        return run
                    reg(make(items[key]))


class Alloc:
    def __init__(self, nc, limit=None):
        self.nc = nc
        self.cur = (nc.sbuf_base + 63) & ~63
        self.limit = nc.sbuf_top - 64 if limit is None else limit
        self.peak = 0
        self.cnt = 0

    def mark(self):
        return self.cur

    def reset(self, m):
        self.cur = m

    def buf(self, shape, dtype, name=None):
        nbytes = int(np.prod(shape[1:])) * mybir.dt.size(dtype)
        nbytes = (nbytes + 63) & ~63
        off = self.cur
        self.cur += nbytes
        self.peak = max(self.peak, self.cur)
        assert self.cur <= self.limit, f"SBUF overflow at {name}: {self.cur}"
        self.cnt += 1
        return self.nc.alloc_sbuf_tensor_at(name or f"b{self.cnt}",
                                            list(shape), dtype, offset=off)


def build_program(stage=STAGE):
    nc = bass.Bass("TRN2", target_bir_lowering=False,
                   detect_race_conditions=_RACE)
    P = Prog(nc)
    al = Alloc(nc)
    sems = {}

    ALIAS = {}
    for n in ("fsvA fsvSc fsvB fsvD fsvW3 fsvH fsstore asv2 asvhead asgn "
              "asmrg aoggate asvones ap4free ap5free avbr p6f p7f sv0 "
              "fwf fwTf fwdf fwdtf fr2f0 fr2f1 fr2f2 frf0 frf1 frf2 fntr0 "
              "fntr1 fsltf0 fsltf1 fhr0 fhr1 fhr2 fh2f0 fh2f1 fh2f2 "
              "avbr0 avbr1 "
              "fxrf0 fxrf1 fwready fdready anbr0 anbr1 axbf0 axbf1 "
              "fwf0 fwf1 fwTf0 fwTf1 aginf0 aginf1 agonr0 agonr1 "
              "avm").split():
        ALIAS[n] = "VM"
    for n in ("fssA ssM ss0 ass2 fssil0 fssil1 assf0 assf1 atrs agsr0 "
              "agsr1 ashead asm").split():
        ALIAS[n] = "SM"
    for n in ("fstA st0 ast2 fstW fstW2 fmm_done fmmC_done fxqf0 fxqf1 "
              "aavp aavf assr0 assr1 aatf0 aatf1 appr0 appr1 fnhfr0 fnhfr1 "
              "agpr0 agpr1 asQf atm").split():
        ALIAS[n] = "TM"
    for n in "fscc agm".split():
        ALIAS[n] = "GM"
    for i in range(22):
        ALIAS[f"fnhr{i}"] = "VM"
    for a, b in (("frl0", "ra0"), ("frl1", "ra1"), ("frl2", "ra2"),
                 ("fr2l0", "ra0"), ("fr2l1", "ra1"), ("fr2l2", "ra2"),
                 ("axbl0", "ra0"), ("axbl1", "ra1"),
                 ("fxrl0", "ra0"), ("fxrl1", "ra1"),
                 ("px1l0", "ra0"), ("px1l1", "ra1"),
                 ("fxql0", "yb0"), ("fxql1", "yb1"),
                 ("fntf0", "yc0"), ("fntf1", "yc1"),
                 ("anbf0", "yc0"), ("anbf1", "yc1"),
                 ("fx1fr0", "yd0"), ("fx1fr1", "yd1"),
                 ("px2fr0", "yd0"), ("px2fr1", "yd1"),
                 ("fdst_done", "ydt"), ("px2done", "ydt"),
                 ("fh2l0", "ye0"), ("fh2l1", "ye1"), ("fh2l2", "ye2"),
                 ("pg8l0", "sgi0"), ("pg8l1", "sgi1"),
                 ("aginl0", "sgi0"), ("aginl1", "sgi1"),
                 ("agonf0", "sgo0"), ("agonf1", "sgo1"),
                 ("asgonst", "sgot"),
                 ("agsf0", "sgg0"), ("agsf1", "sgg1"),
                 ("fswq", "yw"), ("fswq2", "yw"), ("fwdl", "yw"),
                 ("fwdtl", "yw"), ("fsdq2", "yw"),
                 ("fwl0", "ywa"), ("fwl1", "ywb"),
                 ("fwTl0", "ywc"), ("fwTl1", "ywd"),
                 ("axlr", "yx"), ("ascs", "yx"),
                 ("fhfree0", "sf0"), ("fhfree1", "sf1"), ("fhfree2", "sf2"),
                 ("fhidall", "sft"),
                 ("avtr", "svt"), ("avbfree0", "svb0"), ("avbfree1", "svb1"),
                 ("aswl", "sgw"), ("aswqkv", "sgw"), ("pswo", "sgw"),
                 ("fsgd", "sgd"), ("fsspl", "sgsp"),
                 ("pg8f0", "TM"), ("pg8f1", "TM"),
                 ("p5stat", "VM"), ("asxm", "SM"),
                 ("ppof0", "VM"), ("ppof1", "VM"), ("ppof2", "VM"),
                 ("ppof3", "VM"),
                 ("appf0", "VM"), ("appf1", "VM"),
                 ("agpf0", "SM"), ("agpf1", "SM"),
                 ("fpgr0", "TM"), ("fpgr1", "TM"), ("fpgr2", "TM"),
                 ("fpgr3", "TM"),
                 ("fpdr0", "TM"), ("fpdr1", "TM"), ("fpdr2", "TM"),
                 ("fpdr3", "TM"),
                 ("ppor0", "TM"), ("ppor1", "TM"), ("ppor2", "TM"),
                 ("ppor3", "TM"),
                 ("aatr0", "atrs"), ("aatr1", "atrs"),
                 ("px2fr0", "yd0"), ("px2fr1", "yd1")):
        ALIAS[a] = b
    for n in ("fpgf0", "fpgf2"):
        ALIAS[n] = "SM"
    for n in ("fpgf1", "fpgf3"):
        ALIAS[n] = "VM"
    for n in ("fpdf0", "fpdf1", "fpdf2", "fpdf3"):
        ALIAS[n] = "VM"
    for n in ("ppof0", "ppof1", "ppof2", "ppof3"):
        ALIAS[n] = "VM"
    for n in ("px1f0", "px1f1"):
        ALIAS[n] = "VM"

    def S(name):
        cn = name
        seen = set()
        while cn in ALIAS and cn not in seen:
            seen.add(cn)
            cn = ALIAS[cn]
        if cn not in sems:
            sems[cn] = Sem(nc, cn)
        return sems[cn]

    MILE = {"v": S("VM"), "s": S("SM"), "t": S("TM"), "g": S("GM")}

    def dma(q, out, in_, ss, transpose=False, slow=False):
        if isinstance(ss, Sem):
            ss = (ss,)
        merged = {}
        order = []
        for sm in ss:
            sm.n += 16
            if id(sm.h) not in merged:
                merged[id(sm.h)] = [sm.h, 0]
                order.append(id(sm.h))
            merged[id(sm.h)][1] += 16
        hs = [tuple(merged[i]) for i in order]
        assert len(hs) <= 1, "HW: a DMA may carry only one then_inc"

        def run(e, o=out, i=in_, hh=tuple(hs), tr=transpose, sl=slow):
            ins = e.dma_start(out=o, in_=i, transpose=tr,
                              allow_slow_non_contiguous=sl)
            for h, k in hh:
                ins = ins.then_inc(h, k)
        P.items[q].append(run)

    def wait(q, sem, tgt=None):
        v = sem.n if tgt is None else tgt
        if v > 0:
            P.items[q].append(lambda e, s=sem.h, vv=v: e.wait_ge(s, vv))

    def emit(q, f, incs=(), ctl=False):
        if isinstance(incs, Sem):
            incs = ((incs, 1),)
        elif (isinstance(incs, tuple) and len(incs) == 2
              and isinstance(incs[0], Sem) and isinstance(incs[1], int)):
            incs = (incs,)
        merged = {}
        order = []
        for it in incs:
            sm, k = it if isinstance(it, tuple) else (it, 1)
            sm.n += k
            if id(sm.h) not in merged:
                merged[id(sm.h)] = [sm.h, 0]
                order.append(id(sm.h))
            merged[id(sm.h)][1] += k
        if not merged and not ctl:
            vm = MILE[q]
            vm.n += 1
            merged[id(vm.h)] = [vm.h, 1]
            order.append(id(vm.h))
            incs = (vm,)
        pairs = [list(merged[i]) for i in order]
        recs = [(sm, sm.n) for sm in
                {id(it if isinstance(it, Sem) else it[0]):
                 (it if isinstance(it, Sem) else it[0])
                 for it in incs}.values()]

        def run(e, ff=f, pp=pairs):
            ins = ff(e)
            for hh, kk in pp:
                ins = ins.then_inc(hh, kk)
        P.items[q].append(run)
        if not ctl:
            P.last[q] = (pairs, recs)

    def din(name, shape, dtype=F32):
        return nc.dram_tensor(name, list(shape), dtype, kind="ExternalInput").ap()

    def dsc(name, shape, dtype=F32):
        kind = "Internal"
        if _DUMP and name in ("hid_sc", "x1_sc"):
            kind = "ExternalOutput"
        return nc.dram_tensor(name, list(shape), dtype, kind=kind).ap()

    x_in = din("x", [T, E])
    wg_n = din("wg_n", [I, E])
    wu_n = din("wu_n", [I, E])
    wgT_i = din("wgT", [E, I])
    wuT_i = din("wuT", [E, I])
    wd_n = din("wd_n", [E, I])
    wdT_i = din("wdT", [I, E])
    wqT_i = din("wqT", [E, E])
    wkT_i = din("wkT", [E, E])
    wvT_i = din("wvT", [E, E])
    wgaT_i = din("wgaT", [E, E])
    woT_i = din("woT", [E, E])
    wdec_i = din("wdec", [E, 16])
    bfm_i = din("bfm", [128, 48])
    bo_i = din("bo_r", [1, E])
    bd_i = din("bd_r", [1, 16])
    clip_i = din("clip2", [1, 2])
    cos_i = din("cosfm", [128, T])
    sin_i = din("sinmfm", [128, T])
    relum_i = din("relum", [128, 2048])
    qexp_i = din("qexp", [128, 512])
    fbias_i = din("fbias", [128, 28])
    ones_i = din("onesr", [1, 128])
    maskA_i = din("maskA", [1, 128])
    maskB_i = din("maskB", [1, 128])
    ident_i = din("identf", [128, 128])
    identb_i = din("identb", [128, 128], BF16)
    cv_i = din("constv", [128, 4])

    out_t = nc.dram_tensor("out", [T, E], F32, kind="ExternalOutput").ap()

    x1_sc = dsc("x1_sc", [T, E])
    x2_sc = dsc("x2_sc", [T, E])
    hid_sc = dsc("hid_sc", [NICH, NTC, 128, 512])
    gate_sc = dsc("gate_sc", [NP, NTC, 128, 512], BF16)
    gon_sc = dsc("gon_sc", [NP, NTC, 128, 512], BF16)
    wspg_sc = dsc("wspg_sc", [8, 128, I], BF16)
    wspu_sc = dsc("wspu_sc", [8, 128, I], BF16)
    wspd_sc = dsc("wspd_sc", [NICH, 128, E], BF16)
    svec_sc = dsc("svec_sc", [1, I])
    cc_in = [dsc(f"cc_in{i}", [1, 1]) for i in range(4)]
    cc_out = [dsc(f"cc_out{i}", [1, 1]) for i in range(4)]

    constv = al.buf([128, 4], F32, "constv")
    COL_EPS, COL_M5LN2 = 0, 1
    onesr = al.buf([1, 128], F32, "onesr")
    maskA = al.buf([1, 128], F32, "maskA")
    maskB = al.buf([1, 128], F32, "maskB")
    identf = al.buf([128, 128], F32, "identf")
    identb = al.buf([128, 128], BF16, "identb")
    bfm = al.buf([128, 48], F32, "bfm")
    CB_Q, CB_K, CB_V, CB_G, CB_GAM, CB_BET = 0, 8, 16, 24, 32, 40
    borep = al.buf([128, E], F32, "borep")
    bdrow = al.buf([1, 16], F32, "bdrow")
    fbias = al.buf([128, 28], F32, "fbias")
    qexp = al.buf([128, 512], F32, "qexp")
    relum = al.buf([128, 2048], F32, "relum")
    clips = al.buf([1, 2], F32, "clips")
    mu_a = al.buf([128, NTT], F32, "mu_a")
    vv_a = al.buf([128, NTT], F32, "vv_a")
    rr_a = al.buf([128, NTT], F32, "rr_a")
    mraw = al.buf([128, NTT], F32, "mraw")
    rq1 = al.buf([128, NTT], F32, "rq1")
    rq2 = al.buf([128, NTT], F32, "rq2")
    stat2 = al.buf([128, 12], F32, "stat2")
    cells = al.buf([1, 32], F32, "cells")
    (C_SIGX, C_SIGD, C_LMAX, C_GMAX, C_SX, C_ISX, C_SH, C_ISH, C_SHHALF,
     C_MUA, C_MUB, C_VA, C_VB, C_RA, C_RB, C_T0, C_T1, C_T2) = range(18)
    rsc = al.buf([128, 8], F32, "rsc")
    R_ISX, R_SX, R_ISH, R_SHH = 0, 1, 2, 3
    sgate = al.buf([128, NICH], F32, "sgate")
    sup = al.buf([128, NICH], F32, "sup")
    svg = al.buf([128, NICH], F32, "svg")
    svu = al.buf([128, NICH], F32, "svu")
    ldrow = al.buf([1, 16], F32, "ldrow")
    ldb = al.buf([128, 16], F32, "ldb")
    srow = al.buf([1, 512], F32, "srow")
    srow2 = al.buf([1, 512], F32, "srow2")
    wdTb = al.buf([128, 128], F32, "wdTb")
    mean8 = al.buf([128, 8], F32, "mean8")
    bnb = al.buf([128, 48], F32, "bnb")
    bo_row = al.buf([1, E], F32, "bo_row")
    rvec = al.buf([128, 1], F32, "rvec")
    mvec = al.buf([128, 1], F32, "mvec")
    Avec = al.buf([128, 1], F32, "Avec")
    Bvec = al.buf([128, 1], F32, "Bvec")
    tvec = al.buf([128, 1], F32, "tvec")

    W0 = al.mark()
    WBYTES = 98304
    al.cur = W0 + WBYTES

    def wbuf(off, shape, dtype, name):
        nb = int(np.prod(shape[1:])) * mybir.dt.size(dtype)
        assert off + nb <= WBYTES, name
        return nc.alloc_sbuf_tensor_at(name, list(shape), dtype, offset=W0 + off)

    mgT = wbuf(0, [128, 8, I], BF16, "mgT")
    muT = wbuf(45056, [128, 8, I], BF16, "muT")
    mdT = wbuf(0, [128, NICH, E], BF16, "mdT")
    cosfm = wbuf(0, [128, T], F32, "cosfm")
    sinmfm = wbuf(16384, [128, T], F32, "sinmfm")
    xln2fm = wbuf(32768, [128, 8, T], BF16, "xln2fm")
    woTb = wbuf(0, [128, 8, E], BF16, "woTb")

    S0 = al.mark()
    pb = [nc.alloc_psum_tensor(f"pb{i}", [128, 512], F32) for i in range(8)]
    p6f = S("p6f")
    p7f = S("p7f")

    def waitr(q, rec):
        if rec is not None:
            wait(q, rec[0], rec[1])

    def load_lnexp(e):
        import concourse.mybir as _mb
        return e.add_instruction(_mb.InstLoadActFuncSet(
            name=e.bass.get_next_instruction_name(),
            act_func_set_id=6, ins=[], outs=[]))

    class Gate:
        def __init__(self):
            self.full = []
            self.per = {}

    def wait_gate(q, g):
        for sm, n in g.full:
            wait(q, sm, n)

    def chain(q):
        le = P.last.get(q)
        if le is None:
            return None
        pairs, recs = le
        if not recs:
            vm = MILE[q]
            vm.n += 1
            pairs.append([vm.h, 1])
            recs = [(vm, vm.n)]
            P.last[q] = (pairs, recs)
        return recs[0]

    def sep(q):
        waitr(q, chain(q))

    trec = {}

    def pe_tr(dst_ap, src_ap, bank, ident_ap, np_, nf):
        TMs, VMs = S("atm"), S("avm")
        waitr("t", trec.get(bank))
        emit("t", lambda e: e.matmul(pb[bank][0:nf, 0:np_], src_ap, ident_ap,
                                     start=True, stop=True), TMs)
        waitr("v", (TMs, TMs.n))
        emit("v", lambda e: e.tensor_copy(dst_ap, pb[bank][0:nf, 0:np_]), VMs)
        trec[bank] = (VMs, VMs.n)
        return (VMs, VMs.n)

    def rsqrt_batch(dst, src, n, sv, ss):
        wait("s", sv)
        sep("s")
        emit("s", lambda e: e.activation(rq1[:, 0:n], src, AF.Ln,
                                         bias=constv[:, COL_EPS:COL_EPS + 1]))
        sep("s")
        emit("s", lambda e: e.activation(dst, rq1[:, 0:n], AF.Exp, scale=-0.5), ss)
        wait("v", ss)
        sep("v")
        emit("v", lambda e: e.tensor_scalar_add(rq1[:, 0:n], src, EPS))
        emit("v", lambda e: e.tensor_mul(rq2[:, 0:n], dst, dst))
        sep("v")
        emit("v", lambda e: e.tensor_mul(rq2[:, 0:n], rq2[:, 0:n], rq1[:, 0:n]))
        sep("v")
        emit("v", lambda e: e.tensor_scalar(rq2[:, 0:n], rq2[:, 0:n], -0.5, 1.5,
                                            ALU.mult, ALU.add))
        sep("v")
        emit("v", lambda e: e.tensor_mul(dst, dst, rq2[:, 0:n]), sv)

    def rsqrt_row(dst, src, t1, t2, sv, ss):
        wait("s", sv)
        sep("s")
        emit("s", lambda e: e.activation(t1, src, AF.Ln,
                                         bias=constv[0:1, COL_EPS:COL_EPS + 1]))
        sep("s")
        emit("s", lambda e: e.activation(dst, t1, AF.Exp, scale=-0.5), ss)
        wait("v", ss)
        sep("v")
        emit("v", lambda e: e.tensor_scalar_add(t1, src, EPS))
        emit("v", lambda e: e.tensor_mul(t2, dst, dst))
        sep("v")
        emit("v", lambda e: e.tensor_mul(t2, t2, t1))
        sep("v")
        emit("v", lambda e: e.tensor_scalar(t2, t2, -0.5, 1.5, ALU.mult, ALU.add))
        sep("v")
        emit("v", lambda e: e.tensor_mul(dst, dst, t2), sv)

    def bcast_cell(cell_col, rsc_col, sv, st):
        wait("t", p6f)
        emit("t", lambda e: e.matmul(pb[6][0:128, 0:1], onesr[0:1, :],
                                     cells[0:1, cell_col:cell_col + 1],
                                     start=True, stop=True), st)
        wait("v", st)
        emit("v", lambda e: e.tensor_copy(rsc[:, rsc_col:rsc_col + 1],
                                          pb[6][0:128, 0:1]), (sv, p6f))

    def cross_max_to_cell(src_ap, cell_col, sv, st):
        wait("t", sv)
        wait("t", p6f)
        emit("t", lambda e: e.matmul(pb[6][0:1, 0:128], src_ap, identf[:, :],
                                     start=True, stop=True), st)
        wait("v", st)
        emit("v", lambda e: e.reduce_max(cells[0:1, cell_col:cell_col + 1],
                                         pb[6][0:1, 0:128], axis=AX), (sv, p6f))

    def collective(idx, c_src, c_dst, sv, sgd, scc):
        wait("g", sv)
        dma("g", cc_in[idx], cells[0:1, c_src:c_src + 1], sgd)
        wait("g", sgd)
        emit("g", lambda e: e.collective_compute(
            "AllReduce", ALU.max, replica_groups=[list(range(8))],
            ins=[cc_in[idx]], outs=[cc_out[idx]]), scc)
        wait("g", scc)
        dma("g", cells[0:1, c_dst:c_dst + 1], cc_out[idx], sgd)
        wait("v", sgd)

    sc_pre = S("sc_pre")
    for dst, src in ((constv[:, :], cv_i), (onesr[:, :], ones_i),
                     (maskA[:, :], maskA_i), (maskB[:, :], maskB_i),
                     (identf[:, :], ident_i), (identb[:, :], identb_i),
                     (bfm[:, :], bfm_i),
                     (bdrow[:, :], bd_i), (fbias[:, :], fbias_i),
                     (qexp[:, :], qexp_i), (relum[:, :], relum_i),
                     (clips[:, :], clip_i), (bo_row[:, :], bo_i),
                     (wdTb[:, :].rearrange("p (c f) -> p c f", f=16),
                      wdec_i.rearrange("(c p) f -> p c f", p=128))):
        dma("y", dst, src, sc_pre, slow=True)

    ss0, sv0, st0 = S("ss0"), S("sv0"), S("st0")
    emit("s", load_lnexp, ctl=True)
    wait("s", sc_pre)
    emit("s", lambda e: e.activation(cells[0:1, 0:2], clips[0:1, 0:2], AF.Exp,
                                     scale=-1.0), ss0)
    wait("v", ss0)
    sep("v")
    emit("v", lambda e: e.tensor_scalar_add(cells[0:1, 0:2], cells[0:1, 0:2], 1.0))
    sep("v")
    emit("v", lambda e: e.reciprocal(cells[0:1, 0:2], cells[0:1, 0:2]), sv0)
    wait("t", sc_pre)
    for c in range(2):
        emit("t", lambda e, c=c: e.matmul(pb[6 + c][:, :], onesr[0:1, :],
                                          bo_row[0:1, c * 512:(c + 1) * 512],
                                          start=True, stop=True), st0)
    wait("v", st0)
    emit("v", lambda e: e.tensor_copy(borep[:, 0:512], pb[6][:, :]), (sv0, p6f))
    emit("v", lambda e: e.tensor_copy(borep[:, 512:1024], pb[7][:, :]),
         (sv0, p7f))

    def ffn(tag, src_ap, dst_ap, cc_a, cc_b, first, ln_next, have_stats,
            src_gate=None, stats_gate=None, dst2_ap=None, upto="z"):
        Sm = al.mark()
        svA, ssA, stA = S("fsvA"), S("fssA"), S("fstA")

        if not have_stats:
            xt = [al.buf([128, E], F32, f"{tag}xt{i}") for i in range(3)]
            tmpA = al.buf([128, E], F32, f"{tag}tmpA")
            rload = [S(f"frl{i}") for i in range(3)]
            rfree = [S(f"frf{i}") for i in range(3)]
            for tt in range(NTT):
                sl = tt % 3
                wait("y", rfree[sl])
                if src_gate is not None:
                    waitr("y", src_gate.per.get(tt))
                dma("y", xt[sl][:, :], src_ap[tt * 128:(tt + 1) * 128, :],
                    rload[sl])
                wait("v", rload[sl])
                emit("v", lambda e, sl=sl: e.bn_stats(stat2[:, 0:6],
                                                      xt[sl][:, 0:512]))
                emit("v", lambda e, sl=sl: e.bn_stats(stat2[:, 6:12],
                                                      xt[sl][:, 512:1024]))
                sep("v")
                emit("v", lambda e: e.bn_aggr(bnb[:, 0:2], stat2[:, 0:12]))
                sep("v")
                emit("v", lambda e, tt=tt: e.tensor_copy(mu_a[:, tt:tt + 1],
                                                         bnb[:, 0:1]))
                emit("v", lambda e, tt=tt: e.tensor_copy(vv_a[:, tt:tt + 1],
                                                         bnb[:, 1:2]))
                sep("v")
                emit("v", lambda e, sl=sl, tt=tt: e.tensor_scalar(
                    tmpA[:, :], xt[sl][:, :], mu_a[:, tt:tt + 1], None,
                    ALU.subtract, ALU.bypass), rfree[sl])
                sep("v")
                emit("v", lambda e, tt=tt: e.reduce_max(
                    mraw[:, tt:tt + 1], tmpA[:, :], axis=AX,
                    apply_absolute_value=True), svA)
            al.reset(Sm)
        else:
            if stats_gate is not None:
                wait("v", stats_gate)
                wait("s", stats_gate)
        rsqrt_batch(rr_a[:, :], vv_a[:, :], NTT, svA, ssA)
        sep("v")
        emit("v", lambda e: e.tensor_mul(rq1[:, :], mraw[:, :], rr_a[:, :]))
        sep("v")
        emit("v", lambda e: e.reduce_max(stat2[:, 0:1], rq1[:, :], axis=AX), svA)
        cross_max_to_cell(stat2[:, 0:1], C_LMAX, svA, stA)
        sgd, scc = S("fsgd"), S("fscc")
        collective(cc_a, C_LMAX, C_GMAX, svA, sgd, scc)
        emit("v", lambda e: e.tensor_mul(cells[0:1, C_SX:C_SX + 1],
                                         cells[0:1, C_GMAX:C_GMAX + 1],
                                         cells[0:1, C_SIGX:C_SIGX + 1]))
        sep("v")
        emit("v", lambda e: e.tensor_scalar_mul(cells[0:1, C_SX:C_SX + 1],
                                                cells[0:1, C_SX:C_SX + 1],
                                                1.0 / QMAX))
        sep("v")
        emit("v", lambda e: e.reciprocal(cells[0:1, C_ISX:C_ISX + 1],
                                         cells[0:1, C_SX:C_SX + 1]), svA)
        wait("t", svA)
        bcast_cell(C_ISX, R_ISX, svA, stA)
        bcast_cell(C_SX, R_SX, svA, stA)
        if upto == "a":
            return None, None, None

        swq = S("fswq")
        wready = S("fwready")
        sspl = S("fsspl")
        if first:
            wl = [S(f"fwl{i}") for i in range(2)]
            wf = [S(f"fwf{i}") for i in range(2)]
            wnat = [al.buf([128, E], F32, f"{tag}wn{i}") for i in range(2)]
            for mat in range(2):
                w_nat = (wg_n, wu_n)[mat]
                sdst = (sgate, sup)[mat]
                for ich in range(NICH):
                    it = mat * NICH + ich
                    sl = it % 2
                    wait("y", wf[sl])
                    dma("y", wnat[sl][:, :], w_nat[ich * 128:(ich + 1) * 128, :],
                        wl[sl])
                    wait("v", wl[sl])
                    emit("v", lambda e, sl=sl, sdst=sdst, ich=ich: e.reduce_max(
                        sdst[:, ich:ich + 1], wnat[sl][:, :], axis=AX,
                        apply_absolute_value=True), wf[sl])
            sep("v")
            emit("v", lambda e: e.tensor_scalar_mul(sgate[:, :], sgate[:, :],
                                                    1.0 / QMAX))
            emit("v", lambda e: e.tensor_scalar_mul(sup[:, :], sup[:, :],
                                                    1.0 / QMAX))
            wT = [al.buf([128, I], F32, f"{tag}wTt{i}") for i in range(2)]
            irb = al.buf([128, I], F32, f"{tag}irb")
            irow = al.buf([1, I], F32, f"{tag}irow")
            sg_r = svec_sc.rearrange("one (c p) -> p (c one)", p=128)
            wTl = [S(f"fwTl{i}") for i in range(2)]
            wTf = [S(f"fwTf{i}") for i in range(2)]
            stW, svSc, svB = S("fstW"), S("fsvSc"), S("fsvB")
            wtct = 0
            for mat in range(2):
                ssrc = (sgate, sup)[mat]
                wT_in = (wgT_i, wuT_i)[mat]
                mdst = (mgT, muT)[mat]
                wsp = (wspg_sc, wspu_sc)[mat]
                wait("v", swq)
                sep("v")
                emit("v", lambda e, ssrc=ssrc: e.reciprocal(rq1[:, 0:NICH], ssrc[:, :]),
                     svSc)
                wait("y", svSc)
                dma("y", sg_r, rq1[:, 0:NICH], swq, slow=True)
                wait("y", swq)
                dma("y", irow[:, :], svec_sc, swq)
                wait("t", swq)
                for c in range(6):
                    w = min(512, I - c * 512)
                    wait("t", p6f)
                    emit("t", lambda e, c=c, w=w: e.matmul(
                        pb[6][0:128, 0:w], onesr[0:1, :],
                        irow[0:1, c * 512:c * 512 + w], start=True, stop=True),
                        stW)
                    wait("v", stW)
                    emit("v", lambda e, c=c, w=w: e.tensor_copy(
                        irb[:, c * 512:c * 512 + w], pb[6][0:128, 0:w]),
                        (svB, p6f))
                for hch in range(8):
                    sl = wtct % 2
                    wait("y", wTf[sl])
                    wtct += 1
                    dma("y", wT[sl][:, :], wT_in[hch * 128:(hch + 1) * 128, :],
                        wTl[sl])
                    wait("v", wTl[sl])
                    wait("v", svB)
                    emit("v", lambda e, sl=sl: e.tensor_mul(
                        wT[sl][:, :], wT[sl][:, :], irb[:, :]))
                    sep("v")
                    emit("v", lambda e, sl=sl: e.tensor_scalar(
                        wT[sl][:, :], wT[sl][:, :], MAGIC, MAGIC,
                        ALU.add, ALU.subtract))
                    sep("v")
                    emit("v", lambda e, sl=sl, mdst=mdst, hch=hch: e.tensor_copy(
                        mdst[:, hch, :], wT[sl][:, :]), (wTf[sl], wready))
                    wait("g", wready)
                    dma("g", wsp[hch], mdst[:, hch, :], sspl)
            al.reset(Sm)
        else:
            wait("y", sspl, sspl.n)
            if src_gate is not None:
                wait_gate("y", src_gate)
            for hch in range(8):
                dma("y", mgT[:, hch, :], wspg_sc[hch], swq)
                dma("y", muT[:, hch, :], wspu_sc[hch], swq)
        wrec = (wready, wready.n) if first else (swq, swq.n)
        wait("v", svA)
        emit("v", lambda e: e.tensor_scalar(svg[:, :], sgate[:, :],
                                            rsc[:, R_SX:R_SX + 1], None,
                                            ALU.mult, ALU.bypass))
        emit("v", lambda e: e.tensor_scalar(svu[:, :], sup[:, :],
                                            rsc[:, R_SX:R_SX + 1], None,
                                            ALU.mult, ALU.bypass), svA)
        if upto == "w":
            return None, None, None

        xt2 = [al.buf([128, E], F32, f"{tag}x2t{i}") for i in range(3)]
        ntile = [al.buf([128, E], BF16, f"{tag}nt{i}") for i in range(2)]
        ntfree = {}
        xqrec = {}
        tq = al.buf([128, E], F32, f"{tag}tq")
        xq = [al.buf([128, 8, 512], BF16, f"{tag}xq{i}") for i in range(2)]
        slt = [al.buf([128, 512], F32, f"{tag}slt{i}") for i in range(2)]
        hidt = [al.buf([128, 512], F32, f"{tag}hidt{i}") for i in range(3)]
        hmx = al.buf([128, 2], F32, f"{tag}hmx")
        r2load = [S(f"fr2l{i}") for i in range(3)]
        r2free = [S(f"fr2f{i}") for i in range(3)]
        ntr = [S(f"fntr{i}") for i in range(2)]
        xqf = [S(f"fxqf{i}") for i in range(2)]
        pgr = [S(f"fpgr{i}") for i in range(4)]
        pgf = [S(f"fpgf{i}") for i in range(4)]
        s_sil = [S(f"fssil{i}") for i in range(2)]
        sltf = [S(f"fsltf{i}") for i in range(2)]
        hr = [S(f"fhr{i}") for i in range(3)]
        hfree = [S(f"fhfree{i}") for i in range(3)]
        hstrec = {}
        mm_done = S("fmm_done")
        emit("v", lambda e: e.memset(hmx[:, 0:1], 0.0), svA)
        emit("s", lambda e: e.preload_act_table(AF.Silu), ctl=True)
        waitr("s", wrec)
        waitr("t", wrec)
        for tc in range(NTC):
            xsl = tc % 2
            wait("y", xqf[xsl])
            for tti in range(4):
                tt = tc * 4 + tti
                sl = tt % 3
                npar = tt % 2
                wait("y", r2free[sl])
                if src_gate is not None:
                    wait_gate("y", src_gate)
                dma("y", xt2[sl][:, :], src_ap[tt * 128:(tt + 1) * 128, :],
                    r2load[sl])
                wait("v", r2load[sl])
                if tt == 0:
                    wait("v", svA)
                waitr("v", ntfree.get(npar))
                emit("v", lambda e, sl=sl, tt=tt: e.tensor_scalar(
                    tq[:, :], xt2[sl][:, :], mu_a[:, tt:tt + 1], rr_a[:, tt:tt + 1],
                    ALU.subtract, ALU.mult), r2free[sl])
                sep("v")
                emit("v", lambda e: e.tensor_scalar(
                    tq[:, :], tq[:, :], rsc[:, R_ISX:R_ISX + 1], MAGIC, ALU.mult, ALU.add))
                sep("v")
                emit("v", lambda e: e.tensor_scalar(
                    tq[:, :], tq[:, :], MAGIC, -(QMAX + 1.0), ALU.subtract, ALU.max))
                sep("v")
                emit("v", lambda e, npar=npar: e.tensor_scalar_min(
                    ntile[npar][:, :], tq[:, :], QMAX), ntr[npar])
                waitr("t", (S("avm"), S("avm").n))
                for d in ([] if _NOTR else range(8)):
                    lastrec = pe_tr(xq[xsl][:, d, tti * 128:(tti + 1) * 128],
                                    ntile[npar][:, d * 128:(d + 1) * 128],
                                    4 + (d % 2), identb[:, :], 128, 128)
                if not _NOTR:
                    ntfree[npar] = (S("atm"), S("atm").n)
                    xqrec[xsl] = lastrec
            waitr("t", xqrec.get(xsl))
            for ich in ([] if _NOMM else range(NICH)):
                bsel = (tc * NICH + ich) % 2
                bg, bu = 2 * bsel, 2 * bsel + 1
                spar = (tc * NICH + ich) % 2
                if _MMONLY:
                    for d in range(8):
                        emit("t", lambda e, d=d, xsl=xsl: e.matmul(
                            pb[0][:, :], mgT[:, d, 0:128],
                            xq[xsl][:, d, :], start=True, stop=True),
                            S("fpgr0") if d == 7 else ())
                    continue
                wait("t", pgf[bg])
                wait("t", pgf[bu])
                for d in range(8):
                    emit("t", lambda e, bg=bg, ich=ich, d=d, xsl=xsl: e.matmul(
                        pb[bg][:, :], mgT[:, d, ich * 128:(ich + 1) * 128],
                        xq[xsl][:, d, :], start=(d == 0), stop=(d == 7)),
                        pgr[bg] if d == 7 else ())
                for d in range(8):
                    fin = (d == 7) and ich == NICH - 1
                    incs = []
                    if d == 7:
                        incs.append(pgr[bu])
                    if fin:
                        incs.append(xqf[xsl])
                        if tc == NTC - 1:
                            incs.append(mm_done)
                    emit("t", lambda e, bu=bu, ich=ich, d=d, xsl=xsl: e.matmul(
                        pb[bu][:, :], muT[:, d, ich * 128:(ich + 1) * 128],
                        xq[xsl][:, d, :], start=(d == 0), stop=(d == 7)), incs)
                if _NOSILU:
                    hsl = (tc * NICH + ich) % 3
                    wait("s", pgr[bg])
                    emit("s", lambda e, bg=bg, hsl=hsl: e.copy(
                        slt[spar][:, :], pb[bg][:, :]), (s_sil[spar], pgf[bg]))
                    wait("v", hfree[hsl])
                    wait("v", pgr[bu])
                    wait("v", s_sil[spar])
                    emit("v", lambda e, bu=bu, hsl=hsl, spar=spar:
                         e.tensor_add(hidt[hsl][:, :], pb[bu][:, :],
                                      slt[spar][:, :]), (pgf[bu], hr[hsl]))
                    wait("s", hr[hsl])
                    dma("s", hid_sc[ich, tc], hidt[hsl][:, :], hfree[hsl])
                    hstrec[tc * NICH + ich] = (hfree[hsl], hfree[hsl].n)
                    continue
                wait("s", pgr[bg])
                wait("s", sltf[spar])
                emit("s", lambda e, bg=bg, ich=ich, spar=spar: e.activation(
                    slt[spar][:, :], pb[bg][:, :], AF.Silu,
                    scale=svg[:, ich:ich + 1]), (s_sil[spar], pgf[bg]))
                hsl = (tc * NICH + ich) % 3
                wait("v", hfree[hsl])
                wait("v", pgr[bu])
                wait("v", s_sil[spar])
                emit("v", lambda e, bu=bu, ich=ich, hsl=hsl, spar=spar:
                     e.scalar_tensor_tensor(
                         hidt[hsl][:, :], pb[bu][:, :], svu[:, ich:ich + 1],
                         slt[spar][:, :], ALU.mult, ALU.mult),
                     (pgf[bu], sltf[spar]))
                sep("v")
                emit("v", lambda e, hsl=hsl: e.reduce_max(
                    hmx[:, 1:2], hidt[hsl][:, :], axis=AX,
                    apply_absolute_value=True))
                sep("v")
                emit("v", lambda e: e.tensor_max(hmx[:, 0:1], hmx[:, 0:1],
                                                 hmx[:, 1:2]), hr[hsl])
                wait("s", hr[hsl])
                dma("s", hid_sc[ich, tc], hidt[hsl][:, :], hfree[hsl])
                hstrec[tc * NICH + ich] = (hfree[hsl], hfree[hsl].n)

        svH = S("fsvH")
        cross_max_to_cell(hmx[:, 0:1], C_LMAX, svH, stA)
        collective(cc_b, C_LMAX, C_GMAX, svH, sgd, scc)
        emit("v", lambda e: e.tensor_mul(cells[0:1, C_SH:C_SH + 1],
                                         cells[0:1, C_GMAX:C_GMAX + 1],
                                         cells[0:1, C_SIGD:C_SIGD + 1]))
        sep("v")
        emit("v", lambda e: e.tensor_scalar_mul(cells[0:1, C_SH:C_SH + 1],
                                                cells[0:1, C_SH:C_SH + 1],
                                                1.0 / QMAX))
        sep("v")
        emit("v", lambda e: e.reciprocal(cells[0:1, C_ISH:C_ISH + 1],
                                         cells[0:1, C_SH:C_SH + 1]))
        emit("v", lambda e: e.tensor_scalar_mul(
            cells[0:1, C_SHHALF:C_SHHALF + 1], cells[0:1, C_SH:C_SH + 1], 0.5),
            svH)
        wait("t", svH)
        bcast_cell(C_ISH, R_ISH, svH, stA)
        bcast_cell(C_SHHALF, R_SHH, svH, stA)
        if upto == "b":
            return None, None, None

        Sd = al.mark()
        dready = S("fdready")
        if first:
            swd = al.buf([128, 8], F32, f"{tag}swd")
            Sd2 = al.mark()
            wdn = al.buf([128, I], F32, f"{tag}wdn")
            wdl = S("fwdl")
            wdf = S("fwdf")
            svD = S("fsvD")
            for ech in range(8):
                wait("y", wdf)
                dma("y", wdn[:, :], wd_n[ech * 128:(ech + 1) * 128, :], wdl)
                wait("v", wdl)
                emit("v", lambda e, ech=ech, swd=swd: e.reduce_max(
                    swd[:, ech:ech + 1], wdn[:, :], axis=AX,
                    apply_absolute_value=True), wdf)
            al.reset(Sd2)
            sep("v")
            emit("v", lambda e, swd=swd: e.tensor_scalar_mul(
                swd[:, :], swd[:, :], 1.0 / QMAX))
            svd_r = svec_sc.rearrange("one (c p) -> p (c one)", p=128)
            sep("v")
            emit("v", lambda e, swd=swd: e.reciprocal(rq1[:, 0:8], swd[:, :]),
                 svD)
            irowd = al.buf([1, E], F32, f"{tag}irowd")
            srowd = al.buf([1, E], F32, f"{tag}srowd")
            swq2 = S("fswq2")
            wait("y", svD)
            dma("y", svd_r[:, 0:8], rq1[:, 0:8], swq2, slow=True)
            wait("y", swq2)
            dma("y", irowd[:, :], svec_sc[0:1, 0:E], swq2)
            wait("y", swq2)
            dma("y", svd_r[:, 0:8], swd[:, :], swq2, slow=True)
            wait("y", swq2)
            dma("y", srowd[:, :], svec_sc[0:1, 0:E], swq2)
            irbd = al.buf([128, E], F32, f"{tag}irbd")
            srbd = al.buf([128, E], F32, f"{tag}srbd")
            stW2, svW3 = S("fstW2"), S("fsvW3")
            wait("t", swq2)
            for (row, dstb) in ((irowd, irbd), (srowd, srbd)):
                for c in range(2):
                    wait("t", p6f)
                    emit("t", lambda e, row=row, c=c: e.matmul(
                        pb[6][:, :], onesr[0:1, :],
                        row[0:1, c * 512:(c + 1) * 512], start=True, stop=True),
                        stW2)
                    wait("v", stW2)
                    emit("v", lambda e, dstb=dstb, c=c: e.tensor_copy(
                        dstb[:, c * 512:(c + 1) * 512], pb[6][:, :]),
                        (svW3, p6f))
            wdt = al.buf([128, E], F32, f"{tag}wdt")
            wdtl = S("fwdtl")
            wdtf = S("fwdtf")
            for ich in range(NICH):
                wait("y", wdtf)
                dma("y", wdt[:, :], wdT_i[ich * 128:(ich + 1) * 128, :], wdtl)
                wait("v", wdtl)
                wait("v", svW3)
                if ich == 0:
                    wait("v", mm_done)
                    wait("v", sspl, sspl.n)
                emit("v", lambda e: e.tensor_mul(wdt[:, :], wdt[:, :],
                                                 irbd[:, :]))
                sep("v")
                emit("v", lambda e: e.tensor_scalar(
                    wdt[:, :], wdt[:, :], MAGIC, MAGIC, ALU.add, ALU.subtract))
                sep("v")
                emit("v", lambda e: e.tensor_mul(wdt[:, :], wdt[:, :],
                                                 srbd[:, :]))
                sep("v")
                emit("v", lambda e, ich=ich: e.tensor_copy(
                    mdT[:, ich, :], wdt[:, :]), (wdtf, dready))
                wait("g", dready)
                dma("g", wspd_sc[ich], mdT[:, ich, :], sspl)
            al.reset(Sd)
        else:
            wait("y", mm_done)
            wait("y", sspl, sspl.n)
            sdq2 = S("fsdq2")
            for ich in range(NICH):
                dma("y", mdT[:, ich, :], wspd_sc[ich], sdq2)
        drec = (dready, dready.n) if first else (sdq2, sdq2.n)
        if upto == "d":
            return None, None, None

        al.reset(Sm)
        hin = [al.buf([128, 512], F32, f"{tag}hin{i}") for i in range(3)]
        tqv = al.buf([128, 512], F32, f"{tag}tqv")
        tqg = al.buf([128, 512], F32, f"{tag}tqg")
        nh = [al.buf([128, NICH, 512], BF16, f"{tag}nhp{i}") for i in range(2)]
        lnt = al.buf([128, E], F32, f"{tag}lnt")
        xr = [al.buf([128, E], F32, f"{tag}xr{i}") for i in range(2)]
        x1t = [al.buf([128, E], F32, f"{tag}x1t{i}") for i in range(2)]
        h2l = [S(f"fh2l{i}") for i in range(3)]
        h2f = [S(f"fh2f{i}") for i in range(3)]
        nhr = [S(f"fnhr{i}") for i in range(NICH)]
        nhfree = [S(f"fnhfr{i}") for i in range(2)]
        xrl = [S(f"fxrl{i}") for i in range(2)]
        xrf = [S(f"fxrf{i}") for i in range(2)]
        x1fr = [S(f"fx1fr{i}") for i in range(2)]
        pdr = [S(f"fpdr{i}") for i in range(4)]
        pdf = [S(f"fpdf{i}") for i in range(4)]
        sstore = S("fsstore")
        dst_gate = Gate()
        mmC_done = S("fmmC_done")
        for i in range(3):
            wait("y", hfree[i], hfree[i].n)
        wait("y", mm_done)
        waitr("y", drec)
        for tc in range(NTC):
            par = tc % 2
            gwaited = [False]
            vwaited = [False]
            for ich in range(NICH):
                it = tc * NICH + ich
                sl = it % 3
                wait("y", h2f[sl])
                waitr("y", hstrec.get(it))
                dma("y", hin[sl][:, :], hid_sc[ich, tc], h2l[sl])
                qeng = "v"
                tqx = tqv
                wait(qeng, h2l[sl])
                if it == 0:
                    wait("v", svH)
                if not vwaited[0]:
                    wait(qeng, nhfree[par])
                    vwaited[0] = True
                emit(qeng, lambda e, sl=sl, tqx=tqx: e.tensor_scalar(
                    tqx[:, :], hin[sl][:, :], rsc[:, R_ISH:R_ISH + 1], MAGIC,
                    ALU.mult, ALU.add), h2f[sl])
                sep(qeng)
                emit(qeng, lambda e, tqx=tqx: e.tensor_scalar(
                    tqx[:, :], tqx[:, :], MAGIC, -(QMAX + 1.0), ALU.subtract, ALU.max))
                sep(qeng)
                emit(qeng, lambda e, tqx=tqx, ich=ich, par=par:
                     e.tensor_scalar_min(nh[par][:, ich, :], tqx[:, :], QMAX),
                     nhr[ich])
            for tti in range(4):
                tt = tc * 4 + tti
                bsel = tt % 2
                bl, br = 2 * bsel, 2 * bsel + 1
                wait("t", pdf[bl])
                wait("t", pdf[br])
                if tt == 0:
                    waitr("t", drec)
                for ich in range(NICH):
                    if tti == 0:
                        wait("t", nhr[ich])
                    for half in range(2):
                        bk = bl if half == 0 else br
                        incs = []
                        if ich == NICH - 1:
                            incs.append(pdr[bk])
                            if tti == 3 and half == 1:
                                incs.append(nhfree[par])
                                if tc == NTC - 1:
                                    incs.append(mmC_done)
                        emit("t", lambda e, bk=bk, ich=ich, tti=tti, half=half,
                             par=par: e.matmul(
                                 pb[bk][:, :],
                                 nh[par][:, ich, tti * 128:(tti + 1) * 128],
                                 mdT[:, ich, half * 512:(half + 1) * 512],
                                 start=(ich == 0), stop=(ich == NICH - 1)),
                             incs)
                xsl = tt % 2
                wait("y", xrf[xsl])
                if src_gate is not None:
                    wait_gate("y", src_gate)
                dma("y", xr[xsl][:, :], src_ap[tt * 128:(tt + 1) * 128, :],
                    xrl[xsl])
                wait("v", xrl[xsl])
                wait("v", pdr[bl])
                wait("v", pdr[br])
                wait("v", x1fr[xsl])
                emit("v", lambda e, bl=bl, xsl=xsl: e.scalar_tensor_tensor(
                    x1t[xsl][:, 0:512], pb[bl][:, :], rsc[:, R_SHH:R_SHH + 1],
                    xr[xsl][:, 0:512], ALU.mult, ALU.add), pdf[bl])
                emit("v", lambda e, br=br, xsl=xsl: e.scalar_tensor_tensor(
                    x1t[xsl][:, 512:1024], pb[br][:, :], rsc[:, R_SHH:R_SHH + 1],
                    xr[xsl][:, 512:1024], ALU.mult, ALU.add),
                    (pdf[br], xrf[xsl]))
                if ln_next:
                    sep("v")
                    emit("v", lambda e, xsl=xsl: e.bn_stats(
                        stat2[:, 0:6], x1t[xsl][:, 0:512]))
                    emit("v", lambda e, xsl=xsl: e.bn_stats(
                        stat2[:, 6:12], x1t[xsl][:, 512:1024]))
                    sep("v")
                    emit("v", lambda e: e.bn_aggr(bnb[:, 0:2], stat2[:, 0:12]))
                    sep("v")
                    emit("v", lambda e, tt=tt: e.tensor_copy(
                        mu_a[:, tt:tt + 1], bnb[:, 0:1]))
                    emit("v", lambda e, tt=tt: e.tensor_copy(
                        vv_a[:, tt:tt + 1], bnb[:, 1:2]))
                    sep("v")
                    emit("v", lambda e, xsl=xsl, tt=tt: e.tensor_scalar(
                        lnt[:, :], x1t[xsl][:, :], mu_a[:, tt:tt + 1], None,
                        ALU.subtract, ALU.bypass))
                    sep("v")
                    emit("v", lambda e, tt=tt: e.reduce_max(
                        mraw[:, tt:tt + 1], lnt[:, :], axis=AX,
                        apply_absolute_value=True), sstore)
                    wait("y", sstore)
                else:
                    waitr("y", chain("v"))
                dma("y", dst_ap[tt * 128:(tt + 1) * 128, :], x1t[xsl][:, :],
                    x1fr[xsl])
                if dst2_ap is not None:
                    dma("y", dst2_ap[tt * 128:(tt + 1) * 128, :],
                        x1t[xsl][:, :], x1fr[xsl])
                dst_gate.per[tt] = (x1fr[xsl], x1fr[xsl].n)
        dst_gate.full = [(x1fr[0], x1fr[0].n), (x1fr[1], x1fr[1].n)]
        al.reset(Sm)
        return dst_gate, mmC_done, sstore

    def attn_phase(x1done, mmC_done, stats_gate):
        Sm = al.mark()
        sv2, ss2, st2 = S("asv2"), S("ass2"), S("ast2")
        emit("s", load_lnexp, ctl=True)
        wait("s", stats_gate)
        wait("v", stats_gate)
        rsqrt_batch(rr_a[:, :], vv_a[:, :], NTT, sv2, ss2)

        Smb = al.mark()
        xb = [al.buf([128, E], F32, f"axb{i}") for i in range(2)]
        nb = [al.buf([128, E], BF16, f"anb{i}") for i in range(2)]
        nbfree = {}
        xbl = [S(f"axbl{i}") for i in range(2)]
        xbf_ = [S(f"axbf{i}") for i in range(2)]
        nbr = [S(f"anbr{i}") for i in range(2)]
        xlrec = [None]
        wait("y", mmC_done)
        wait_gate("y", x1done)
        wait("v", S("fsspl"), S("fsspl").n)
        for tt in range(NTT):
            sl = tt % 2
            wait("y", xbf_[sl])
            dma("y", xb[sl][:, :], x1_sc[tt * 128:(tt + 1) * 128, :], xbl[sl])
            wait("v", xbl[sl])
            if tt == 0:
                wait("v", sv2)
            waitr("v", nbfree.get(sl))
            emit("v", lambda e, sl=sl, tt=tt: e.tensor_scalar(
                nb[sl][:, :], xb[sl][:, :], mu_a[:, tt:tt + 1],
                rr_a[:, tt:tt + 1], ALU.subtract, ALU.mult),
                (nbr[sl], xbf_[sl]))
            waitr("t", (S("avm"), S("avm").n))
            for d in range(8):
                xlrec[0] = pe_tr(xln2fm[:, d, tt * 128:(tt + 1) * 128],
                                 nb[sl][:, d * 128:(d + 1) * 128],
                                 4 + (d % 2), identb[:, :], 128, 128)
            nbfree[sl] = (S("atm"), S("atm").n)
        scs = S("ascs")
        dma("y", cosfm[:, :], cos_i, scs)
        dma("y", sinmfm[:, :], sin_i, scs)
        al.reset(Smb)

        waitr("v", xlrec[0])
        for ech in range(8):
            emit("v", lambda e, ech=ech: e.reduce_sum(
                mean8[:, ech:ech + 1], xln2fm[:, ech, :], axis=AX))
        sep("v")
        emit("v", lambda e: e.tensor_scalar_mul(mean8[:, :], mean8[:, :],
                                                1.0 / T), sv2)
        wait("t", sv2)
        wait("t", p6f)
        for ech in range(8):
            emit("t", lambda e, ech=ech: e.matmul(
                pb[6][0:1, 0:16], mean8[:, ech:ech + 1],
                wdTb[:, ech * 16:(ech + 1) * 16],
                start=(ech == 0), stop=(ech == 7)),
                st2 if ech == 7 else ())
        wait("v", st2)
        emit("v", lambda e: e.tensor_add(srow[0:1, 0:16], pb[6][0:1, 0:16],
                                         bdrow[0:1, :]), (sv2, p6f))
        wait("s", sv2)
        emit("s", lambda e: e.activation(srow2[0:1, 0:16], srow[0:1, 0:16],
                                         AF.Exp, scale=-1.0), ss2)
        wait("v", ss2)
        emit("v", lambda e: e.tensor_scalar_add(srow2[0:1, 0:16],
                                                srow2[0:1, 0:16], 1.0))
        sep("v")
        emit("v", lambda e: e.reciprocal(srow[0:1, 16:32], srow2[0:1, 0:16]),
             sv2)
        wait("s", sv2)
        emit("s", lambda e: e.activation(srow2[0:1, 0:16], srow[0:1, 16:32],
                                         AF.Exp, scale=-8.0 * LN2,
                                         bias=constv[0:1, COL_M5LN2:COL_M5LN2 + 1]),
             ss2)
        wait("v", ss2)
        emit("v", lambda e: e.tensor_scalar(srow2[0:1, 0:16], srow2[0:1, 0:16],
                                            -1.0, 1.0, ALU.mult, ALU.add), sv2)
        wait("s", sv2)
        emit("s", lambda e: e.activation(ldrow[0:1, :], srow2[0:1, 0:16],
                                         AF.Ln), ss2)
        wait("t", ss2)
        wait("t", p6f)
        emit("t", lambda e: e.matmul(pb[6][0:128, 0:16], onesr[0:1, :],
                                     ldrow[0:1, :], start=True, stop=True), st2)
        wait("v", st2)
        emit("v", lambda e: e.tensor_copy(ldb[:, :], pb[6][0:128, 0:16]),
             (sv2, p6f))

        wga = al.buf([128, 8, 128], BF16, "awga")
        gtile = [al.buf([128, 512], BF16, f"agt{i}") for i in range(2)]
        swl = S("aswl")
        gpr = [S(f"agpr{i}") for i in range(2)]
        gpf = [S(f"agpf{i}") for i in range(2)]
        gsr = [S(f"agsr{i}") for i in range(2)]
        gsf = [S(f"agsf{i}") for i in range(2)]
        emit("s", lambda e: e.preload_act_table(AF.Silu), ctl=True)

        def wslice(w_i, pi):
            return w_i.rearrange("(c p) f -> p c f", p=128)[:, :,
                                                            pi * 128:(pi + 1) * 128]

        for pi in range(NP):
            wait("g", swl, swl.n)
            waitr("g", xlrec[0])
            dma("g", wga[:, :, :], wslice(wgaT_i, pi), swl)
            wait("t", swl)
            waitr("t", xlrec[0])
            for tc in range(NTC):
                gb = tc % 2
                wait("t", gpf[gb])
                for d in range(8):
                    emit("t", lambda e, gb=gb, d=d, tc=tc: e.matmul(
                        pb[gb][:, :], wga[:, d, :],
                        xln2fm[:, d, tc * 512:(tc + 1) * 512],
                        start=(d == 0), stop=(d == 7)),
                        gpr[gb] if d == 7 else ())
                wait("s", gpr[gb])
                wait("s", gsf[gb])
                emit("s", lambda e, gb=gb, pi=pi: e.activation(
                    gtile[gb][:, :], pb[gb][:, :], AF.Silu,
                    bias=bfm[:, CB_G + pi:CB_G + pi + 1]), (gsr[gb], gpf[gb]))
                wait("g", gsr[gb])
                dma("g", gate_sc[pi, tc], gtile[gb][:, :], gsf[gb])

        emit("s", load_lnexp, ctl=True)
        Qp = al.buf([128, T], BF16, "aQp")
        Kp = al.buf([128, T], BF16, "aKp")
        Vaug = [al.buf([128, 32 * 80], BF16, f"aVa{i}") for i in range(2)]
        og = al.buf([128, T], F32, "aog")
        attnT = [al.buf([128, 512], BF16, f"aaT{i}") for i in range(2)]
        Dmix = al.buf([128, 2048], BF16, "aDmix")
        gtl = al.buf([128, 512], F32, "agtl")
        fbt = al.buf([128, 28], F32, "afbt")
        qraw = al.buf([128, 512], F32, "aqraw")
        qrot = al.buf([128, 512], F32, "aqrot")
        tr1 = al.buf([128, 512], F32, "atr1")
        vbf = [al.buf([128, 512], BF16, f"avbf{i}") for i in range(2)]
        vbfree = {}
        vrec = [None]
        nmbuf = al.buf([128, 512], F32, "anm")
        tfut = al.buf([128, 512], F32, "atfut")
        ont = al.buf([128, 512], F32, "aont")
        gont = [al.buf([128, 512], BF16, f"agont{i}") for i in range(2)]
        gin = [al.buf([128, 512], BF16, f"agin{i}") for i in range(2)]
        wqb = al.buf([128, 8, 128], BF16, "awqb")
        wkb = al.buf([128, 8, 128], BF16, "awkb")
        wvb = al.buf([128, 8, 128], BF16, "awvb")
        sv_ones = S("asvones")
        for hh in range(2):
            emit("v", lambda e, hh=hh: e.memset(
                Vaug[hh].rearrange("p (k c) -> p k c", c=80)[:, :, 64:65], 1.0),
                sv_ones)

        swqkv = S("aswqkv")
        ppr = [S(f"appr{i}") for i in range(2)]
        ppf = [S(f"appf{i}") for i in range(2)]
        sQf = S("asQf")
        vbr = [S(f"avbr{i}") for i in range(2)]
        ssr = [S(f"assr{i}") for i in range(2)]
        ssf = [S(f"assf{i}") for i in range(2)]
        atr = [S(f"aatr{i}") for i in range(2)]
        atf = [S(f"aatf{i}") for i in range(2)]
        sxm = S("asxm")
        avp = S("aavp")
        avf = S("aavf")
        p4free = S("ap4free")
        p5free = S("ap5free")
        smrg = S("asmrg")
        shead = S("ashead")
        svhead = S("asvhead")
        sgn = S("asgn")
        oggate = S("aoggate")
        ginl = [S(f"aginl{i}") for i in range(2)]
        ginf = [S(f"aginf{i}") for i in range(2)]
        gonr = [S(f"agonr{i}") for i in range(2)]
        gonf = [S(f"agonf{i}") for i in range(2)]

        wait("v", scs)
        for pi in range(NP):
            wait("g", swqkv, swqkv.n)
            dma("g", wqb[:, :, :], wslice(wqT_i, pi), swqkv)
            dma("g", wkb[:, :, :], wslice(wkT_i, pi), swqkv)
            dma("g", wvb[:, :, :], wslice(wvT_i, pi), swqkv)
            wait("t", swqkv)
            for proj in range(3):
                wb_ = (wqb, wkb, wvb)[proj]
                bcol = (CB_Q, CB_K, CB_V)[proj]
                dstfm = (Qp, Kp, None)[proj]
                for tc in range(NTC):
                    pbk = tc % 2
                    wait("t", ppf[pbk])
                    for d in range(8):
                        emit("t", lambda e, pbk=pbk, d=d, tc=tc, wb_=wb_:
                             e.matmul(pb[pbk][:, :], wb_[:, d, :],
                                      xln2fm[:, d, tc * 512:(tc + 1) * 512],
                                      start=(d == 0), stop=(d == 7)),
                             ppr[pbk] if d == 7 else ())
                    wait("v", ppr[pbk])
                    if proj < 2:
                        emit("v", lambda e, pbk=pbk, pi=pi, bcol=bcol:
                             e.tensor_scalar_add(qraw[:, :], pb[pbk][:, :],
                                                 bfm[:, bcol + pi:bcol + pi + 1]),
                             ppf[pbk])
                        mask = [i + 1 if i % 2 == 0 else i - 1 for i in range(32)]
                        sep("v")
                        emit("v", lambda e: e.stream_shuffle(qrot[:, :], qraw[:, :],
                                                             mask=mask))
                        emit("v", lambda e, tc=tc: e.tensor_mul(
                            tr1[:, :], qraw[:, :], cosfm[:, tc * 512:(tc + 1) * 512]))
                        sep("v")
                        emit("v", lambda e, tc=tc: e.tensor_mul(
                            qrot[:, :], qrot[:, :], sinmfm[:, tc * 512:(tc + 1) * 512]))
                        sep("v")
                        emit("v", lambda e, tc=tc, dstfm=dstfm: e.tensor_add(
                            dstfm[:, tc * 512:(tc + 1) * 512], tr1[:, :], qrot[:, :]), sQf)
                    else:
                        vsl = tc % 2
                        waitr("v", vbfree.get(vsl))
                        emit("v", lambda e, pbk=pbk, pi=pi, vsl=vsl:
                             e.tensor_scalar_add(vbf[vsl][:, :], pb[pbk][:, :],
                                                 bfm[:, CB_V + pi:CB_V + pi + 1]),
                             (ppf[pbk], vbr[vsl]))
                        waitr("t", (S("avm"), S("avm").n))
                        for blk in range(4):
                            kt = tc * 4 + blk
                            for hh in range(2):
                                idsl = (identb[0:64, 0:64] if hh == 0
                                        else identb[64:128, 64:128])
                                vrec[0] = pe_tr(
                                    Vaug[hh][:, kt * 80:kt * 80 + 64],
                                    vbf[vsl][hh * 64:(hh + 1) * 64,
                                             blk * 128:(blk + 1) * 128],
                                    2 + (blk % 2), idsl, 64, 128)
                        vbfree[vsl] = (S("atm"), S("atm").n)
            for hh in range(2):
                base = 64 * hh
                hcol = pi * 2 + hh
                wait("v", sv2)
                emit("v", lambda e, hcol=hcol: e.tensor_scalar(
                    fbt[:, :], fbias[:, :], ldb[:, hcol:hcol + 1], None,
                    ALU.mult, ALU.bypass), svhead)
                wait("s", svhead)
                wait("s", atf[0], atf[0].n)
                wait("s", atf[1], atf[1].n)
                wait("s", smrg, smrg.n)
                emit("s", lambda e, hcol=hcol: e.activation(
                    Dmix[:, :], relum[:, :], AF.Exp,
                    scale=ldb[:, hcol:hcol + 1]), shead)
                emit("s", lambda e, hcol=hcol: e.activation(
                    gtl[:, :], qexp[:, :], AF.Exp,
                    scale=ldb[:, hcol:hcol + 1]), shead)
                wait("t", sQf, sQf.n)
                waitr("t", vrec[0])
                for qc in range(NTC):
                    lastpast = qc * 4 + 3

                    def emit_av(kt):
                        par = kt % 2
                        is_past = kt <= lastpast
                        dstb = 4 if is_past else 5
                        first_mm = (kt == 0) or (kt == lastpast + 1)
                        stop_mm = (kt == lastpast) or (kt == 31)
                        if first_mm:
                            wait("t", p4free if is_past else p5free)
                        rs, rv = atrec[kt]
                        wait("t", rs, rv)
                        incs = [atf[par]]
                        if stop_mm:
                            incs.append(avp if is_past else avf)
                        emit("t", lambda e, dstb=dstb, hh=hh, kt=kt, par=par,
                             first_mm=first_mm, stop_mm=stop_mm: e.matmul(
                                 pb[dstb][0:65, :],
                                 Vaug[hh][:, kt * 80:kt * 80 + 65],
                                 attnT[par][:, :], start=first_mm,
                                 stop=stop_mm), incs)

                    atrec = {}
                    for kt in range(32):
                        par = kt % 2
                        bank = 2 + par
                        dm = kt * 128 - qc * 512
                        wait("t", ssf[par])
                        emit("t", lambda e, bank=bank, base=base, kt=kt, qc=qc:
                             e.matmul(pb[bank][:, :],
                                      Kp[base:base + 64, kt * 128:(kt + 1) * 128],
                                      Qp[base:base + 64, qc * 512:(qc + 1) * 512],
                                      start=True, stop=True), ssr[par])
                        wait("s", ssr[par])
                        wait("s", atf[par])
                        if dm <= -128:
                            emit("s", lambda e, bank=bank, par=par: e.activation(
                                attnT[par][:, :], pb[bank][:, :], AF.Exp,
                                scale=0.125), (atr[par], ssf[par]))
                            atrec[kt] = (atr[par], atr[par].n)
                        elif dm <= 384:
                            emit("s", lambda e, bank=bank, par=par: e.activation(
                                attnT[par][:, :], pb[bank][:, :], AF.Exp,
                                scale=0.125), (sxm, ssf[par]))
                            wait("v", sxm)
                            wait("v", shead)
                            m = dm // 128
                            emit("v", lambda e, par=par, m=m: e.tensor_mul(
                                attnT[par][:, :], attnT[par][:, :],
                                Dmix[:, m * 512:(m + 1) * 512]), S("avm"))
                            atrec[kt] = (S("avm"), S("avm").n)
                        else:
                            m = (dm - 512) // 128
                            wait("s", shead)
                            emit("s", lambda e, bank=bank, par=par, m=m:
                                 e.activation(attnT[par][:, :], pb[bank][:, :],
                                              AF.Exp, scale=0.125,
                                              bias=fbt[:, m:m + 1]),
                                 (atr[par], ssf[par]))
                            atrec[kt] = (atr[par], atr[par].n)
                        if kt >= 1:
                            emit_av(kt - 1)
                    emit_av(31)
                    has_fut = lastpast < 31
                    wait("v", avp)
                    if has_fut:
                        wait("v", avf)
                        wait("v", shead)
                        emit("v", lambda e: e.tensor_mul(
                            tfut[0:65, :], pb[5][0:65, :], gtl[0:65, :]),
                            p5free)
                        sep("v")
                        emit("v", lambda e: e.tensor_add(
                            nmbuf[0:65, :], pb[4][0:65, :], tfut[0:65, :]),
                            (smrg, p4free))
                    else:
                        emit("v", lambda e: e.tensor_scalar_add(
                            nmbuf[0:65, :], pb[4][0:65, :], 0.0),
                            (smrg, p4free))
                    sep("v")
                    emit("v", lambda e: e.reciprocal(srow[0:1, :],
                                                     nmbuf[64:65, :]), smrg)
                    wait("t", smrg)
                    wait("t", p7f)
                    emit("t", lambda e: e.matmul(pb[7][0:64, :],
                                                 onesr[0:1, 0:64], srow[0:1, :],
                                                 start=True, stop=True), st2)
                    wait("v", st2)
                    emit("v", lambda e, base=base, qc=qc: e.tensor_mul(
                        og[base:base + 64, qc * 512:(qc + 1) * 512],
                        nmbuf[0:64, :], pb[7][0:64, :]), (oggate, p7f))
            wait("v", oggate, oggate.n)
            for c in range(8):
                emit("v", lambda e, c=c: e.bn_stats(
                    bnb[:, 6 * c:6 * c + 6], og[:, c * 512:(c + 1) * 512]))
            sep("v")
            emit("v", lambda e: e.bn_aggr(stat2[:, 0:2], bnb[:, 0:48]), sgn)
            wait("t", sgn)
            wait("t", p7f)
            emit("t", lambda e: e.matmul(pb[7][0:1, 0:128], stat2[:, 0:1],
                                         identf[:, :], start=True, stop=True))
            emit("t", lambda e: e.matmul(pb[7][0:1, 128:256], stat2[:, 1:2],
                                         identf[:, :], start=True, stop=True),
                 st2)
            wait("v", st2)
            emit("v", lambda e: e.reduce_sum(cells[0:1, C_MUA:C_MUA + 1],
                                             pb[7][0:1, 0:64], axis=AX))
            emit("v", lambda e: e.reduce_sum(cells[0:1, C_MUB:C_MUB + 1],
                                             pb[7][0:1, 64:128], axis=AX))
            emit("v", lambda e: e.reduce_sum(cells[0:1, C_VA:C_VA + 1],
                                             pb[7][0:1, 128:192], axis=AX))
            emit("v", lambda e: e.reduce_sum(cells[0:1, C_VB:C_VB + 1],
                                             pb[7][0:1, 192:256], axis=AX))
            emit("v", lambda e: e.tensor_copy(srow[0:1, 0:128],
                                              pb[7][0:1, 0:128]), p7f)
            sep("v")
            emit("v", lambda e: e.tensor_mul(srow[0:1, 0:128],
                                             srow[0:1, 0:128],
                                             srow[0:1, 0:128]))
            sep("v")
            emit("v", lambda e: e.reduce_sum(cells[0:1, C_T0:C_T0 + 1],
                                             srow[0:1, 0:64], axis=AX))
            emit("v", lambda e: e.reduce_sum(cells[0:1, C_T1:C_T1 + 1],
                                             srow[0:1, 64:128], axis=AX))
            emit("v", lambda e: e.tensor_scalar_mul(cells[0:1, C_MUA:C_MUB + 1],
                                                    cells[0:1, C_MUA:C_MUB + 1],
                                                    1.0 / 64))
            sep("v")
            emit("v", lambda e: e.tensor_add(cells[0:1, C_VA:C_VB + 1],
                                             cells[0:1, C_VA:C_VB + 1],
                                             cells[0:1, C_T0:C_T1 + 1]))
            sep("v")
            emit("v", lambda e: e.tensor_scalar_mul(cells[0:1, C_VA:C_VB + 1],
                                                    cells[0:1, C_VA:C_VB + 1],
                                                    1.0 / 64))
            emit("v", lambda e: e.tensor_mul(cells[0:1, C_T0:C_T1 + 1],
                                             cells[0:1, C_MUA:C_MUB + 1],
                                             cells[0:1, C_MUA:C_MUB + 1]))
            sep("v")
            emit("v", lambda e: e.tensor_sub(cells[0:1, C_VA:C_VB + 1],
                                             cells[0:1, C_VA:C_VB + 1],
                                             cells[0:1, C_T0:C_T1 + 1]), sgn)
            wait("s", sgn)
            rsqrt_row(cells[0:1, C_RA:C_RB + 1], cells[0:1, C_VA:C_VB + 1],
                      srow[0:1, 0:2], srow2[0:1, 0:2], sgn, ss2)
            wait("t", sgn)
            wait("t", p6f)
            emit("t", lambda e: e.matmul(pb[6][0:128, 0:1], maskA[0:1, :],
                                         cells[0:1, C_RA:C_RA + 1],
                                         start=True, stop=False))
            emit("t", lambda e: e.matmul(pb[6][0:128, 0:1], maskB[0:1, :],
                                         cells[0:1, C_RB:C_RB + 1],
                                         start=False, stop=True))
            emit("t", lambda e: e.matmul(pb[6][0:128, 1:2], maskA[0:1, :],
                                         cells[0:1, C_MUA:C_MUA + 1],
                                         start=True, stop=False))
            emit("t", lambda e: e.matmul(pb[6][0:128, 1:2], maskB[0:1, :],
                                         cells[0:1, C_MUB:C_MUB + 1],
                                         start=False, stop=True), st2)
            wait("v", st2)
            emit("v", lambda e: e.tensor_copy(rvec[:, :], pb[6][0:128, 0:1]))
            emit("v", lambda e: e.tensor_copy(mvec[:, :], pb[6][0:128, 1:2]),
                 p6f)
            sep("v")
            emit("v", lambda e, pi=pi: e.tensor_mul(
                Avec[:, :], rvec[:, :], bfm[:, CB_GAM + pi:CB_GAM + pi + 1]))
            sep("v")
            emit("v", lambda e: e.tensor_mul(tvec[:, :], mvec[:, :],
                                             Avec[:, :]))
            sep("v")
            emit("v", lambda e, pi=pi: e.tensor_sub(
                Bvec[:, :], bfm[:, CB_BET + pi:CB_BET + pi + 1], tvec[:, :]),
                sgn)
            for tc in range(NTC):
                gsl = tc % 2
                wait("g", ginf[gsl])
                wait("g", gsf[0], gsf[0].n)
                wait("g", gsf[1], gsf[1].n)
                dma("g", gin[gsl][:, :], gate_sc[pi, tc], ginl[gsl])
                wait("v", ginl[gsl])
                wait("v", sgn, sgn.n)
                wait("v", gonf[gsl])
                emit("v", lambda e, tc=tc: e.tensor_scalar(
                    ont[:, :], og[:, tc * 512:(tc + 1) * 512], Avec[:, :],
                    Bvec[:, :], ALU.mult, ALU.add))
                sep("v")
                emit("v", lambda e, gsl=gsl: e.tensor_mul(
                    gont[gsl][:, :], ont[:, :], gin[gsl][:, :]),
                    (gonr[gsl], ginf[gsl]))
                wait("g", gonr[gsl])
                dma("g", gon_sc[pi, tc], gont[gsl][:, :], gonf[gsl])

        al.reset(Sm)
        gin8 = [al.buf([128, 8, 512], BF16, f"pgin8{i}") for i in range(2)]
        xr1 = [al.buf([128, E], F32, f"pxr1{i}") for i in range(2)]
        x2t = [al.buf([128, E], F32, f"px2t{i}") for i in range(2)]
        lnt5 = al.buf([128, E], F32, "plnt5")
        t5 = al.buf([128, 512], F32, "pt5")
        g8l = [S(f"pg8l{i}") for i in range(2)]
        g8f = [S(f"pg8f{i}") for i in range(2)]
        x1l = [S(f"px1l{i}") for i in range(2)]
        x1f = [S(f"px1f{i}") for i in range(2)]
        x2fr = [S(f"px2fr{i}") for i in range(2)]
        por = [S(f"ppor{i}") for i in range(4)]
        pof = [S(f"ppof{i}") for i in range(4)]
        swo = S("pswo")
        p5stat = S("p5stat")
        x2gate = Gate()
        wait("g", sQf, sQf.n)
        waitr("g", vrec[0])
        for pi in range(NP):
            dma("g", woTb[:, pi, :],
                woT_i[pi * 128:(pi + 1) * 128, :], swo)
        for tc in range(NTC):
            gsl = tc % 2
            wait("g", g8f[gsl])
            wait("g", gonf[0], gonf[0].n)
            wait("g", gonf[1], gonf[1].n)
            for pi in range(NP):
                dma("g", gin8[gsl][:, pi, :], gon_sc[pi, tc], g8l[gsl])
            wait("t", g8l[gsl])
            if tc == 0:
                wait("t", swo)
            for tti in range(4):
                tt = tc * 4 + tti
                bsel = tt % 2
                bl, br = 2 * bsel, 2 * bsel + 1
                wait("t", pof[bl])
                wait("t", pof[br])
                for pi in range(NP):
                    for half in range(2):
                        bk = bl if half == 0 else br
                        incs = []
                        if pi == NP - 1:
                            incs.append(por[bk])
                            if tti == 3 and half == 1:
                                incs.append(g8f[gsl])
                        emit("t", lambda e, bk=bk, pi=pi, tti=tti, half=half,
                             gsl=gsl: e.matmul(
                                 pb[bk][:, :],
                                 gin8[gsl][:, pi, tti * 128:(tti + 1) * 128],
                                 woTb[:, pi, half * 512:(half + 1) * 512],
                                 start=(pi == 0), stop=(pi == NP - 1)),
                             incs)
                xsl = tt % 2
                wait("y", x1f[xsl])
                dma("y", xr1[xsl][:, :], x1_sc[tt * 128:(tt + 1) * 128, :],
                    x1l[xsl])
                wait("v", x1l[xsl])
                wait("v", por[bl])
                wait("v", por[br])
                wait("v", x2fr[xsl])
                for half, bk in ((0, bl), (1, br)):
                    sep("v")
                    emit("v", lambda e, bk=bk, half=half: e.tensor_add(
                        t5[:, :], pb[bk][:, :],
                        borep[:, half * 512:(half + 1) * 512]), pof[bk])
                    sep("v")
                    emit("v", lambda e, xsl=xsl, half=half: e.tensor_add(
                        x2t[xsl][:, half * 512:(half + 1) * 512], t5[:, :],
                        xr1[xsl][:, half * 512:(half + 1) * 512]),
                        (x1f[xsl] if half == 1 else ()))
                sep("v")
                emit("v", lambda e, xsl=xsl: e.bn_stats(stat2[:, 0:6],
                                                        x2t[xsl][:, 0:512]))
                emit("v", lambda e, xsl=xsl: e.bn_stats(stat2[:, 6:12],
                                                        x2t[xsl][:, 512:1024]))
                sep("v")
                emit("v", lambda e: e.bn_aggr(bnb[:, 0:2], stat2[:, 0:12]))
                sep("v")
                emit("v", lambda e, tt=tt: e.tensor_copy(mu_a[:, tt:tt + 1],
                                                         bnb[:, 0:1]))
                emit("v", lambda e, tt=tt: e.tensor_copy(vv_a[:, tt:tt + 1],
                                                         bnb[:, 1:2]))
                sep("v")
                emit("v", lambda e, xsl=xsl, tt=tt: e.tensor_scalar(
                    lnt5[:, :], x2t[xsl][:, :], mu_a[:, tt:tt + 1], None,
                    ALU.subtract, ALU.bypass))
                sep("v")
                emit("v", lambda e, tt=tt: e.reduce_max(
                    mraw[:, tt:tt + 1], lnt5[:, :], axis=AX,
                    apply_absolute_value=True), p5stat)
                wait("y", p5stat)
                dma("y", x2_sc[tt * 128:(tt + 1) * 128, :], x2t[xsl][:, :],
                    x2fr[xsl])
                if stage == 2:
                    dma("y", out_t[tt * 128:(tt + 1) * 128, :], x2t[xsl][:, :],
                        x2fr[xsl])
                x2gate.per[tt] = (x2fr[xsl], x2fr[xsl].n)
        x2gate.full = [(x2fr[0], x2fr[0].n), (x2fr[1], x2fr[1].n)]
        al.reset(Sm)
        return x2gate, p5stat

    upto = "z"
    if isinstance(stage, str) and len(stage) == 2:
        upto = stage[1]
        stage = 1
    x1done, mmC1, sstore1 = ffn(
        "f1", x_in, x1_sc, 0, 1, first=True, ln_next=True, have_stats=False,
        dst2_ap=(out_t if stage == 1 else None), upto=upto)
    if stage == 1:
        if x1done is None:
            for sm in list(sems.values()):
                wait("y", sm)
        else:
            wait_gate("y", x1done)
        if _DUMP:
            dbg_t = nc.dram_tensor("dbg", [128, 128], F32,
                                   kind="ExternalOutput").ap()
            for sm in list(sems.values()):
                wait("y", sm)
            sdbg = S("dbgs")
            dma("y", dbg_t[0:1, 0:9], cells[0:1, 0:9], sdbg)
            dma("y", dbg_t[:, 32:36], rsc[:, 0:4], sdbg)
            dma("y", dbg_t[:, 40:62], svg[:, :], sdbg)
            dma("y", dbg_t[:, 62:84], svu[:, :], sdbg)
            dma("y", dbg_t[:, 84:106], sgate[:, :], sdbg)
            dma("y", dbg_t[:, 106:128], sup[:, :], sdbg)
            wait("y", sdbg)
        P.flush()
        return nc

    x2done, p5stat = attn_phase(x1done, mmC1, sstore1)
    if stage == 2:
        wait_gate("y", x2done)
        P.flush()
        return nc

    outdone, _, _ = ffn(
        "f2", x2_sc, out_t, 2, 3, first=False, ln_next=False, have_stats=True,
        src_gate=x2done, stats_gate=p5stat)
    wait_gate("y", outdone)
    P.flush()
    return nc


def _host_prep(inputs):
    f32 = np.float32

    def C(a):
        return np.ascontiguousarray(np.asarray(a), dtype=f32)

    x = C(inputs["L0_attn_x"])
    wq = C(inputs["L0_attn_wq"])
    wk = C(inputs["L0_attn_wk"])
    wv = C(inputs["L0_attn_wv"])
    wg = C(inputs["L0_attn_wg"])
    wo = C(inputs["L0_attn_wo"])
    wd = C(inputs["L0_attn_wd"])
    gamma = C(inputs["L0_attn_gamma"])
    beta = C(inputs["L0_attn_beta"])
    bq = C(inputs["L0_attn_bq"])
    bk = C(inputs["L0_attn_bk"])
    bv = C(inputs["L0_attn_bv"])
    bg = C(inputs["L0_attn_bg"])
    bo = C(inputs["L0_attn_bo"])
    bd = C(inputs["L0_attn_bd"])
    w_gate = C(inputs["L0_ffn_w_gate"])
    w_up = C(inputs["L0_ffn_w_up"])
    w_down = C(inputs["L0_ffn_w_down"])
    clip_x = float(np.asarray(inputs["L0_ffn_clip_x"]).ravel()[0])
    clip_down = float(np.asarray(inputs["L0_ffn_clip_down"]).ravel()[0])

    def fmcols(vec):
        return vec.reshape(8, 128).T

    bfm = np.concatenate([fmcols(v) for v in (bq, bk, bv, bg, gamma, beta)],
                         axis=1)

    hd = np.arange(128) % 64
    inv = 1.0 / 10000.0 ** (np.arange(0, 32, dtype=np.float64) * 2 / 64)
    freq = inv[hd % 32]
    pos = np.arange(T, dtype=np.float64)
    ang = pos[None, :] * freq[:, None]
    cosfm = np.cos(ang)
    sign = np.where(hd % 2 == 0, -1.0, 1.0)
    sinmfm = np.sin(ang) * sign[:, None]

    i_ = np.arange(128)[:, None]
    j_ = np.arange(512)[None, :]
    relum = np.concatenate(
        [np.maximum(128 * m + i_ - j_, 0).astype(f32) for m in range(4)],
        axis=1)
    qexpc = np.broadcast_to((512.0 - j_).astype(f32), (128, 512)).copy()
    fbias = (np.arange(128)[:, None] + 128.0 * np.arange(28)[None, :])
    onesr = np.ones((1, 128), f32)
    maskA = np.concatenate([np.ones((1, 64)), np.zeros((1, 64))], 1)
    maskB = np.concatenate([np.zeros((1, 64)), np.ones((1, 64))], 1)
    identf = np.eye(128, dtype=f32)
    constv = np.zeros((128, 4), f32)
    constv[:, 0] = EPS
    constv[:, 1] = -5.0 * LN2

    shared = {
        "wg_n": w_gate, "wu_n": w_up,
        "wgT": C(w_gate.T), "wuT": C(w_up.T),
        "wd_n": w_down, "wdT": C(w_down.T),
        "wqT": C(wq.T), "wkT": C(wk.T), "wvT": C(wv.T), "wgaT": C(wg.T),
        "woT": C(wo.T), "wdec": C(wd.T),
        "bfm": C(bfm), "bo_r": bo.reshape(1, E).copy(),
        "bd_r": bd.reshape(1, 16).copy(),
        "clip2": np.array([[clip_x, clip_down]], f32),
        "cosfm": C(cosfm), "sinmfm": C(sinmfm), "relum": relum,
        "qexp": qexpc, "fbias": C(fbias), "onesr": onesr,
        "maskA": C(maskA), "maskB": C(maskB), "identf": identf,
        "identb": identf.astype(ml_dtypes.bfloat16),
        "constv": constv,
    }
    in_maps = []
    for b in range(8):
        m = dict(shared)
        m["x"] = np.ascontiguousarray(x[b])
        in_maps.append(m)
    return in_maps


def kernel(**inputs):
    in_maps = _host_prep(inputs)
    nc = build_program(STAGE)
    res = run_bass_kernel_spmd(nc, in_maps, core_ids=list(range(8)))
    out = np.stack([np.asarray(res.results[i]["out"], dtype=np.float32)
                    for i in range(8)], axis=0)
    return out


# version 02
# speedup vs baseline: 1.2799x; 1.2799x over previous
import os as _os

import ml_dtypes
import numpy as np

import concourse.bass as bass
import concourse.mybir as mybir
from concourse.bass_utils import run_bass_kernel_spmd

F32 = mybir.dt.float32
BF16 = mybir.dt.bfloat16
AF = mybir.ActivationFunctionType
ALU = mybir.AluOpType
AX = mybir.AxisListType.X

T = 4096
E = 1024
I = 2816
NP = 8
NTT = 32
NTC = 8
NICH = 22
MAGIC = 1.5 * 2 ** 23
QMAX = 127.0
EPS = 1e-5
LN2 = float(np.log(2.0))

STAGE = 3
_NOMM = _os.environ.get("K_NOMM", "0") == "1"
_NOSILU = _os.environ.get("K_NOSILU", "0") == "1"
_NOTR = _os.environ.get("K_NOTR", "0") == "1"
_MMONLY = _os.environ.get("K_MMONLY", "0") == "1"
_DUMP = _os.environ.get("K_DUMP", "0") == "1"
_RACE = _os.environ.get("K_RACE", "0") == "1"


class Sem:
    def __init__(self, nc, name):
        self.h = nc.alloc_semaphore(name)
        self.n = 0


class Prog:
    def __init__(self, nc):
        self.nc = nc
        self.items = {k: [] for k in "tvsgy"}
        self.last = {}

    def flush(self):
        nc, items = self.nc, self.items
        self.items = {k: [] for k in "tvsgy"}
        with nc.Block() as blk:
            for key, reg in (("y", blk.sync), ("g", blk.gpsimd),
                             ("t", blk.tensor), ("v", blk.vector),
                             ("s", blk.scalar)):
                if items[key]:
                    def make(fs):
                        def run(eng):
                            for f in fs:
                                f(eng)
                        return run
                    reg(make(items[key]))


class Alloc:
    def __init__(self, nc, limit=None):
        self.nc = nc
        self.cur = (nc.sbuf_base + 63) & ~63
        self.limit = nc.sbuf_top - 64 if limit is None else limit
        self.peak = 0
        self.cnt = 0

    def mark(self):
        return self.cur

    def reset(self, m):
        self.cur = m

    def buf(self, shape, dtype, name=None):
        nbytes = int(np.prod(shape[1:])) * mybir.dt.size(dtype)
        nbytes = (nbytes + 63) & ~63
        off = self.cur
        self.cur += nbytes
        self.peak = max(self.peak, self.cur)
        assert self.cur <= self.limit, f"SBUF overflow at {name}: {self.cur}"
        self.cnt += 1
        return self.nc.alloc_sbuf_tensor_at(name or f"b{self.cnt}",
                                            list(shape), dtype, offset=off)


def build_program(stage=STAGE):
    nc = bass.Bass("TRN2", target_bir_lowering=False,
                   detect_race_conditions=_RACE)
    P = Prog(nc)
    al = Alloc(nc)
    sems = {}

    ALIAS = {}
    for n in ("fsvA fsvSc fsvB fsvD fsvW3 fsvH fsstore asv2 asvhead asgn "
              "asmrg aoggate asvones ap4free ap5free avbr p6f p7f sv0 "
              "fwf fwTf fwdf fwdtf fr2f0 fr2f1 fr2f2 frf0 frf1 frf2 fntr0 "
              "fntr1 fsltf0 fsltf1 fhr0 fhr1 fhr2 fh2f0 fh2f1 fh2f2 "
              "avbr0 avbr1 asQf "
              "fxrf0 fxrf1 fwready fdready anbr0 anbr1 axbf0 axbf1 "
              "fwf0 fwf1 fwTf0 fwTf1 aginf0 aginf1 agonr0 agonr1 "
              "avm").split():
        ALIAS[n] = "VM"
    for n in ("fssA ssM ss0 ass2 fssil0 fssil1 assf0 assf1 atrs agsr0 "
              "agsr1 ashead asm").split():
        ALIAS[n] = "SM"
    for n in ("fstA st0 ast2 fstW fstW2 fmm_done fmmC_done fxqf0 fxqf1 "
              "aavp aavf assr0 assr1 aatf0 aatf1 appr0 appr1 fnhfr0 fnhfr1 "
              "agpr0 agpr1 atm").split():
        ALIAS[n] = "TM"
    for n in "fscc agm".split():
        ALIAS[n] = "GM"
    for i in range(22):
        ALIAS[f"fnhr{i}"] = "VM"
    for a, b in (("frl0", "ra0"), ("frl1", "ra1"), ("frl2", "ra2"),
                 ("fr2l0", "ra0"), ("fr2l1", "ra1"), ("fr2l2", "ra2"),
                 ("axbl0", "ra0"), ("axbl1", "ra1"),
                 ("fxrl0", "ra0"), ("fxrl1", "ra1"),
                 ("px1l0", "ra0"), ("px1l1", "ra1"),
                 ("fxql0", "yb0"), ("fxql1", "yb1"),
                 ("fntf0", "yc0"), ("fntf1", "yc1"),
                 ("anbf0", "yc0"), ("anbf1", "yc1"),
                 ("fx1fr0", "yd0"), ("fx1fr1", "yd1"),
                 ("px2fr0", "yd0"), ("px2fr1", "yd1"),
                 ("fdst_done", "ydt"), ("px2done", "ydt"),
                 ("fh2l0", "ye0"), ("fh2l1", "ye1"), ("fh2l2", "ye2"),
                 ("pg8l0", "sgi0"), ("pg8l1", "sgi1"),
                 ("aginl0", "sgi0"), ("aginl1", "sgi1"),
                 ("agonf0", "sgo0"), ("agonf1", "sgo1"),
                 ("asgonst", "sgot"),
                 ("agsf0", "sgg0"), ("agsf1", "sgg1"),
                 ("fswq", "yw"), ("fswq2", "yw"), ("fwdl", "yw"),
                 ("fwdtl", "yw"), ("fsdq2", "yw"),
                 ("fwl0", "ywa"), ("fwl1", "ywb"),
                 ("fwTl0", "ywc"), ("fwTl1", "ywd"),
                 ("axlr", "yx"), ("ascs", "yx"),
                 ("fhfree0", "sf0"), ("fhfree1", "sf1"), ("fhfree2", "sf2"),
                 ("fhidall", "sft"),
                 ("avtr", "svt"), ("avbfree0", "svb0"), ("avbfree1", "svb1"),
                 ("aswl", "sgw"), ("aswqkv", "sgw"), ("pswo", "sgw"),
                 ("fsgd", "sgd"), ("fsspl", "sgsp"),
                 ("pg8f0", "TM"), ("pg8f1", "TM"),
                 ("p5stat", "VM"), ("asxm", "SM"),
                 ("ppof0", "VM"), ("ppof1", "VM"), ("ppof2", "VM"),
                 ("ppof3", "VM"),
                 ("appf0", "VM"), ("appf1", "VM"),
                 ("agpf0", "SM"), ("agpf1", "SM"),
                 ("fpgr0", "TM"), ("fpgr1", "TM"), ("fpgr2", "TM"),
                 ("fpgr3", "TM"),
                 ("fpdr0", "TM"), ("fpdr1", "TM"), ("fpdr2", "TM"),
                 ("fpdr3", "TM"),
                 ("ppor0", "TM"), ("ppor1", "TM"), ("ppor2", "TM"),
                 ("ppor3", "TM"),
                 ("aatr0", "atrs"), ("aatr1", "atrs"),
                 ("px2fr0", "yd0"), ("px2fr1", "yd1")):
        ALIAS[a] = b
    for n in ("fpgf0", "fpgf2"):
        ALIAS[n] = "SM"
    for n in ("fpgf1", "fpgf3"):
        ALIAS[n] = "VM"
    for n in ("fpdf0", "fpdf1", "fpdf2", "fpdf3"):
        ALIAS[n] = "VM"
    for n in ("ppof0", "ppof1", "ppof2", "ppof3"):
        ALIAS[n] = "VM"
    for n in ("px1f0", "px1f1"):
        ALIAS[n] = "VM"

    def S(name):
        cn = name
        seen = set()
        while cn in ALIAS and cn not in seen:
            seen.add(cn)
            cn = ALIAS[cn]
        if cn not in sems:
            sems[cn] = Sem(nc, cn)
        return sems[cn]

    MILE = {"v": S("VM"), "s": S("SM"), "t": S("TM"), "g": S("GM")}

    def dma(q, out, in_, ss, transpose=False, slow=False):
        if isinstance(ss, Sem):
            ss = (ss,)
        merged = {}
        order = []
        for sm in ss:
            sm.n += 16
            if id(sm.h) not in merged:
                merged[id(sm.h)] = [sm.h, 0]
                order.append(id(sm.h))
            merged[id(sm.h)][1] += 16
        hs = [tuple(merged[i]) for i in order]
        assert len(hs) <= 1, "HW: a DMA may carry only one then_inc"

        def run(e, o=out, i=in_, hh=tuple(hs), tr=transpose, sl=slow):
            ins = e.dma_start(out=o, in_=i, transpose=tr,
                              allow_slow_non_contiguous=sl)
            for h, k in hh:
                ins = ins.then_inc(h, k)
        P.items[q].append(run)

    def wait(q, sem, tgt=None):
        v = sem.n if tgt is None else tgt
        if v > 0:
            P.items[q].append(lambda e, s=sem.h, vv=v: e.wait_ge(s, vv))

    def emit(q, f, incs=(), ctl=False):
        if isinstance(incs, Sem):
            incs = ((incs, 1),)
        elif (isinstance(incs, tuple) and len(incs) == 2
              and isinstance(incs[0], Sem) and isinstance(incs[1], int)):
            incs = (incs,)
        merged = {}
        order = []
        for it in incs:
            sm, k = it if isinstance(it, tuple) else (it, 1)
            sm.n += k
            if id(sm.h) not in merged:
                merged[id(sm.h)] = [sm.h, 0]
                order.append(id(sm.h))
            merged[id(sm.h)][1] += k
        if not merged and not ctl:
            vm = MILE[q]
            vm.n += 1
            merged[id(vm.h)] = [vm.h, 1]
            order.append(id(vm.h))
            incs = (vm,)
        pairs = [list(merged[i]) for i in order]
        recs = [(sm, sm.n) for sm in
                {id(it if isinstance(it, Sem) else it[0]):
                 (it if isinstance(it, Sem) else it[0])
                 for it in incs}.values()]

        def run(e, ff=f, pp=pairs):
            ins = ff(e)
            for hh, kk in pp:
                ins = ins.then_inc(hh, kk)
        P.items[q].append(run)
        if not ctl:
            P.last[q] = (pairs, recs)

    def din(name, shape, dtype=F32):
        return nc.dram_tensor(name, list(shape), dtype, kind="ExternalInput").ap()

    def dsc(name, shape, dtype=F32):
        kind = "Internal"
        if _DUMP and name in ("hid_sc", "x1_sc"):
            kind = "ExternalOutput"
        return nc.dram_tensor(name, list(shape), dtype, kind=kind).ap()

    x_in = din("x", [T, E])
    wg_n = din("wg_n", [I, E])
    wu_n = din("wu_n", [I, E])
    wgT_i = din("wgT", [E, I])
    wuT_i = din("wuT", [E, I])
    wd_n = din("wd_n", [E, I])
    wdT_i = din("wdT", [I, E])
    wqT_i = din("wqT", [E, E])
    wkT_i = din("wkT", [E, E])
    wvT_i = din("wvT", [E, E])
    wgaT_i = din("wgaT", [E, E])
    woT_i = din("woT", [E, E])
    wdec_i = din("wdec", [E, 16])
    bfm_i = din("bfm", [128, 48])
    bo_i = din("bo_r", [1, E])
    bd_i = din("bd_r", [1, 16])
    clip_i = din("clip2", [1, 2])
    cos_i = din("cosfm", [128, T])
    sin_i = din("sinmfm", [128, T])
    relum_i = din("relum", [128, 2048])
    qexp_i = din("qexp", [128, 512])
    fbias_i = din("fbias", [128, 28])
    ones_i = din("onesr", [1, 128])
    maskA_i = din("maskA", [1, 128])
    maskB_i = din("maskB", [1, 128])
    ident_i = din("identf", [128, 128])
    identb_i = din("identb", [128, 128], BF16)
    cv_i = din("constv", [128, 4])

    out_t = nc.dram_tensor("out", [T, E], F32, kind="ExternalOutput").ap()

    x1_sc = dsc("x1_sc", [T, E])
    x2_sc = dsc("x2_sc", [T, E])
    hid_sc = dsc("hid_sc", [NICH, NTC, 128, 512])
    gate_sc = dsc("gate_sc", [NP, NTC, 128, 512], BF16)
    gon_sc = dsc("gon_sc", [NP, NTC, 128, 512], BF16)
    wspg_sc = dsc("wspg_sc", [8, 128, I], BF16)
    wspu_sc = dsc("wspu_sc", [8, 128, I], BF16)
    wspd_sc = dsc("wspd_sc", [NICH, 128, E], BF16)
    svec_sc = dsc("svec_sc", [1, I])
    cc_in = [dsc(f"cc_in{i}", [1, 1]) for i in range(4)]
    cc_out = [dsc(f"cc_out{i}", [1, 1]) for i in range(4)]

    constv = al.buf([128, 4], F32, "constv")
    COL_EPS, COL_M5LN2 = 0, 1
    onesr = al.buf([1, 128], F32, "onesr")
    maskA = al.buf([1, 128], F32, "maskA")
    maskB = al.buf([1, 128], F32, "maskB")
    identf = al.buf([128, 128], F32, "identf")
    identb = al.buf([128, 128], BF16, "identb")
    bfm = al.buf([128, 48], F32, "bfm")
    CB_Q, CB_K, CB_V, CB_G, CB_GAM, CB_BET = 0, 8, 16, 24, 32, 40
    borep = al.buf([128, E], F32, "borep")
    bdrow = al.buf([1, 16], F32, "bdrow")
    fbias = al.buf([128, 28], F32, "fbias")
    qexp = al.buf([128, 512], F32, "qexp")
    relum = al.buf([128, 2048], F32, "relum")
    clips = al.buf([1, 2], F32, "clips")
    mu_a = al.buf([128, NTT], F32, "mu_a")
    vv_a = al.buf([128, NTT], F32, "vv_a")
    rr_a = al.buf([128, NTT], F32, "rr_a")
    mraw = al.buf([128, NTT], F32, "mraw")
    rq1 = al.buf([128, NTT], F32, "rq1")
    rq2 = al.buf([128, NTT], F32, "rq2")
    stat2 = al.buf([128, 12], F32, "stat2")
    cells = al.buf([1, 32], F32, "cells")
    (C_SIGX, C_SIGD, C_LMAX, C_GMAX, C_SX, C_ISX, C_SH, C_ISH, C_SHHALF,
     C_MUA, C_MUB, C_VA, C_VB, C_RA, C_RB, C_T0, C_T1, C_T2) = range(18)
    rsc = al.buf([128, 8], F32, "rsc")
    R_ISX, R_SX, R_ISH, R_SHH = 0, 1, 2, 3
    sgate = al.buf([128, NICH], F32, "sgate")
    sup = al.buf([128, NICH], F32, "sup")
    svg = al.buf([128, NICH], F32, "svg")
    svu = al.buf([128, NICH], F32, "svu")
    ldrow = al.buf([1, 16], F32, "ldrow")
    ldb = al.buf([128, 16], F32, "ldb")
    srow = al.buf([1, 512], F32, "srow")
    srow2 = al.buf([1, 512], F32, "srow2")
    wdTb = al.buf([128, 128], F32, "wdTb")
    mean8 = al.buf([128, 8], F32, "mean8")
    bnb = al.buf([128, 48], F32, "bnb")
    bo_row = al.buf([1, E], F32, "bo_row")
    rvec = al.buf([128, 1], F32, "rvec")
    mvec = al.buf([128, 1], F32, "mvec")
    Avec = al.buf([128, 1], F32, "Avec")
    Bvec = al.buf([128, 1], F32, "Bvec")
    tvec = al.buf([128, 1], F32, "tvec")

    W0 = al.mark()
    WBYTES = 98304
    al.cur = W0 + WBYTES

    def wbuf(off, shape, dtype, name):
        nb = int(np.prod(shape[1:])) * mybir.dt.size(dtype)
        assert off + nb <= WBYTES, name
        return nc.alloc_sbuf_tensor_at(name, list(shape), dtype, offset=W0 + off)

    mgT = wbuf(0, [128, 8, I], BF16, "mgT")
    muT = wbuf(45056, [128, 8, I], BF16, "muT")
    mdT = wbuf(0, [128, NICH, E], BF16, "mdT")
    cosfm = wbuf(0, [128, T], F32, "cosfm")
    sinmfm = wbuf(16384, [128, T], F32, "sinmfm")
    xln2fm = wbuf(32768, [128, 8, T], BF16, "xln2fm")
    woTb = wbuf(0, [128, 8, E], BF16, "woTb")

    S0 = al.mark()
    pb = [nc.alloc_psum_tensor(f"pb{i}", [128, 512], F32) for i in range(8)]
    p6f = S("p6f")
    p7f = S("p7f")

    def waitr(q, rec):
        if rec is not None:
            wait(q, rec[0], rec[1])

    def load_lnexp(e):
        import concourse.mybir as _mb
        return e.add_instruction(_mb.InstLoadActFuncSet(
            name=e.bass.get_next_instruction_name(),
            act_func_set_id=6, ins=[], outs=[]))

    class Gate:
        def __init__(self):
            self.full = []
            self.per = {}

    def wait_gate(q, g):
        for sm, n in g.full:
            wait(q, sm, n)

    def chain(q):
        le = P.last.get(q)
        if le is None:
            return None
        pairs, recs = le
        if not recs:
            vm = MILE[q]
            vm.n += 1
            pairs.append([vm.h, 1])
            recs = [(vm, vm.n)]
            P.last[q] = (pairs, recs)
        return recs[0]

    def sep(q):
        waitr(q, chain(q))

    trec = {}

    def pe_tr(dst_ap, src_ap, bank, ident_ap, np_, nf):
        TMs, VMs = S("atm"), S("avm")
        waitr("t", trec.get(bank))
        emit("t", lambda e: e.matmul(pb[bank][0:nf, 0:np_], src_ap, ident_ap,
                                     start=True, stop=True), TMs)
        waitr("v", (TMs, TMs.n))
        emit("v", lambda e: e.tensor_copy(dst_ap, pb[bank][0:nf, 0:np_]), VMs)
        trec[bank] = (VMs, VMs.n)
        return (VMs, VMs.n)

    def rsqrt_batch(dst, src, n, sv, ss):
        wait("s", sv)
        sep("s")
        emit("s", lambda e: e.activation(rq1[:, 0:n], src, AF.Ln,
                                         bias=constv[:, COL_EPS:COL_EPS + 1]))
        sep("s")
        emit("s", lambda e: e.activation(dst, rq1[:, 0:n], AF.Exp, scale=-0.5), ss)
        wait("v", ss)
        sep("v")
        emit("v", lambda e: e.tensor_scalar_add(rq1[:, 0:n], src, EPS))
        emit("v", lambda e: e.tensor_mul(rq2[:, 0:n], dst, dst))
        sep("v")
        emit("v", lambda e: e.tensor_mul(rq2[:, 0:n], rq2[:, 0:n], rq1[:, 0:n]))
        sep("v")
        emit("v", lambda e: e.tensor_scalar(rq2[:, 0:n], rq2[:, 0:n], -0.5, 1.5,
                                            ALU.mult, ALU.add))
        sep("v")
        emit("v", lambda e: e.tensor_mul(dst, dst, rq2[:, 0:n]), sv)

    def rsqrt_row(dst, src, t1, t2, sv, ss):
        wait("s", sv)
        sep("s")
        emit("s", lambda e: e.activation(t1, src, AF.Ln,
                                         bias=constv[0:1, COL_EPS:COL_EPS + 1]))
        sep("s")
        emit("s", lambda e: e.activation(dst, t1, AF.Exp, scale=-0.5), ss)
        wait("v", ss)
        sep("v")
        emit("v", lambda e: e.tensor_scalar_add(t1, src, EPS))
        emit("v", lambda e: e.tensor_mul(t2, dst, dst))
        sep("v")
        emit("v", lambda e: e.tensor_mul(t2, t2, t1))
        sep("v")
        emit("v", lambda e: e.tensor_scalar(t2, t2, -0.5, 1.5, ALU.mult, ALU.add))
        sep("v")
        emit("v", lambda e: e.tensor_mul(dst, dst, t2), sv)

    def bcast_cell(cell_col, rsc_col, sv, st):
        wait("t", p6f)
        emit("t", lambda e: e.matmul(pb[6][0:128, 0:1], onesr[0:1, :],
                                     cells[0:1, cell_col:cell_col + 1],
                                     start=True, stop=True), st)
        wait("v", st)
        emit("v", lambda e: e.tensor_copy(rsc[:, rsc_col:rsc_col + 1],
                                          pb[6][0:128, 0:1]), (sv, p6f))

    def cross_max_to_cell(src_ap, cell_col, sv, st):
        wait("t", sv)
        wait("t", p6f)
        emit("t", lambda e: e.matmul(pb[6][0:1, 0:128], src_ap, identf[:, :],
                                     start=True, stop=True), st)
        wait("v", st)
        emit("v", lambda e: e.reduce_max(cells[0:1, cell_col:cell_col + 1],
                                         pb[6][0:1, 0:128], axis=AX), (sv, p6f))

    def collective(idx, c_src, c_dst, sv, sgd, scc):
        wait("g", sv)
        dma("g", cc_in[idx], cells[0:1, c_src:c_src + 1], sgd)
        wait("g", sgd)
        emit("g", lambda e: e.collective_compute(
            "AllReduce", ALU.max, replica_groups=[list(range(8))],
            ins=[cc_in[idx]], outs=[cc_out[idx]]), scc)
        wait("g", scc)
        dma("g", cells[0:1, c_dst:c_dst + 1], cc_out[idx], sgd)
        wait("v", sgd)

    sc_pre = S("sc_pre")
    for dst, src in ((constv[:, :], cv_i), (onesr[:, :], ones_i),
                     (maskA[:, :], maskA_i), (maskB[:, :], maskB_i),
                     (identf[:, :], ident_i), (identb[:, :], identb_i),
                     (bfm[:, :], bfm_i),
                     (bdrow[:, :], bd_i), (fbias[:, :], fbias_i),
                     (qexp[:, :], qexp_i), (relum[:, :], relum_i),
                     (clips[:, :], clip_i), (bo_row[:, :], bo_i),
                     (wdTb[:, :].rearrange("p (c f) -> p c f", f=16),
                      wdec_i.rearrange("(c p) f -> p c f", p=128))):
        dma("y", dst, src, sc_pre, slow=True)

    ss0, sv0, st0 = S("ss0"), S("sv0"), S("st0")
    emit("s", load_lnexp, ctl=True)
    wait("s", sc_pre)
    emit("s", lambda e: e.activation(cells[0:1, 0:2], clips[0:1, 0:2], AF.Exp,
                                     scale=-1.0), ss0)
    wait("v", ss0)
    sep("v")
    emit("v", lambda e: e.tensor_scalar_add(cells[0:1, 0:2], cells[0:1, 0:2], 1.0))
    sep("v")
    emit("v", lambda e: e.reciprocal(cells[0:1, 0:2], cells[0:1, 0:2]), sv0)
    wait("t", sc_pre)
    for c in range(2):
        emit("t", lambda e, c=c: e.matmul(pb[6 + c][:, :], onesr[0:1, :],
                                          bo_row[0:1, c * 512:(c + 1) * 512],
                                          start=True, stop=True), st0)
    wait("v", st0)
    emit("v", lambda e: e.tensor_copy(borep[:, 0:512], pb[6][:, :]), (sv0, p6f))
    emit("v", lambda e: e.tensor_copy(borep[:, 512:1024], pb[7][:, :]),
         (sv0, p7f))

    def ffn(tag, src_ap, dst_ap, cc_a, cc_b, first, ln_next, have_stats,
            src_gate=None, stats_gate=None, dst2_ap=None, upto="z"):
        Sm = al.mark()
        svA, ssA, stA = S("fsvA"), S("fssA"), S("fstA")

        if not have_stats:
            xt = [al.buf([128, E], F32, f"{tag}xt{i}") for i in range(3)]
            tmpA = al.buf([128, E], F32, f"{tag}tmpA")
            rload = [S(f"frl{i}") for i in range(3)]
            rfree = [S(f"frf{i}") for i in range(3)]
            for tt in range(NTT):
                sl = tt % 3
                wait("y", rfree[sl])
                if src_gate is not None:
                    waitr("y", src_gate.per.get(tt))
                dma("y", xt[sl][:, :], src_ap[tt * 128:(tt + 1) * 128, :],
                    rload[sl])
                wait("v", rload[sl])
                emit("v", lambda e, sl=sl: e.bn_stats(stat2[:, 0:6],
                                                      xt[sl][:, 0:512]))
                emit("v", lambda e, sl=sl: e.bn_stats(stat2[:, 6:12],
                                                      xt[sl][:, 512:1024]))
                sep("v")
                emit("v", lambda e: e.bn_aggr(bnb[:, 0:2], stat2[:, 0:12]))
                sep("v")
                emit("v", lambda e, tt=tt: e.tensor_copy(mu_a[:, tt:tt + 1],
                                                         bnb[:, 0:1]))
                emit("v", lambda e, tt=tt: e.tensor_copy(vv_a[:, tt:tt + 1],
                                                         bnb[:, 1:2]))
                sep("v")
                emit("v", lambda e, sl=sl, tt=tt: e.tensor_scalar(
                    tmpA[:, :], xt[sl][:, :], mu_a[:, tt:tt + 1], None,
                    ALU.subtract, ALU.bypass), rfree[sl])
                sep("v")
                emit("v", lambda e, tt=tt: e.reduce_max(
                    mraw[:, tt:tt + 1], tmpA[:, :], axis=AX,
                    apply_absolute_value=True), svA)
            al.reset(Sm)
        else:
            if stats_gate is not None:
                wait("v", stats_gate)
                wait("s", stats_gate)
        rsqrt_batch(rr_a[:, :], vv_a[:, :], NTT, svA, ssA)
        sep("v")
        emit("v", lambda e: e.tensor_mul(rq1[:, :], mraw[:, :], rr_a[:, :]))
        sep("v")
        emit("v", lambda e: e.reduce_max(stat2[:, 0:1], rq1[:, :], axis=AX), svA)
        cross_max_to_cell(stat2[:, 0:1], C_LMAX, svA, stA)
        sgd, scc = S("fsgd"), S("fscc")
        collective(cc_a, C_LMAX, C_GMAX, svA, sgd, scc)
        emit("v", lambda e: e.tensor_mul(cells[0:1, C_SX:C_SX + 1],
                                         cells[0:1, C_GMAX:C_GMAX + 1],
                                         cells[0:1, C_SIGX:C_SIGX + 1]))
        sep("v")
        emit("v", lambda e: e.tensor_scalar_mul(cells[0:1, C_SX:C_SX + 1],
                                                cells[0:1, C_SX:C_SX + 1],
                                                1.0 / QMAX))
        sep("v")
        emit("v", lambda e: e.reciprocal(cells[0:1, C_ISX:C_ISX + 1],
                                         cells[0:1, C_SX:C_SX + 1]), svA)
        wait("t", svA)
        bcast_cell(C_ISX, R_ISX, svA, stA)
        bcast_cell(C_SX, R_SX, svA, stA)
        if upto == "a":
            return None, None, None

        swq = S("fswq")
        wready = S("fwready")
        sspl = S("fsspl")
        if first:
            wl = [S(f"fwl{i}") for i in range(2)]
            wf = [S(f"fwf{i}") for i in range(2)]
            wnat = [al.buf([128, E], F32, f"{tag}wn{i}") for i in range(2)]
            for mat in range(2):
                w_nat = (wg_n, wu_n)[mat]
                sdst = (sgate, sup)[mat]
                for ich in range(NICH):
                    it = mat * NICH + ich
                    sl = it % 2
                    wait("y", wf[sl])
                    dma("y", wnat[sl][:, :], w_nat[ich * 128:(ich + 1) * 128, :],
                        wl[sl])
                    wait("v", wl[sl])
                    emit("v", lambda e, sl=sl, sdst=sdst, ich=ich: e.reduce_max(
                        sdst[:, ich:ich + 1], wnat[sl][:, :], axis=AX,
                        apply_absolute_value=True), wf[sl])
            sep("v")
            emit("v", lambda e: e.tensor_scalar_mul(sgate[:, :], sgate[:, :],
                                                    1.0 / QMAX))
            emit("v", lambda e: e.tensor_scalar_mul(sup[:, :], sup[:, :],
                                                    1.0 / QMAX))
            wT = [al.buf([128, I], F32, f"{tag}wTt{i}") for i in range(2)]
            irb = al.buf([128, I], F32, f"{tag}irb")
            irow = al.buf([1, I], F32, f"{tag}irow")
            sg_r = svec_sc.rearrange("one (c p) -> p (c one)", p=128)
            wTl = [S(f"fwTl{i}") for i in range(2)]
            wTf = [S(f"fwTf{i}") for i in range(2)]
            stW, svSc, svB = S("fstW"), S("fsvSc"), S("fsvB")
            wtct = 0
            for mat in range(2):
                ssrc = (sgate, sup)[mat]
                wT_in = (wgT_i, wuT_i)[mat]
                mdst = (mgT, muT)[mat]
                wsp = (wspg_sc, wspu_sc)[mat]
                wait("v", swq)
                sep("v")
                emit("v", lambda e, ssrc=ssrc: e.reciprocal(rq1[:, 0:NICH], ssrc[:, :]),
                     svSc)
                wait("y", svSc)
                dma("y", sg_r, rq1[:, 0:NICH], swq, slow=True)
                wait("y", swq)
                dma("y", irow[:, :], svec_sc, swq)
                wait("t", swq)
                for c in range(6):
                    w = min(512, I - c * 512)
                    wait("t", p6f)
                    emit("t", lambda e, c=c, w=w: e.matmul(
                        pb[6][0:128, 0:w], onesr[0:1, :],
                        irow[0:1, c * 512:c * 512 + w], start=True, stop=True),
                        stW)
                    wait("v", stW)
                    emit("v", lambda e, c=c, w=w: e.tensor_copy(
                        irb[:, c * 512:c * 512 + w], pb[6][0:128, 0:w]),
                        (svB, p6f))
                for hch in range(8):
                    sl = wtct % 2
                    wait("y", wTf[sl])
                    wtct += 1
                    dma("y", wT[sl][:, :], wT_in[hch * 128:(hch + 1) * 128, :],
                        wTl[sl])
                    wait("v", wTl[sl])
                    wait("v", svB)
                    emit("v", lambda e, sl=sl: e.tensor_mul(
                        wT[sl][:, :], wT[sl][:, :], irb[:, :]))
                    sep("v")
                    emit("v", lambda e, sl=sl: e.tensor_scalar(
                        wT[sl][:, :], wT[sl][:, :], MAGIC, MAGIC,
                        ALU.add, ALU.subtract))
                    sep("v")
                    emit("v", lambda e, sl=sl, mdst=mdst, hch=hch: e.tensor_copy(
                        mdst[:, hch, :], wT[sl][:, :]), (wTf[sl], wready))
                    wait("g", wready)
                    dma("g", wsp[hch], mdst[:, hch, :], sspl)
            al.reset(Sm)
        else:
            wait("y", sspl, sspl.n)
            if src_gate is not None:
                wait_gate("y", src_gate)
            for hch in range(8):
                dma("y", mgT[:, hch, :], wspg_sc[hch], swq)
                dma("y", muT[:, hch, :], wspu_sc[hch], swq)
        wrec = (wready, wready.n) if first else (swq, swq.n)
        wait("v", svA)
        emit("v", lambda e: e.tensor_scalar(svg[:, :], sgate[:, :],
                                            rsc[:, R_SX:R_SX + 1], None,
                                            ALU.mult, ALU.bypass))
        emit("v", lambda e: e.tensor_scalar(svu[:, :], sup[:, :],
                                            rsc[:, R_SX:R_SX + 1], None,
                                            ALU.mult, ALU.bypass), svA)
        if upto == "w":
            return None, None, None

        xt2 = [al.buf([128, E], F32, f"{tag}x2t{i}") for i in range(3)]
        ntile = [al.buf([128, E], BF16, f"{tag}nt{i}") for i in range(2)]
        ntfree = {}
        xqrec = {}
        tq = al.buf([128, E], F32, f"{tag}tq")
        xq = [al.buf([128, 8, 512], BF16, f"{tag}xq{i}") for i in range(2)]
        slt = [al.buf([128, 512], F32, f"{tag}slt{i}") for i in range(2)]
        hidt = [al.buf([128, 512], F32, f"{tag}hidt{i}") for i in range(3)]
        hmx = al.buf([128, 2], F32, f"{tag}hmx")
        r2load = [S(f"fr2l{i}") for i in range(3)]
        r2free = [S(f"fr2f{i}") for i in range(3)]
        ntr = [S(f"fntr{i}") for i in range(2)]
        xqf = [S(f"fxqf{i}") for i in range(2)]
        pgr = [S(f"fpgr{i}") for i in range(4)]
        pgf = [S(f"fpgf{i}") for i in range(4)]
        s_sil = [S(f"fssil{i}") for i in range(2)]
        sltf = [S(f"fsltf{i}") for i in range(2)]
        hr = [S(f"fhr{i}") for i in range(3)]
        hfree = [S(f"fhfree{i}") for i in range(3)]
        hstrec = {}
        bfree = {}
        sltrec = {}
        mm_done = S("fmm_done")
        emit("v", lambda e: e.memset(hmx[:, 0:1], 0.0), svA)
        emit("s", lambda e: e.preload_act_table(AF.Silu), ctl=True)
        waitr("s", wrec)
        waitr("t", wrec)
        for tc in range(NTC):
            xsl = tc % 2
            wait("y", xqf[xsl])
            for tti in range(4):
                tt = tc * 4 + tti
                sl = tt % 3
                npar = tt % 2
                wait("y", r2free[sl])
                if src_gate is not None:
                    wait_gate("y", src_gate)
                dma("y", xt2[sl][:, :], src_ap[tt * 128:(tt + 1) * 128, :],
                    r2load[sl])
                wait("v", r2load[sl])
                if tt == 0:
                    wait("v", svA)
                waitr("v", ntfree.get(npar))
                emit("v", lambda e, sl=sl, tt=tt: e.tensor_scalar(
                    tq[:, :], xt2[sl][:, :], mu_a[:, tt:tt + 1], rr_a[:, tt:tt + 1],
                    ALU.subtract, ALU.mult), r2free[sl])
                sep("v")
                emit("v", lambda e: e.tensor_scalar(
                    tq[:, :], tq[:, :], rsc[:, R_ISX:R_ISX + 1], MAGIC, ALU.mult, ALU.add))
                sep("v")
                emit("v", lambda e: e.tensor_scalar(
                    tq[:, :], tq[:, :], MAGIC, -(QMAX + 1.0), ALU.subtract, ALU.max))
                sep("v")
                emit("v", lambda e, npar=npar: e.tensor_scalar_min(
                    ntile[npar][:, :], tq[:, :], QMAX), ntr[npar])
                waitr("t", (S("avm"), S("avm").n))
                for d in ([] if _NOTR else range(8)):
                    lastrec = pe_tr(xq[xsl][:, d, tti * 128:(tti + 1) * 128],
                                    ntile[npar][:, d * 128:(d + 1) * 128],
                                    4 + (d % 2), identb[:, :], 128, 128)
                if not _NOTR:
                    ntfree[npar] = (S("atm"), S("atm").n)
                    xqrec[xsl] = lastrec
            waitr("t", xqrec.get(xsl))
            for ich in ([] if _NOMM else range(NICH)):
                bsel = (tc * NICH + ich) % 2
                bg, bu = 2 * bsel, 2 * bsel + 1
                spar = (tc * NICH + ich) % 2
                if _MMONLY:
                    for d in range(8):
                        emit("t", lambda e, d=d, xsl=xsl: e.matmul(
                            pb[0][:, :], mgT[:, d, 0:128],
                            xq[xsl][:, d, :], start=True, stop=True),
                            S("fpgr0") if d == 7 else ())
                    continue
                waitr("t", bfree.get(bg))
                waitr("t", bfree.get(bu))
                for d in range(8):
                    emit("t", lambda e, bg=bg, ich=ich, d=d, xsl=xsl: e.matmul(
                        pb[bg][:, :], mgT[:, d, ich * 128:(ich + 1) * 128],
                        xq[xsl][:, d, :], start=(d == 0), stop=(d == 7)),
                        pgr[bg] if d == 7 else ())
                for d in range(8):
                    fin = (d == 7) and ich == NICH - 1
                    incs = []
                    if d == 7:
                        incs.append(pgr[bu])
                    if fin:
                        incs.append(xqf[xsl])
                        if tc == NTC - 1:
                            incs.append(mm_done)
                    emit("t", lambda e, bu=bu, ich=ich, d=d, xsl=xsl: e.matmul(
                        pb[bu][:, :], muT[:, d, ich * 128:(ich + 1) * 128],
                        xq[xsl][:, d, :], start=(d == 0), stop=(d == 7)), incs)
                if _NOSILU:
                    hsl = (tc * NICH + ich) % 3
                    wait("s", pgr[bg])
                    emit("s", lambda e, bg=bg, hsl=hsl: e.copy(
                        slt[spar][:, :], pb[bg][:, :]), (s_sil[spar], pgf[bg]))
                    wait("v", hfree[hsl])
                    wait("v", pgr[bu])
                    wait("v", s_sil[spar])
                    emit("v", lambda e, bu=bu, hsl=hsl, spar=spar:
                         e.tensor_add(hidt[hsl][:, :], pb[bu][:, :],
                                      slt[spar][:, :]), (pgf[bu], hr[hsl]))
                    wait("s", hr[hsl])
                    dma("s", hid_sc[ich, tc], hidt[hsl][:, :], hfree[hsl])
                    hstrec[tc * NICH + ich] = (hfree[hsl], hfree[hsl].n)
                    continue
                wait("s", pgr[bg])
                waitr("s", sltrec.get(spar))
                emit("s", lambda e, bg=bg, ich=ich, spar=spar: e.activation(
                    slt[spar][:, :], pb[bg][:, :], AF.Silu,
                    scale=svg[:, ich:ich + 1]), (s_sil[spar], pgf[bg]))
                bfree[bg] = (pgf[bg], pgf[bg].n)
                hsl = (tc * NICH + ich) % 3
                wait("v", hfree[hsl])
                wait("v", pgr[bu])
                wait("v", s_sil[spar])
                emit("v", lambda e, bu=bu, ich=ich, hsl=hsl, spar=spar:
                     e.scalar_tensor_tensor(
                         hidt[hsl][:, :], pb[bu][:, :], svu[:, ich:ich + 1],
                         slt[spar][:, :], ALU.mult, ALU.mult),
                     (pgf[bu], sltf[spar]))
                bfree[bu] = (pgf[bu], pgf[bu].n)
                sltrec[spar] = (sltf[spar], sltf[spar].n)
                sep("v")
                emit("v", lambda e, hsl=hsl: e.reduce_max(
                    hmx[:, 1:2], hidt[hsl][:, :], axis=AX,
                    apply_absolute_value=True))
                sep("v")
                emit("v", lambda e: e.tensor_max(hmx[:, 0:1], hmx[:, 0:1],
                                                 hmx[:, 1:2]), hr[hsl])
                wait("s", hr[hsl])
                dma("s", hid_sc[ich, tc], hidt[hsl][:, :], hfree[hsl])
                hstrec[tc * NICH + ich] = (hfree[hsl], hfree[hsl].n)

        svH = S("fsvH")
        cross_max_to_cell(hmx[:, 0:1], C_LMAX, svH, stA)
        collective(cc_b, C_LMAX, C_GMAX, svH, sgd, scc)
        emit("v", lambda e: e.tensor_mul(cells[0:1, C_SH:C_SH + 1],
                                         cells[0:1, C_GMAX:C_GMAX + 1],
                                         cells[0:1, C_SIGD:C_SIGD + 1]))
        sep("v")
        emit("v", lambda e: e.tensor_scalar_mul(cells[0:1, C_SH:C_SH + 1],
                                                cells[0:1, C_SH:C_SH + 1],
                                                1.0 / QMAX))
        sep("v")
        emit("v", lambda e: e.reciprocal(cells[0:1, C_ISH:C_ISH + 1],
                                         cells[0:1, C_SH:C_SH + 1]))
        emit("v", lambda e: e.tensor_scalar_mul(
            cells[0:1, C_SHHALF:C_SHHALF + 1], cells[0:1, C_SH:C_SH + 1], 0.5),
            svH)
        wait("t", svH)
        bcast_cell(C_ISH, R_ISH, svH, stA)
        bcast_cell(C_SHHALF, R_SHH, svH, stA)
        if upto == "b":
            return None, None, None

        Sd = al.mark()
        dready = S("fdready")
        if first:
            swd = al.buf([128, 8], F32, f"{tag}swd")
            Sd2 = al.mark()
            wdn = al.buf([128, I], F32, f"{tag}wdn")
            wdl = S("fwdl")
            wdf = S("fwdf")
            svD = S("fsvD")
            for ech in range(8):
                wait("y", wdf)
                dma("y", wdn[:, :], wd_n[ech * 128:(ech + 1) * 128, :], wdl)
                wait("v", wdl)
                emit("v", lambda e, ech=ech, swd=swd: e.reduce_max(
                    swd[:, ech:ech + 1], wdn[:, :], axis=AX,
                    apply_absolute_value=True), wdf)
            al.reset(Sd2)
            sep("v")
            emit("v", lambda e, swd=swd: e.tensor_scalar_mul(
                swd[:, :], swd[:, :], 1.0 / QMAX))
            svd_r = svec_sc.rearrange("one (c p) -> p (c one)", p=128)
            sep("v")
            emit("v", lambda e, swd=swd: e.reciprocal(rq1[:, 0:8], swd[:, :]),
                 svD)
            irowd = al.buf([1, E], F32, f"{tag}irowd")
            srowd = al.buf([1, E], F32, f"{tag}srowd")
            swq2 = S("fswq2")
            wait("y", svD)
            dma("y", svd_r[:, 0:8], rq1[:, 0:8], swq2, slow=True)
            wait("y", swq2)
            dma("y", irowd[:, :], svec_sc[0:1, 0:E], swq2)
            wait("y", swq2)
            dma("y", svd_r[:, 0:8], swd[:, :], swq2, slow=True)
            wait("y", swq2)
            dma("y", srowd[:, :], svec_sc[0:1, 0:E], swq2)
            irbd = al.buf([128, E], F32, f"{tag}irbd")
            srbd = al.buf([128, E], F32, f"{tag}srbd")
            stW2, svW3 = S("fstW2"), S("fsvW3")
            wait("t", swq2)
            for (row, dstb) in ((irowd, irbd), (srowd, srbd)):
                for c in range(2):
                    wait("t", p6f)
                    emit("t", lambda e, row=row, c=c: e.matmul(
                        pb[6][:, :], onesr[0:1, :],
                        row[0:1, c * 512:(c + 1) * 512], start=True, stop=True),
                        stW2)
                    wait("v", stW2)
                    emit("v", lambda e, dstb=dstb, c=c: e.tensor_copy(
                        dstb[:, c * 512:(c + 1) * 512], pb[6][:, :]),
                        (svW3, p6f))
            wdt = al.buf([128, E], F32, f"{tag}wdt")
            wdtl = S("fwdtl")
            wdtf = S("fwdtf")
            for ich in range(NICH):
                wait("y", wdtf)
                dma("y", wdt[:, :], wdT_i[ich * 128:(ich + 1) * 128, :], wdtl)
                wait("v", wdtl)
                wait("v", svW3)
                if ich == 0:
                    wait("v", mm_done)
                    wait("v", sspl, sspl.n)
                emit("v", lambda e: e.tensor_mul(wdt[:, :], wdt[:, :],
                                                 irbd[:, :]))
                sep("v")
                emit("v", lambda e: e.tensor_scalar(
                    wdt[:, :], wdt[:, :], MAGIC, MAGIC, ALU.add, ALU.subtract))
                sep("v")
                emit("v", lambda e: e.tensor_mul(wdt[:, :], wdt[:, :],
                                                 srbd[:, :]))
                sep("v")
                emit("v", lambda e, ich=ich: e.tensor_copy(
                    mdT[:, ich, :], wdt[:, :]), (wdtf, dready))
                wait("g", dready)
                dma("g", wspd_sc[ich], mdT[:, ich, :], sspl)
            al.reset(Sd)
        else:
            wait("y", mm_done)
            wait("y", sspl, sspl.n)
            sdq2 = S("fsdq2")
            for ich in range(NICH):
                dma("y", mdT[:, ich, :], wspd_sc[ich], sdq2)
        drec = (dready, dready.n) if first else (sdq2, sdq2.n)
        if upto == "d":
            return None, None, None

        al.reset(Sm)
        hin = [al.buf([128, 512], F32, f"{tag}hin{i}") for i in range(3)]
        tqv = al.buf([128, 512], F32, f"{tag}tqv")
        tqg = al.buf([128, 512], F32, f"{tag}tqg")
        nh = [al.buf([128, NICH, 512], BF16, f"{tag}nhp{i}") for i in range(2)]
        lnt = al.buf([128, E], F32, f"{tag}lnt")
        xr = [al.buf([128, E], F32, f"{tag}xr{i}") for i in range(2)]
        x1t = [al.buf([128, E], F32, f"{tag}x1t{i}") for i in range(2)]
        h2l = [S(f"fh2l{i}") for i in range(3)]
        h2f = [S(f"fh2f{i}") for i in range(3)]
        nhr = [S(f"fnhr{i}") for i in range(NICH)]
        nhfree = [S(f"fnhfr{i}") for i in range(2)]
        xrl = [S(f"fxrl{i}") for i in range(2)]
        xrf = [S(f"fxrf{i}") for i in range(2)]
        x1fr = [S(f"fx1fr{i}") for i in range(2)]
        pdr = [S(f"fpdr{i}") for i in range(4)]
        pdf = [S(f"fpdf{i}") for i in range(4)]
        sstore = S("fsstore")
        nhrec = {}
        pdfree = {}
        dst_gate = Gate()
        mmC_done = S("fmmC_done")
        for i in range(3):
            wait("y", hfree[i], hfree[i].n)
        wait("y", mm_done)
        waitr("y", drec)
        for tc in range(NTC):
            par = tc % 2
            gwaited = [False]
            vwaited = [False]
            for ich in range(NICH):
                it = tc * NICH + ich
                sl = it % 3
                wait("y", h2f[sl])
                waitr("y", hstrec.get(it))
                dma("y", hin[sl][:, :], hid_sc[ich, tc], h2l[sl])
                qeng = "v"
                tqx = tqv
                wait(qeng, h2l[sl])
                if it == 0:
                    wait("v", svH)
                if not vwaited[0]:
                    wait(qeng, nhfree[par])
                    vwaited[0] = True
                emit(qeng, lambda e, sl=sl, tqx=tqx: e.tensor_scalar(
                    tqx[:, :], hin[sl][:, :], rsc[:, R_ISH:R_ISH + 1], MAGIC,
                    ALU.mult, ALU.add), h2f[sl])
                sep(qeng)
                emit(qeng, lambda e, tqx=tqx: e.tensor_scalar(
                    tqx[:, :], tqx[:, :], MAGIC, -(QMAX + 1.0), ALU.subtract, ALU.max))
                sep(qeng)
                emit(qeng, lambda e, tqx=tqx, ich=ich, par=par:
                     e.tensor_scalar_min(nh[par][:, ich, :], tqx[:, :], QMAX),
                     nhr[ich])
                nhrec[ich] = (nhr[ich], nhr[ich].n)
            for tti in range(4):
                tt = tc * 4 + tti
                bsel = tt % 2
                bl, br = 2 * bsel, 2 * bsel + 1
                waitr("t", pdfree.get(bl))
                waitr("t", pdfree.get(br))
                if tt == 0:
                    waitr("t", drec)
                for ich in range(NICH):
                    if tti == 0:
                        waitr("t", nhrec.get(ich))
                    for half in range(2):
                        bk = bl if half == 0 else br
                        incs = []
                        if ich == NICH - 1:
                            incs.append(pdr[bk])
                            if tti == 3 and half == 1:
                                incs.append(nhfree[par])
                                if tc == NTC - 1:
                                    incs.append(mmC_done)
                        emit("t", lambda e, bk=bk, ich=ich, tti=tti, half=half,
                             par=par: e.matmul(
                                 pb[bk][:, :],
                                 nh[par][:, ich, tti * 128:(tti + 1) * 128],
                                 mdT[:, ich, half * 512:(half + 1) * 512],
                                 start=(ich == 0), stop=(ich == NICH - 1)),
                             incs)
                xsl = tt % 2
                wait("y", xrf[xsl])
                if src_gate is not None:
                    wait_gate("y", src_gate)
                dma("y", xr[xsl][:, :], src_ap[tt * 128:(tt + 1) * 128, :],
                    xrl[xsl])
                wait("v", xrl[xsl])
                wait("v", pdr[bl])
                wait("v", pdr[br])
                wait("v", x1fr[xsl])
                emit("v", lambda e, bl=bl, xsl=xsl: e.scalar_tensor_tensor(
                    x1t[xsl][:, 0:512], pb[bl][:, :], rsc[:, R_SHH:R_SHH + 1],
                    xr[xsl][:, 0:512], ALU.mult, ALU.add), pdf[bl])
                pdfree[bl] = (pdf[bl], pdf[bl].n)
                emit("v", lambda e, br=br, xsl=xsl: e.scalar_tensor_tensor(
                    x1t[xsl][:, 512:1024], pb[br][:, :], rsc[:, R_SHH:R_SHH + 1],
                    xr[xsl][:, 512:1024], ALU.mult, ALU.add),
                    (pdf[br], xrf[xsl]))
                pdfree[br] = (pdf[br], pdf[br].n)
                if ln_next:
                    sep("v")
                    emit("v", lambda e, xsl=xsl: e.bn_stats(
                        stat2[:, 0:6], x1t[xsl][:, 0:512]))
                    emit("v", lambda e, xsl=xsl: e.bn_stats(
                        stat2[:, 6:12], x1t[xsl][:, 512:1024]))
                    sep("v")
                    emit("v", lambda e: e.bn_aggr(bnb[:, 0:2], stat2[:, 0:12]))
                    sep("v")
                    emit("v", lambda e, tt=tt: e.tensor_copy(
                        mu_a[:, tt:tt + 1], bnb[:, 0:1]))
                    emit("v", lambda e, tt=tt: e.tensor_copy(
                        vv_a[:, tt:tt + 1], bnb[:, 1:2]))
                    sep("v")
                    emit("v", lambda e, xsl=xsl, tt=tt: e.tensor_scalar(
                        lnt[:, :], x1t[xsl][:, :], mu_a[:, tt:tt + 1], None,
                        ALU.subtract, ALU.bypass))
                    sep("v")
                    emit("v", lambda e, tt=tt: e.reduce_max(
                        mraw[:, tt:tt + 1], lnt[:, :], axis=AX,
                        apply_absolute_value=True), sstore)
                    wait("y", sstore)
                else:
                    waitr("y", chain("v"))
                dma("y", dst_ap[tt * 128:(tt + 1) * 128, :], x1t[xsl][:, :],
                    x1fr[xsl])
                if dst2_ap is not None:
                    dma("y", dst2_ap[tt * 128:(tt + 1) * 128, :],
                        x1t[xsl][:, :], x1fr[xsl])
                dst_gate.per[tt] = (x1fr[xsl], x1fr[xsl].n)
        dst_gate.full = [(x1fr[0], x1fr[0].n), (x1fr[1], x1fr[1].n)]
        al.reset(Sm)
        return dst_gate, mmC_done, sstore

    def attn_phase(x1done, mmC_done, stats_gate):
        Sm = al.mark()
        ppfree = {}
        ssfree = {}
        gpfree = {}
        sv2, ss2, st2 = S("asv2"), S("ass2"), S("ast2")
        emit("s", load_lnexp, ctl=True)
        wait("s", stats_gate)
        wait("v", stats_gate)
        rsqrt_batch(rr_a[:, :], vv_a[:, :], NTT, sv2, ss2)

        Smb = al.mark()
        xb = [al.buf([128, E], F32, f"axb{i}") for i in range(2)]
        nb = [al.buf([128, E], BF16, f"anb{i}") for i in range(2)]
        nbfree = {}
        xbl = [S(f"axbl{i}") for i in range(2)]
        xbf_ = [S(f"axbf{i}") for i in range(2)]
        nbr = [S(f"anbr{i}") for i in range(2)]
        xlrec = [None]
        wait("y", mmC_done)
        wait_gate("y", x1done)
        wait("v", S("fsspl"), S("fsspl").n)
        for tt in range(NTT):
            sl = tt % 2
            wait("y", xbf_[sl])
            dma("y", xb[sl][:, :], x1_sc[tt * 128:(tt + 1) * 128, :], xbl[sl])
            wait("v", xbl[sl])
            if tt == 0:
                wait("v", sv2)
            waitr("v", nbfree.get(sl))
            emit("v", lambda e, sl=sl, tt=tt: e.tensor_scalar(
                nb[sl][:, :], xb[sl][:, :], mu_a[:, tt:tt + 1],
                rr_a[:, tt:tt + 1], ALU.subtract, ALU.mult),
                (nbr[sl], xbf_[sl]))
            waitr("t", (S("avm"), S("avm").n))
            for d in range(8):
                xlrec[0] = pe_tr(xln2fm[:, d, tt * 128:(tt + 1) * 128],
                                 nb[sl][:, d * 128:(d + 1) * 128],
                                 4 + (d % 2), identb[:, :], 128, 128)
            nbfree[sl] = (S("atm"), S("atm").n)
        scs = S("ascs")
        dma("y", cosfm[:, :], cos_i, scs)
        dma("y", sinmfm[:, :], sin_i, scs)
        al.reset(Smb)

        waitr("v", xlrec[0])
        for ech in range(8):
            emit("v", lambda e, ech=ech: e.reduce_sum(
                mean8[:, ech:ech + 1], xln2fm[:, ech, :], axis=AX))
        sep("v")
        emit("v", lambda e: e.tensor_scalar_mul(mean8[:, :], mean8[:, :],
                                                1.0 / T), sv2)
        wait("t", sv2)
        wait("t", p6f)
        for ech in range(8):
            emit("t", lambda e, ech=ech: e.matmul(
                pb[6][0:1, 0:16], mean8[:, ech:ech + 1],
                wdTb[:, ech * 16:(ech + 1) * 16],
                start=(ech == 0), stop=(ech == 7)),
                st2 if ech == 7 else ())
        wait("v", st2)
        emit("v", lambda e: e.tensor_add(srow[0:1, 0:16], pb[6][0:1, 0:16],
                                         bdrow[0:1, :]), (sv2, p6f))
        wait("s", sv2)
        emit("s", lambda e: e.activation(srow2[0:1, 0:16], srow[0:1, 0:16],
                                         AF.Exp, scale=-1.0), ss2)
        wait("v", ss2)
        emit("v", lambda e: e.tensor_scalar_add(srow2[0:1, 0:16],
                                                srow2[0:1, 0:16], 1.0))
        sep("v")
        emit("v", lambda e: e.reciprocal(srow[0:1, 16:32], srow2[0:1, 0:16]),
             sv2)
        wait("s", sv2)
        emit("s", lambda e: e.activation(srow2[0:1, 0:16], srow[0:1, 16:32],
                                         AF.Exp, scale=-8.0 * LN2,
                                         bias=constv[0:1, COL_M5LN2:COL_M5LN2 + 1]),
             ss2)
        wait("v", ss2)
        emit("v", lambda e: e.tensor_scalar(srow2[0:1, 0:16], srow2[0:1, 0:16],
                                            -1.0, 1.0, ALU.mult, ALU.add), sv2)
        wait("s", sv2)
        emit("s", lambda e: e.activation(ldrow[0:1, :], srow2[0:1, 0:16],
                                         AF.Ln), ss2)
        wait("t", ss2)
        wait("t", p6f)
        emit("t", lambda e: e.matmul(pb[6][0:128, 0:16], onesr[0:1, :],
                                     ldrow[0:1, :], start=True, stop=True), st2)
        wait("v", st2)
        emit("v", lambda e: e.tensor_copy(ldb[:, :], pb[6][0:128, 0:16]),
             (sv2, p6f))

        wga = al.buf([128, 8, 128], BF16, "awga")
        gtile = [al.buf([128, 512], BF16, f"agt{i}") for i in range(2)]
        swl = S("aswl")
        gpr = [S(f"agpr{i}") for i in range(2)]
        gpf = [S(f"agpf{i}") for i in range(2)]
        gsr = [S(f"agsr{i}") for i in range(2)]
        gsf = [S(f"agsf{i}") for i in range(2)]
        emit("s", lambda e: e.preload_act_table(AF.Silu), ctl=True)

        def wslice(w_i, pi):
            return w_i.rearrange("(c p) f -> p c f", p=128)[:, :,
                                                            pi * 128:(pi + 1) * 128]

        for pi in range(NP):
            wait("g", swl, swl.n)
            waitr("g", xlrec[0])
            dma("g", wga[:, :, :], wslice(wgaT_i, pi), swl)
            wait("t", swl)
            waitr("t", xlrec[0])
            for tc in range(NTC):
                gb = tc % 2
                waitr("t", gpfree.get(gb))
                for d in range(8):
                    emit("t", lambda e, gb=gb, d=d, tc=tc: e.matmul(
                        pb[gb][:, :], wga[:, d, :],
                        xln2fm[:, d, tc * 512:(tc + 1) * 512],
                        start=(d == 0), stop=(d == 7)),
                        gpr[gb] if d == 7 else ())
                wait("s", gpr[gb])
                wait("s", gsf[gb])
                emit("s", lambda e, gb=gb, pi=pi: e.activation(
                    gtile[gb][:, :], pb[gb][:, :], AF.Silu,
                    bias=bfm[:, CB_G + pi:CB_G + pi + 1]), (gsr[gb], gpf[gb]))
                gpfree[gb] = (gpf[gb], gpf[gb].n)
                wait("g", gsr[gb])
                dma("g", gate_sc[pi, tc], gtile[gb][:, :], gsf[gb])

        emit("s", load_lnexp, ctl=True)
        Qp = al.buf([128, T], BF16, "aQp")
        Kp = al.buf([128, T], BF16, "aKp")
        Vaug = [al.buf([128, 32 * 80], BF16, f"aVa{i}") for i in range(2)]
        og = al.buf([128, T], F32, "aog")
        attnT = [al.buf([128, 512], BF16, f"aaT{i}") for i in range(2)]
        Dmix = al.buf([128, 2048], BF16, "aDmix")
        gtl = al.buf([128, 512], F32, "agtl")
        fbt = al.buf([128, 28], F32, "afbt")
        qraw = al.buf([128, 512], F32, "aqraw")
        qrot = al.buf([128, 512], F32, "aqrot")
        tr1 = al.buf([128, 512], F32, "atr1")
        vbf = [al.buf([128, 512], BF16, f"avbf{i}") for i in range(2)]
        vbfree = {}
        vrec = [None]
        nmbuf = al.buf([128, 512], F32, "anm")
        tfut = al.buf([128, 512], F32, "atfut")
        ont = al.buf([128, 512], F32, "aont")
        gont = [al.buf([128, 512], BF16, f"agont{i}") for i in range(2)]
        gin = [al.buf([128, 512], BF16, f"agin{i}") for i in range(2)]
        wqb = al.buf([128, 8, 128], BF16, "awqb")
        wkb = al.buf([128, 8, 128], BF16, "awkb")
        wvb = al.buf([128, 8, 128], BF16, "awvb")
        sv_ones = S("asvones")
        for hh in range(2):
            emit("v", lambda e, hh=hh: e.memset(
                Vaug[hh].rearrange("p (k c) -> p k c", c=80)[:, :, 64:65], 1.0),
                sv_ones)

        swqkv = S("aswqkv")
        ppr = [S(f"appr{i}") for i in range(2)]
        ppf = [S(f"appf{i}") for i in range(2)]
        sQf = S("asQf")
        vbr = [S(f"avbr{i}") for i in range(2)]
        ssr = [S(f"assr{i}") for i in range(2)]
        ssf = [S(f"assf{i}") for i in range(2)]
        atr = [S(f"aatr{i}") for i in range(2)]
        atf = [S(f"aatf{i}") for i in range(2)]
        sxm = S("asxm")
        avp = S("aavp")
        avf = S("aavf")
        p4free = S("ap4free")
        p5free = S("ap5free")
        smrg = S("asmrg")
        shead = S("ashead")
        svhead = S("asvhead")
        sgn = S("asgn")
        oggate = S("aoggate")
        ginl = [S(f"aginl{i}") for i in range(2)]
        ginf = [S(f"aginf{i}") for i in range(2)]
        gonr = [S(f"agonr{i}") for i in range(2)]
        gonf = [S(f"agonf{i}") for i in range(2)]

        wait("v", scs)
        for pi in range(NP):
            wait("g", swqkv, swqkv.n)
            dma("g", wqb[:, :, :], wslice(wqT_i, pi), swqkv)
            dma("g", wkb[:, :, :], wslice(wkT_i, pi), swqkv)
            dma("g", wvb[:, :, :], wslice(wvT_i, pi), swqkv)
            wait("t", swqkv)
            for proj in range(3):
                wb_ = (wqb, wkb, wvb)[proj]
                bcol = (CB_Q, CB_K, CB_V)[proj]
                dstfm = (Qp, Kp, None)[proj]
                for tc in range(NTC):
                    pbk = tc % 2
                    waitr("t", ppfree.get(pbk))
                    for d in range(8):
                        emit("t", lambda e, pbk=pbk, d=d, tc=tc, wb_=wb_:
                             e.matmul(pb[pbk][:, :], wb_[:, d, :],
                                      xln2fm[:, d, tc * 512:(tc + 1) * 512],
                                      start=(d == 0), stop=(d == 7)),
                             ppr[pbk] if d == 7 else ())
                    wait("v", ppr[pbk])
                    if proj < 2:
                        emit("v", lambda e, pbk=pbk, pi=pi, bcol=bcol:
                             e.tensor_scalar_add(qraw[:, :], pb[pbk][:, :],
                                                 bfm[:, bcol + pi:bcol + pi + 1]),
                             ppf[pbk])
                        ppfree[pbk] = (ppf[pbk], ppf[pbk].n)
                        mask = [i + 1 if i % 2 == 0 else i - 1 for i in range(32)]
                        sep("v")
                        emit("v", lambda e: e.stream_shuffle(qrot[:, :], qraw[:, :],
                                                             mask=mask))
                        emit("v", lambda e, tc=tc: e.tensor_mul(
                            tr1[:, :], qraw[:, :], cosfm[:, tc * 512:(tc + 1) * 512]))
                        sep("v")
                        emit("v", lambda e, tc=tc: e.tensor_mul(
                            qrot[:, :], qrot[:, :], sinmfm[:, tc * 512:(tc + 1) * 512]))
                        sep("v")
                        emit("v", lambda e, tc=tc, dstfm=dstfm: e.tensor_add(
                            dstfm[:, tc * 512:(tc + 1) * 512], tr1[:, :], qrot[:, :]), sQf)
                    else:
                        vsl = tc % 2
                        waitr("v", vbfree.get(vsl))
                        emit("v", lambda e, pbk=pbk, pi=pi, vsl=vsl:
                             e.tensor_scalar_add(vbf[vsl][:, :], pb[pbk][:, :],
                                                 bfm[:, CB_V + pi:CB_V + pi + 1]),
                             (ppf[pbk], vbr[vsl]))
                        ppfree[pbk] = (ppf[pbk], ppf[pbk].n)
                        waitr("t", (S("avm"), S("avm").n))
                        for blk in range(4):
                            kt = tc * 4 + blk
                            for hh in range(2):
                                idsl = (identb[0:64, 0:64] if hh == 0
                                        else identb[64:128, 64:128])
                                vrec[0] = pe_tr(
                                    Vaug[hh][:, kt * 80:kt * 80 + 64],
                                    vbf[vsl][hh * 64:(hh + 1) * 64,
                                             blk * 128:(blk + 1) * 128],
                                    2 + (blk % 2), idsl, 64, 128)
                        vbfree[vsl] = (S("atm"), S("atm").n)
            for hh in range(2):
                base = 64 * hh
                hcol = pi * 2 + hh
                wait("v", sv2)
                emit("v", lambda e, hcol=hcol: e.tensor_scalar(
                    fbt[:, :], fbias[:, :], ldb[:, hcol:hcol + 1], None,
                    ALU.mult, ALU.bypass), svhead)
                wait("s", svhead)
                wait("s", atf[0], atf[0].n)
                wait("s", atf[1], atf[1].n)
                wait("s", smrg, smrg.n)
                emit("s", lambda e, hcol=hcol: e.activation(
                    Dmix[:, :], relum[:, :], AF.Exp,
                    scale=ldb[:, hcol:hcol + 1]), shead)
                emit("s", lambda e, hcol=hcol: e.activation(
                    gtl[:, :], qexp[:, :], AF.Exp,
                    scale=ldb[:, hcol:hcol + 1]), shead)
                wait("t", sQf, sQf.n)
                waitr("t", vrec[0])
                for qc in range(NTC):
                    lastpast = qc * 4 + 3

                    def emit_av(kt):
                        par = kt % 2
                        is_past = kt <= lastpast
                        dstb = 4 if is_past else 5
                        first_mm = (kt == 0) or (kt == lastpast + 1)
                        stop_mm = (kt == lastpast) or (kt == 31)
                        if first_mm:
                            wait("t", p4free if is_past else p5free)
                        rs, rv = atrec[kt]
                        wait("t", rs, rv)
                        incs = [atf[par]]
                        if stop_mm:
                            incs.append(avp if is_past else avf)
                        emit("t", lambda e, dstb=dstb, hh=hh, kt=kt, par=par,
                             first_mm=first_mm, stop_mm=stop_mm: e.matmul(
                                 pb[dstb][0:65, :],
                                 Vaug[hh][:, kt * 80:kt * 80 + 65],
                                 attnT[par][:, :], start=first_mm,
                                 stop=stop_mm), incs)

                    atrec = {}
                    for kt in range(32):
                        par = kt % 2
                        bank = 2 + par
                        dm = kt * 128 - qc * 512
                        waitr("t", ssfree.get(par))
                        emit("t", lambda e, bank=bank, base=base, kt=kt, qc=qc:
                             e.matmul(pb[bank][:, :],
                                      Kp[base:base + 64, kt * 128:(kt + 1) * 128],
                                      Qp[base:base + 64, qc * 512:(qc + 1) * 512],
                                      start=True, stop=True), ssr[par])
                        wait("s", ssr[par])
                        wait("s", atf[par])
                        if dm <= -128:
                            emit("s", lambda e, bank=bank, par=par: e.activation(
                                attnT[par][:, :], pb[bank][:, :], AF.Exp,
                                scale=0.125), (atr[par], ssf[par]))
                            ssfree[par] = (ssf[par], ssf[par].n)
                            atrec[kt] = (atr[par], atr[par].n)
                        elif dm <= 384:
                            emit("s", lambda e, bank=bank, par=par: e.activation(
                                attnT[par][:, :], pb[bank][:, :], AF.Exp,
                                scale=0.125), (sxm, ssf[par]))
                            ssfree[par] = (ssf[par], ssf[par].n)
                            wait("v", sxm)
                            wait("v", shead)
                            m = dm // 128
                            emit("v", lambda e, par=par, m=m: e.tensor_mul(
                                attnT[par][:, :], attnT[par][:, :],
                                Dmix[:, m * 512:(m + 1) * 512]), S("avm"))
                            atrec[kt] = (S("avm"), S("avm").n)
                        else:
                            m = (dm - 512) // 128
                            wait("s", shead)
                            emit("s", lambda e, bank=bank, par=par, m=m:
                                 e.activation(attnT[par][:, :], pb[bank][:, :],
                                              AF.Exp, scale=0.125,
                                              bias=fbt[:, m:m + 1]),
                                 (atr[par], ssf[par]))
                            ssfree[par] = (ssf[par], ssf[par].n)
                            atrec[kt] = (atr[par], atr[par].n)
                        if kt >= 1:
                            emit_av(kt - 1)
                    emit_av(31)
                    has_fut = lastpast < 31
                    wait("v", avp)
                    if has_fut:
                        wait("v", avf)
                        wait("v", shead)
                        emit("v", lambda e: e.tensor_mul(
                            tfut[0:65, :], pb[5][0:65, :], gtl[0:65, :]),
                            p5free)
                        sep("v")
                        emit("v", lambda e: e.tensor_add(
                            nmbuf[0:65, :], pb[4][0:65, :], tfut[0:65, :]),
                            (smrg, p4free))
                    else:
                        emit("v", lambda e: e.tensor_scalar_add(
                            nmbuf[0:65, :], pb[4][0:65, :], 0.0),
                            (smrg, p4free))
                    sep("v")
                    emit("v", lambda e: e.reciprocal(srow[0:1, :],
                                                     nmbuf[64:65, :]), smrg)
                    wait("t", smrg)
                    wait("t", p7f)
                    emit("t", lambda e: e.matmul(pb[7][0:64, :],
                                                 onesr[0:1, 0:64], srow[0:1, :],
                                                 start=True, stop=True), st2)
                    wait("v", st2)
                    emit("v", lambda e, base=base, qc=qc: e.tensor_mul(
                        og[base:base + 64, qc * 512:(qc + 1) * 512],
                        nmbuf[0:64, :], pb[7][0:64, :]), (oggate, p7f))
            wait("v", oggate, oggate.n)
            for c in range(8):
                emit("v", lambda e, c=c: e.bn_stats(
                    bnb[:, 6 * c:6 * c + 6], og[:, c * 512:(c + 1) * 512]))
            sep("v")
            emit("v", lambda e: e.bn_aggr(stat2[:, 0:2], bnb[:, 0:48]), sgn)
            wait("t", sgn)
            wait("t", p7f)
            emit("t", lambda e: e.matmul(pb[7][0:1, 0:128], stat2[:, 0:1],
                                         identf[:, :], start=True, stop=True))
            emit("t", lambda e: e.matmul(pb[7][0:1, 128:256], stat2[:, 1:2],
                                         identf[:, :], start=True, stop=True),
                 st2)
            wait("v", st2)
            emit("v", lambda e: e.reduce_sum(cells[0:1, C_MUA:C_MUA + 1],
                                             pb[7][0:1, 0:64], axis=AX))
            emit("v", lambda e: e.reduce_sum(cells[0:1, C_MUB:C_MUB + 1],
                                             pb[7][0:1, 64:128], axis=AX))
            emit("v", lambda e: e.reduce_sum(cells[0:1, C_VA:C_VA + 1],
                                             pb[7][0:1, 128:192], axis=AX))
            emit("v", lambda e: e.reduce_sum(cells[0:1, C_VB:C_VB + 1],
                                             pb[7][0:1, 192:256], axis=AX))
            emit("v", lambda e: e.tensor_copy(srow[0:1, 0:128],
                                              pb[7][0:1, 0:128]), p7f)
            sep("v")
            emit("v", lambda e: e.tensor_mul(srow[0:1, 0:128],
                                             srow[0:1, 0:128],
                                             srow[0:1, 0:128]))
            sep("v")
            emit("v", lambda e: e.reduce_sum(cells[0:1, C_T0:C_T0 + 1],
                                             srow[0:1, 0:64], axis=AX))
            emit("v", lambda e: e.reduce_sum(cells[0:1, C_T1:C_T1 + 1],
                                             srow[0:1, 64:128], axis=AX))
            emit("v", lambda e: e.tensor_scalar_mul(cells[0:1, C_MUA:C_MUB + 1],
                                                    cells[0:1, C_MUA:C_MUB + 1],
                                                    1.0 / 64))
            sep("v")
            emit("v", lambda e: e.tensor_add(cells[0:1, C_VA:C_VB + 1],
                                             cells[0:1, C_VA:C_VB + 1],
                                             cells[0:1, C_T0:C_T1 + 1]))
            sep("v")
            emit("v", lambda e: e.tensor_scalar_mul(cells[0:1, C_VA:C_VB + 1],
                                                    cells[0:1, C_VA:C_VB + 1],
                                                    1.0 / 64))
            emit("v", lambda e: e.tensor_mul(cells[0:1, C_T0:C_T1 + 1],
                                             cells[0:1, C_MUA:C_MUB + 1],
                                             cells[0:1, C_MUA:C_MUB + 1]))
            sep("v")
            emit("v", lambda e: e.tensor_sub(cells[0:1, C_VA:C_VB + 1],
                                             cells[0:1, C_VA:C_VB + 1],
                                             cells[0:1, C_T0:C_T1 + 1]), sgn)
            wait("s", sgn)
            rsqrt_row(cells[0:1, C_RA:C_RB + 1], cells[0:1, C_VA:C_VB + 1],
                      srow[0:1, 0:2], srow2[0:1, 0:2], sgn, ss2)
            wait("t", sgn)
            wait("t", p6f)
            emit("t", lambda e: e.matmul(pb[6][0:128, 0:1], maskA[0:1, :],
                                         cells[0:1, C_RA:C_RA + 1],
                                         start=True, stop=False))
            emit("t", lambda e: e.matmul(pb[6][0:128, 0:1], maskB[0:1, :],
                                         cells[0:1, C_RB:C_RB + 1],
                                         start=False, stop=True))
            emit("t", lambda e: e.matmul(pb[6][0:128, 1:2], maskA[0:1, :],
                                         cells[0:1, C_MUA:C_MUA + 1],
                                         start=True, stop=False))
            emit("t", lambda e: e.matmul(pb[6][0:128, 1:2], maskB[0:1, :],
                                         cells[0:1, C_MUB:C_MUB + 1],
                                         start=False, stop=True), st2)
            wait("v", st2)
            emit("v", lambda e: e.tensor_copy(rvec[:, :], pb[6][0:128, 0:1]))
            emit("v", lambda e: e.tensor_copy(mvec[:, :], pb[6][0:128, 1:2]),
                 p6f)
            sep("v")
            emit("v", lambda e, pi=pi: e.tensor_mul(
                Avec[:, :], rvec[:, :], bfm[:, CB_GAM + pi:CB_GAM + pi + 1]))
            sep("v")
            emit("v", lambda e: e.tensor_mul(tvec[:, :], mvec[:, :],
                                             Avec[:, :]))
            sep("v")
            emit("v", lambda e, pi=pi: e.tensor_sub(
                Bvec[:, :], bfm[:, CB_BET + pi:CB_BET + pi + 1], tvec[:, :]),
                sgn)
            for tc in range(NTC):
                gsl = tc % 2
                wait("g", ginf[gsl])
                wait("g", gsf[0], gsf[0].n)
                wait("g", gsf[1], gsf[1].n)
                dma("g", gin[gsl][:, :], gate_sc[pi, tc], ginl[gsl])
                wait("v", ginl[gsl])
                wait("v", sgn, sgn.n)
                wait("v", gonf[gsl])
                emit("v", lambda e, tc=tc: e.tensor_scalar(
                    ont[:, :], og[:, tc * 512:(tc + 1) * 512], Avec[:, :],
                    Bvec[:, :], ALU.mult, ALU.add))
                sep("v")
                emit("v", lambda e, gsl=gsl: e.tensor_mul(
                    gont[gsl][:, :], ont[:, :], gin[gsl][:, :]),
                    (gonr[gsl], ginf[gsl]))
                wait("g", gonr[gsl])
                dma("g", gon_sc[pi, tc], gont[gsl][:, :], gonf[gsl])

        al.reset(Sm)
        gin8 = [al.buf([128, 8, 512], BF16, f"pgin8{i}") for i in range(2)]
        xr1 = [al.buf([128, E], F32, f"pxr1{i}") for i in range(2)]
        x2t = [al.buf([128, E], F32, f"px2t{i}") for i in range(2)]
        lnt5 = al.buf([128, E], F32, "plnt5")
        t5 = al.buf([128, 512], F32, "pt5")
        g8l = [S(f"pg8l{i}") for i in range(2)]
        g8f = [S(f"pg8f{i}") for i in range(2)]
        x1l = [S(f"px1l{i}") for i in range(2)]
        x1f = [S(f"px1f{i}") for i in range(2)]
        x2fr = [S(f"px2fr{i}") for i in range(2)]
        por = [S(f"ppor{i}") for i in range(4)]
        pof = [S(f"ppof{i}") for i in range(4)]
        swo = S("pswo")
        pofree = {}
        p5stat = S("p5stat")
        x2gate = Gate()
        wait("g", sQf, sQf.n)
        waitr("g", vrec[0])
        for pi in range(NP):
            dma("g", woTb[:, pi, :],
                woT_i[pi * 128:(pi + 1) * 128, :], swo)
        for tc in range(NTC):
            gsl = tc % 2
            wait("g", g8f[gsl])
            wait("g", gonf[0], gonf[0].n)
            wait("g", gonf[1], gonf[1].n)
            for pi in range(NP):
                dma("g", gin8[gsl][:, pi, :], gon_sc[pi, tc], g8l[gsl])
            wait("t", g8l[gsl])
            if tc == 0:
                wait("t", swo)
            for tti in range(4):
                tt = tc * 4 + tti
                bsel = tt % 2
                bl, br = 2 * bsel, 2 * bsel + 1
                waitr("t", pofree.get(bl))
                waitr("t", pofree.get(br))
                for pi in range(NP):
                    for half in range(2):
                        bk = bl if half == 0 else br
                        incs = []
                        if pi == NP - 1:
                            incs.append(por[bk])
                            if tti == 3 and half == 1:
                                incs.append(g8f[gsl])
                        emit("t", lambda e, bk=bk, pi=pi, tti=tti, half=half,
                             gsl=gsl: e.matmul(
                                 pb[bk][:, :],
                                 gin8[gsl][:, pi, tti * 128:(tti + 1) * 128],
                                 woTb[:, pi, half * 512:(half + 1) * 512],
                                 start=(pi == 0), stop=(pi == NP - 1)),
                             incs)
                xsl = tt % 2
                wait("y", x1f[xsl])
                dma("y", xr1[xsl][:, :], x1_sc[tt * 128:(tt + 1) * 128, :],
                    x1l[xsl])
                wait("v", x1l[xsl])
                wait("v", por[bl])
                wait("v", por[br])
                wait("v", x2fr[xsl])
                for half, bk in ((0, bl), (1, br)):
                    sep("v")
                    emit("v", lambda e, bk=bk, half=half: e.tensor_add(
                        t5[:, :], pb[bk][:, :],
                        borep[:, half * 512:(half + 1) * 512]), pof[bk])
                    pofree[bk] = (pof[bk], pof[bk].n)
                    sep("v")
                    emit("v", lambda e, xsl=xsl, half=half: e.tensor_add(
                        x2t[xsl][:, half * 512:(half + 1) * 512], t5[:, :],
                        xr1[xsl][:, half * 512:(half + 1) * 512]),
                        (x1f[xsl] if half == 1 else ()))
                sep("v")
                emit("v", lambda e, xsl=xsl: e.bn_stats(stat2[:, 0:6],
                                                        x2t[xsl][:, 0:512]))
                emit("v", lambda e, xsl=xsl: e.bn_stats(stat2[:, 6:12],
                                                        x2t[xsl][:, 512:1024]))
                sep("v")
                emit("v", lambda e: e.bn_aggr(bnb[:, 0:2], stat2[:, 0:12]))
                sep("v")
                emit("v", lambda e, tt=tt: e.tensor_copy(mu_a[:, tt:tt + 1],
                                                         bnb[:, 0:1]))
                emit("v", lambda e, tt=tt: e.tensor_copy(vv_a[:, tt:tt + 1],
                                                         bnb[:, 1:2]))
                sep("v")
                emit("v", lambda e, xsl=xsl, tt=tt: e.tensor_scalar(
                    lnt5[:, :], x2t[xsl][:, :], mu_a[:, tt:tt + 1], None,
                    ALU.subtract, ALU.bypass))
                sep("v")
                emit("v", lambda e, tt=tt: e.reduce_max(
                    mraw[:, tt:tt + 1], lnt5[:, :], axis=AX,
                    apply_absolute_value=True), p5stat)
                wait("y", p5stat)
                dma("y", x2_sc[tt * 128:(tt + 1) * 128, :], x2t[xsl][:, :],
                    x2fr[xsl])
                if stage == 2:
                    dma("y", out_t[tt * 128:(tt + 1) * 128, :], x2t[xsl][:, :],
                        x2fr[xsl])
                x2gate.per[tt] = (x2fr[xsl], x2fr[xsl].n)
        x2gate.full = [(x2fr[0], x2fr[0].n), (x2fr[1], x2fr[1].n)]
        al.reset(Sm)
        return x2gate, p5stat

    upto = "z"
    if isinstance(stage, str) and len(stage) == 2:
        upto = stage[1]
        stage = 1
    x1done, mmC1, sstore1 = ffn(
        "f1", x_in, x1_sc, 0, 1, first=True, ln_next=True, have_stats=False,
        dst2_ap=(out_t if stage == 1 else None), upto=upto)
    if stage == 1:
        if x1done is None:
            for sm in list(sems.values()):
                wait("y", sm)
        else:
            wait_gate("y", x1done)
        if _DUMP:
            dbg_t = nc.dram_tensor("dbg", [128, 128], F32,
                                   kind="ExternalOutput").ap()
            for sm in list(sems.values()):
                wait("y", sm)
            sdbg = S("dbgs")
            dma("y", dbg_t[0:1, 0:9], cells[0:1, 0:9], sdbg)
            dma("y", dbg_t[:, 32:36], rsc[:, 0:4], sdbg)
            dma("y", dbg_t[:, 40:62], svg[:, :], sdbg)
            dma("y", dbg_t[:, 62:84], svu[:, :], sdbg)
            dma("y", dbg_t[:, 84:106], sgate[:, :], sdbg)
            dma("y", dbg_t[:, 106:128], sup[:, :], sdbg)
            wait("y", sdbg)
        P.flush()
        return nc

    x2done, p5stat = attn_phase(x1done, mmC1, sstore1)
    if stage == 2:
        wait_gate("y", x2done)
        P.flush()
        return nc

    outdone, _, _ = ffn(
        "f2", x2_sc, out_t, 2, 3, first=False, ln_next=False, have_stats=True,
        src_gate=x2done, stats_gate=p5stat)
    wait_gate("y", outdone)
    P.flush()
    return nc


def _host_prep(inputs):
    f32 = np.float32

    def C(a):
        return np.ascontiguousarray(np.asarray(a), dtype=f32)

    x = C(inputs["L0_attn_x"])
    wq = C(inputs["L0_attn_wq"])
    wk = C(inputs["L0_attn_wk"])
    wv = C(inputs["L0_attn_wv"])
    wg = C(inputs["L0_attn_wg"])
    wo = C(inputs["L0_attn_wo"])
    wd = C(inputs["L0_attn_wd"])
    gamma = C(inputs["L0_attn_gamma"])
    beta = C(inputs["L0_attn_beta"])
    bq = C(inputs["L0_attn_bq"])
    bk = C(inputs["L0_attn_bk"])
    bv = C(inputs["L0_attn_bv"])
    bg = C(inputs["L0_attn_bg"])
    bo = C(inputs["L0_attn_bo"])
    bd = C(inputs["L0_attn_bd"])
    w_gate = C(inputs["L0_ffn_w_gate"])
    w_up = C(inputs["L0_ffn_w_up"])
    w_down = C(inputs["L0_ffn_w_down"])
    clip_x = float(np.asarray(inputs["L0_ffn_clip_x"]).ravel()[0])
    clip_down = float(np.asarray(inputs["L0_ffn_clip_down"]).ravel()[0])

    def fmcols(vec):
        return vec.reshape(8, 128).T

    bfm = np.concatenate([fmcols(v) for v in (bq, bk, bv, bg, gamma, beta)],
                         axis=1)

    hd = np.arange(128) % 64
    inv = 1.0 / 10000.0 ** (np.arange(0, 32, dtype=np.float64) * 2 / 64)
    freq = inv[hd % 32]
    pos = np.arange(T, dtype=np.float64)
    ang = pos[None, :] * freq[:, None]
    cosfm = np.cos(ang)
    sign = np.where(hd % 2 == 0, -1.0, 1.0)
    sinmfm = np.sin(ang) * sign[:, None]

    i_ = np.arange(128)[:, None]
    j_ = np.arange(512)[None, :]
    relum = np.concatenate(
        [np.maximum(128 * m + i_ - j_, 0).astype(f32) for m in range(4)],
        axis=1)
    qexpc = np.broadcast_to((512.0 - j_).astype(f32), (128, 512)).copy()
    fbias = (np.arange(128)[:, None] + 128.0 * np.arange(28)[None, :])
    onesr = np.ones((1, 128), f32)
    maskA = np.concatenate([np.ones((1, 64)), np.zeros((1, 64))], 1)
    maskB = np.concatenate([np.zeros((1, 64)), np.ones((1, 64))], 1)
    identf = np.eye(128, dtype=f32)
    constv = np.zeros((128, 4), f32)
    constv[:, 0] = EPS
    constv[:, 1] = -5.0 * LN2

    shared = {
        "wg_n": w_gate, "wu_n": w_up,
        "wgT": C(w_gate.T), "wuT": C(w_up.T),
        "wd_n": w_down, "wdT": C(w_down.T),
        "wqT": C(wq.T), "wkT": C(wk.T), "wvT": C(wv.T), "wgaT": C(wg.T),
        "woT": C(wo.T), "wdec": C(wd.T),
        "bfm": C(bfm), "bo_r": bo.reshape(1, E).copy(),
        "bd_r": bd.reshape(1, 16).copy(),
        "clip2": np.array([[clip_x, clip_down]], f32),
        "cosfm": C(cosfm), "sinmfm": C(sinmfm), "relum": relum,
        "qexp": qexpc, "fbias": C(fbias), "onesr": onesr,
        "maskA": C(maskA), "maskB": C(maskB), "identf": identf,
        "identb": identf.astype(ml_dtypes.bfloat16),
        "constv": constv,
    }
    in_maps = []
    for b in range(8):
        m = dict(shared)
        m["x"] = np.ascontiguousarray(x[b])
        in_maps.append(m)
    return in_maps


def kernel(**inputs):
    in_maps = _host_prep(inputs)
    nc = build_program(STAGE)
    res = run_bass_kernel_spmd(nc, in_maps, core_ids=list(range(8)))
    out = np.stack([np.asarray(res.results[i]["out"], dtype=np.float32)
                    for i in range(8)], axis=0)
    return out


# version 03 (final)
# speedup vs baseline: 1.3002x; 1.0158x over previous
import os as _os

import ml_dtypes
import numpy as np

import concourse.bass as bass
import concourse.mybir as mybir
from concourse.bass_utils import run_bass_kernel_spmd

F32 = mybir.dt.float32
BF16 = mybir.dt.bfloat16
AF = mybir.ActivationFunctionType
ALU = mybir.AluOpType
AX = mybir.AxisListType.X

T = 4096
E = 1024
I = 2816
NP = 8
NTT = 32
NTC = 8
NICH = 22
MAGIC = 1.5 * 2 ** 23
QMAX = 127.0
EPS = 1e-5
LN2 = float(np.log(2.0))

STAGE = 3
_NOMM = _os.environ.get("K_NOMM", "0") == "1"
_NOSILU = _os.environ.get("K_NOSILU", "0") == "1"
_NOTR = _os.environ.get("K_NOTR", "0") == "1"
_MMONLY = _os.environ.get("K_MMONLY", "0") == "1"
_DUMP = _os.environ.get("K_DUMP", "0") == "1"
_RACE = _os.environ.get("K_RACE", "0") == "1"


class Sem:
    def __init__(self, nc, name):
        self.h = nc.alloc_semaphore(name)
        self.n = 0


class Prog:
    def __init__(self, nc):
        self.nc = nc
        self.items = {k: [] for k in "tvsgy"}
        self.last = {}

    def flush(self):
        nc, items = self.nc, self.items
        self.items = {k: [] for k in "tvsgy"}
        with nc.Block() as blk:
            for key, reg in (("y", blk.sync), ("g", blk.gpsimd),
                             ("t", blk.tensor), ("v", blk.vector),
                             ("s", blk.scalar)):
                if items[key]:
                    def make(fs):
                        def run(eng):
                            for f in fs:
                                f(eng)
                        return run
                    reg(make(items[key]))


class Alloc:
    def __init__(self, nc, limit=None):
        self.nc = nc
        self.cur = (nc.sbuf_base + 63) & ~63
        self.limit = nc.sbuf_top - 64 if limit is None else limit
        self.peak = 0
        self.cnt = 0

    def mark(self):
        return self.cur

    def reset(self, m):
        self.cur = m

    def buf(self, shape, dtype, name=None):
        nbytes = int(np.prod(shape[1:])) * mybir.dt.size(dtype)
        nbytes = (nbytes + 63) & ~63
        off = self.cur
        self.cur += nbytes
        self.peak = max(self.peak, self.cur)
        assert self.cur <= self.limit, f"SBUF overflow at {name}: {self.cur}"
        self.cnt += 1
        return self.nc.alloc_sbuf_tensor_at(name or f"b{self.cnt}",
                                            list(shape), dtype, offset=off)


def build_program(stage=STAGE):
    nc = bass.Bass("TRN2", target_bir_lowering=False,
                   detect_race_conditions=_RACE)
    P = Prog(nc)
    al = Alloc(nc)
    sems = {}

    ALIAS = {}
    for n in ("fsvA fsvSc fsvB fsvD fsvW3 fsvH fsstore asv2 asvhead asgn "
              "asmrg aoggate asvones ap4free ap5free avbr p6f p7f sv0 "
              "fwf fwTf fwdf fwdtf fr2f0 fr2f1 fr2f2 frf0 frf1 frf2 fntr0 "
              "fntr1 fsltf0 fsltf1 fhr0 fhr1 fhr2 fh2f0 fh2f1 fh2f2 "
              "avbr0 avbr1 asQf "
              "fxrf0 fxrf1 fwready fdready anbr0 anbr1 axbf0 axbf1 "
              "fwf0 fwf1 fwTf0 fwTf1 aginf0 aginf1 agonr0 agonr1 "
              "avm").split():
        ALIAS[n] = "VM"
    for n in ("fssA ssM ss0 ass2 fssil0 fssil1 assf0 assf1 atrs agsr0 "
              "agsr1 ashead asm").split():
        ALIAS[n] = "SM"
    for n in ("fstA st0 ast2 fstW fstW2 fmm_done fmmC_done fxqf0 fxqf1 "
              "aavp aavf assr0 assr1 aatf0 aatf1 appr0 appr1 fnhfr0 fnhfr1 "
              "agpr0 agpr1 atm").split():
        ALIAS[n] = "TM"
    for n in "fscc agm".split():
        ALIAS[n] = "GM"
    for i in range(22):
        ALIAS[f"fnhr{i}"] = "VM"
    for a, b in (("frl0", "ra0"), ("frl1", "ra1"), ("frl2", "ra2"),
                 ("fr2l0", "ra0"), ("fr2l1", "ra1"), ("fr2l2", "ra2"),
                 ("axbl0", "ra0"), ("axbl1", "ra1"),
                 ("fxrl0", "ra0"), ("fxrl1", "ra1"),
                 ("px1l0", "ra0"), ("px1l1", "ra1"),
                 ("fxql0", "yb0"), ("fxql1", "yb1"),
                 ("fntf0", "yc0"), ("fntf1", "yc1"),
                 ("anbf0", "yc0"), ("anbf1", "yc1"),
                 ("fx1fr0", "yd0"), ("fx1fr1", "yd1"),
                 ("px2fr0", "yd0"), ("px2fr1", "yd1"),
                 ("fdst_done", "ydt"), ("px2done", "ydt"),
                 ("fh2l0", "ye0"), ("fh2l1", "ye1"), ("fh2l2", "ye2"),
                 ("pg8l0", "sgi0"), ("pg8l1", "sgi1"),
                 ("aginl0", "sgi0"), ("aginl1", "sgi1"),
                 ("agonf0", "sgo0"), ("agonf1", "sgo1"),
                 ("asgonst", "sgot"),
                 ("agsf0", "sgg0"), ("agsf1", "sgg1"),
                 ("fswq", "yw"), ("fswq2", "yw"), ("fwdl", "yw"),
                 ("fwdtl", "yw"), ("fsdq2", "yw"),
                 ("fwl0", "ywa"), ("fwl1", "ywb"),
                 ("fwTl0", "ywc"), ("fwTl1", "ywd"),
                 ("axlr", "yx"), ("ascs", "yx"),
                 ("fhfree0", "sf0"), ("fhfree1", "sf1"), ("fhfree2", "sf2"),
                 ("fhidall", "sft"),
                 ("avtr", "svt"), ("avbfree0", "svb0"), ("avbfree1", "svb1"),
                 ("aswl", "sgw"), ("aswqkv", "sgw"), ("pswo", "sgw"),
                 ("fsgd", "sgd"), ("fsspl", "sgsp"),
                 ("pg8f0", "TM"), ("pg8f1", "TM"),
                 ("p5stat", "VM"), ("asxm", "SM"),
                 ("ppof0", "VM"), ("ppof1", "VM"), ("ppof2", "VM"),
                 ("ppof3", "VM"),
                 ("appf0", "VM"), ("appf1", "VM"),
                 ("agpf0", "SM"), ("agpf1", "SM"),
                 ("fpgr0", "TM"), ("fpgr1", "TM"), ("fpgr2", "TM"),
                 ("fpgr3", "TM"),
                 ("fpdr0", "TM"), ("fpdr1", "TM"), ("fpdr2", "TM"),
                 ("fpdr3", "TM"),
                 ("ppor0", "TM"), ("ppor1", "TM"), ("ppor2", "TM"),
                 ("ppor3", "TM"),
                 ("aatr0", "atrs"), ("aatr1", "atrs"),
                 ("px2fr0", "yd0"), ("px2fr1", "yd1")):
        ALIAS[a] = b
    for n in ("fpgf0", "fpgf2"):
        ALIAS[n] = "SM"
    for n in ("fpgf1", "fpgf3"):
        ALIAS[n] = "VM"
    for n in ("fpdf0", "fpdf1", "fpdf2", "fpdf3"):
        ALIAS[n] = "VM"
    for n in ("ppof0", "ppof1", "ppof2", "ppof3"):
        ALIAS[n] = "VM"
    for n in ("px1f0", "px1f1"):
        ALIAS[n] = "VM"

    def S(name):
        cn = name
        seen = set()
        while cn in ALIAS and cn not in seen:
            seen.add(cn)
            cn = ALIAS[cn]
        if cn not in sems:
            sems[cn] = Sem(nc, cn)
        return sems[cn]

    MILE = {"v": S("VM"), "s": S("SM"), "t": S("TM"), "g": S("GM")}

    def dma(q, out, in_, ss, transpose=False, slow=False):
        if isinstance(ss, Sem):
            ss = (ss,)
        merged = {}
        order = []
        for sm in ss:
            sm.n += 16
            if id(sm.h) not in merged:
                merged[id(sm.h)] = [sm.h, 0]
                order.append(id(sm.h))
            merged[id(sm.h)][1] += 16
        hs = [tuple(merged[i]) for i in order]
        assert len(hs) <= 1, "HW: a DMA may carry only one then_inc"

        def run(e, o=out, i=in_, hh=tuple(hs), tr=transpose, sl=slow):
            ins = e.dma_start(out=o, in_=i, transpose=tr,
                              allow_slow_non_contiguous=sl)
            for h, k in hh:
                ins = ins.then_inc(h, k)
        P.items[q].append(run)

    _waited = {k: {} for k in "tvsgy"}

    def wait(q, sem, tgt=None):
        v = sem.n if tgt is None else tgt
        hw_ = _waited[q]
        key = id(sem.h)
        if v > 0 and v > hw_.get(key, 0):
            hw_[key] = v
            P.items[q].append(lambda e, s=sem.h, vv=v: e.wait_ge(s, vv))

    def emit(q, f, incs=(), ctl=False):
        if isinstance(incs, Sem):
            incs = ((incs, 1),)
        elif (isinstance(incs, tuple) and len(incs) == 2
              and isinstance(incs[0], Sem) and isinstance(incs[1], int)):
            incs = (incs,)
        merged = {}
        order = []
        for it in incs:
            sm, k = it if isinstance(it, tuple) else (it, 1)
            sm.n += k
            if id(sm.h) not in merged:
                merged[id(sm.h)] = [sm.h, 0]
                order.append(id(sm.h))
            merged[id(sm.h)][1] += k
        if not merged and not ctl:
            vm = MILE[q]
            vm.n += 1
            merged[id(vm.h)] = [vm.h, 1]
            order.append(id(vm.h))
            incs = (vm,)
        pairs = [list(merged[i]) for i in order]
        recs = [(sm, sm.n) for sm in
                {id(it if isinstance(it, Sem) else it[0]):
                 (it if isinstance(it, Sem) else it[0])
                 for it in incs}.values()]

        def run(e, ff=f, pp=pairs):
            ins = ff(e)
            for hh, kk in pp:
                ins = ins.then_inc(hh, kk)
        P.items[q].append(run)
        if not ctl:
            P.last[q] = (pairs, recs)

    def din(name, shape, dtype=F32):
        return nc.dram_tensor(name, list(shape), dtype, kind="ExternalInput").ap()

    def dsc(name, shape, dtype=F32):
        kind = "Internal"
        if _DUMP and name in ("hid_sc", "x1_sc"):
            kind = "ExternalOutput"
        return nc.dram_tensor(name, list(shape), dtype, kind=kind).ap()

    x_in = din("x", [T, E])
    wg_n = din("wg_n", [I, E])
    wu_n = din("wu_n", [I, E])
    wgT_i = din("wgT", [E, I])
    wuT_i = din("wuT", [E, I])
    wd_n = din("wd_n", [E, I])
    wdT_i = din("wdT", [I, E])
    wqT_i = din("wqT", [E, E])
    wkT_i = din("wkT", [E, E])
    wvT_i = din("wvT", [E, E])
    wgaT_i = din("wgaT", [E, E])
    woT_i = din("woT", [E, E])
    wdec_i = din("wdec", [E, 16])
    bfm_i = din("bfm", [128, 48])
    bo_i = din("bo_r", [1, E])
    bd_i = din("bd_r", [1, 16])
    clip_i = din("clip2", [1, 2])
    cos_i = din("cosfm", [128, T])
    sin_i = din("sinmfm", [128, T])
    relum_i = din("relum", [128, 2048])
    qexp_i = din("qexp", [128, 512])
    fbias_i = din("fbias", [128, 28])
    ones_i = din("onesr", [1, 128])
    maskA_i = din("maskA", [1, 128])
    maskB_i = din("maskB", [1, 128])
    ident_i = din("identf", [128, 128])
    identb_i = din("identb", [128, 128], BF16)
    cv_i = din("constv", [128, 4])

    out_t = nc.dram_tensor("out", [T, E], F32, kind="ExternalOutput").ap()

    x1_sc = dsc("x1_sc", [T, E])
    x2_sc = dsc("x2_sc", [T, E])
    hid_sc = dsc("hid_sc", [NICH, NTC, 128, 512])
    gate_sc = dsc("gate_sc", [NP, NTC, 128, 512], BF16)
    gon_sc = dsc("gon_sc", [NP, NTC, 128, 512], BF16)
    wspg_sc = dsc("wspg_sc", [8, 128, I], BF16)
    wspu_sc = dsc("wspu_sc", [8, 128, I], BF16)
    wspd_sc = dsc("wspd_sc", [NICH, 128, E], BF16)
    svec_sc = dsc("svec_sc", [1, I])
    cc_in = [dsc(f"cc_in{i}", [1, 1]) for i in range(4)]
    cc_out = [dsc(f"cc_out{i}", [1, 1]) for i in range(4)]

    constv = al.buf([128, 4], F32, "constv")
    COL_EPS, COL_M5LN2 = 0, 1
    onesr = al.buf([1, 128], F32, "onesr")
    maskA = al.buf([1, 128], F32, "maskA")
    maskB = al.buf([1, 128], F32, "maskB")
    identf = al.buf([128, 128], F32, "identf")
    identb = al.buf([128, 128], BF16, "identb")
    bfm = al.buf([128, 48], F32, "bfm")
    CB_Q, CB_K, CB_V, CB_G, CB_GAM, CB_BET = 0, 8, 16, 24, 32, 40
    borep = al.buf([128, E], F32, "borep")
    bdrow = al.buf([1, 16], F32, "bdrow")
    fbias = al.buf([128, 28], F32, "fbias")
    qexp = al.buf([128, 512], F32, "qexp")
    relum = al.buf([128, 2048], F32, "relum")
    clips = al.buf([1, 2], F32, "clips")
    mu_a = al.buf([128, NTT], F32, "mu_a")
    vv_a = al.buf([128, NTT], F32, "vv_a")
    rr_a = al.buf([128, NTT], F32, "rr_a")
    mraw = al.buf([128, NTT], F32, "mraw")
    rq1 = al.buf([128, NTT], F32, "rq1")
    rq2 = al.buf([128, NTT], F32, "rq2")
    stat2 = al.buf([128, 12], F32, "stat2")
    cells = al.buf([1, 32], F32, "cells")
    (C_SIGX, C_SIGD, C_LMAX, C_GMAX, C_SX, C_ISX, C_SH, C_ISH, C_SHHALF,
     C_MUA, C_MUB, C_VA, C_VB, C_RA, C_RB, C_T0, C_T1, C_T2) = range(18)
    rsc = al.buf([128, 8], F32, "rsc")
    R_ISX, R_SX, R_ISH, R_SHH = 0, 1, 2, 3
    sgate = al.buf([128, NICH], F32, "sgate")
    sup = al.buf([128, NICH], F32, "sup")
    svg = al.buf([128, NICH], F32, "svg")
    svu = al.buf([128, NICH], F32, "svu")
    ldrow = al.buf([1, 16], F32, "ldrow")
    ldb = al.buf([128, 16], F32, "ldb")
    srow = al.buf([1, 512], F32, "srow")
    srow2 = al.buf([1, 512], F32, "srow2")
    wdTb = al.buf([128, 128], F32, "wdTb")
    mean8 = al.buf([128, 8], F32, "mean8")
    bnb = al.buf([128, 48], F32, "bnb")
    bo_row = al.buf([1, E], F32, "bo_row")
    rvec = al.buf([128, 1], F32, "rvec")
    mvec = al.buf([128, 1], F32, "mvec")
    Avec = al.buf([128, 1], F32, "Avec")
    Bvec = al.buf([128, 1], F32, "Bvec")
    tvec = al.buf([128, 1], F32, "tvec")

    W0 = al.mark()
    WBYTES = 98304
    al.cur = W0 + WBYTES

    def wbuf(off, shape, dtype, name):
        nb = int(np.prod(shape[1:])) * mybir.dt.size(dtype)
        assert off + nb <= WBYTES, name
        return nc.alloc_sbuf_tensor_at(name, list(shape), dtype, offset=W0 + off)

    mgT = wbuf(0, [128, 8, I], BF16, "mgT")
    muT = wbuf(45056, [128, 8, I], BF16, "muT")
    mdT = wbuf(0, [128, NICH, E], BF16, "mdT")
    cosfm = wbuf(0, [128, T], F32, "cosfm")
    sinmfm = wbuf(16384, [128, T], F32, "sinmfm")
    xln2fm = wbuf(32768, [128, 8, T], BF16, "xln2fm")
    woTb = wbuf(0, [128, 8, E], BF16, "woTb")

    S0 = al.mark()
    pb = [nc.alloc_psum_tensor(f"pb{i}", [128, 512], F32) for i in range(8)]
    p6f = S("p6f")
    p7f = S("p7f")

    def waitr(q, rec):
        if rec is not None:
            wait(q, rec[0], rec[1])

    def load_lnexp(e):
        import concourse.mybir as _mb
        return e.add_instruction(_mb.InstLoadActFuncSet(
            name=e.bass.get_next_instruction_name(),
            act_func_set_id=6, ins=[], outs=[]))

    class Gate:
        def __init__(self):
            self.full = []
            self.per = {}

    def wait_gate(q, g):
        for sm, n in g.full:
            wait(q, sm, n)

    def chain(q):
        le = P.last.get(q)
        if le is None:
            return None
        pairs, recs = le
        if not recs:
            vm = MILE[q]
            vm.n += 1
            pairs.append([vm.h, 1])
            recs = [(vm, vm.n)]
            P.last[q] = (pairs, recs)
        return recs[0]

    def sep(q):
        waitr(q, chain(q))

    trec = {}

    def pe_tr(dst_ap, src_ap, bank, ident_ap, np_, nf):
        TMs, VMs = S("atm"), S("avm")
        waitr("t", trec.get(bank))
        emit("t", lambda e: e.matmul(pb[bank][0:nf, 0:np_], src_ap, ident_ap,
                                     start=True, stop=True), TMs)
        waitr("v", (TMs, TMs.n))
        emit("v", lambda e: e.tensor_copy(dst_ap, pb[bank][0:nf, 0:np_]), VMs)
        trec[bank] = (VMs, VMs.n)
        return (VMs, VMs.n)

    def rsqrt_batch(dst, src, n, sv, ss):
        wait("s", sv)
        sep("s")
        emit("s", lambda e: e.activation(rq1[:, 0:n], src, AF.Ln,
                                         bias=constv[:, COL_EPS:COL_EPS + 1]))
        sep("s")
        emit("s", lambda e: e.activation(dst, rq1[:, 0:n], AF.Exp, scale=-0.5), ss)
        wait("v", ss)
        sep("v")
        emit("v", lambda e: e.tensor_scalar_add(rq1[:, 0:n], src, EPS))
        emit("v", lambda e: e.tensor_mul(rq2[:, 0:n], dst, dst))
        sep("v")
        emit("v", lambda e: e.tensor_mul(rq2[:, 0:n], rq2[:, 0:n], rq1[:, 0:n]))
        sep("v")
        emit("v", lambda e: e.tensor_scalar(rq2[:, 0:n], rq2[:, 0:n], -0.5, 1.5,
                                            ALU.mult, ALU.add))
        sep("v")
        emit("v", lambda e: e.tensor_mul(dst, dst, rq2[:, 0:n]), sv)

    def rsqrt_row(dst, src, t1, t2, sv, ss):
        wait("s", sv)
        sep("s")
        emit("s", lambda e: e.activation(t1, src, AF.Ln,
                                         bias=constv[0:1, COL_EPS:COL_EPS + 1]))
        sep("s")
        emit("s", lambda e: e.activation(dst, t1, AF.Exp, scale=-0.5), ss)
        wait("v", ss)
        sep("v")
        emit("v", lambda e: e.tensor_scalar_add(t1, src, EPS))
        emit("v", lambda e: e.tensor_mul(t2, dst, dst))
        sep("v")
        emit("v", lambda e: e.tensor_mul(t2, t2, t1))
        sep("v")
        emit("v", lambda e: e.tensor_scalar(t2, t2, -0.5, 1.5, ALU.mult, ALU.add))
        sep("v")
        emit("v", lambda e: e.tensor_mul(dst, dst, t2), sv)

    def bcast_cell(cell_col, rsc_col, sv, st):
        wait("t", p6f)
        emit("t", lambda e: e.matmul(pb[6][0:128, 0:1], onesr[0:1, :],
                                     cells[0:1, cell_col:cell_col + 1],
                                     start=True, stop=True), st)
        wait("v", st)
        emit("v", lambda e: e.tensor_copy(rsc[:, rsc_col:rsc_col + 1],
                                          pb[6][0:128, 0:1]), (sv, p6f))

    def cross_max_to_cell(src_ap, cell_col, sv, st):
        wait("t", sv)
        wait("t", p6f)
        emit("t", lambda e: e.matmul(pb[6][0:1, 0:128], src_ap, identf[:, :],
                                     start=True, stop=True), st)
        wait("v", st)
        emit("v", lambda e: e.reduce_max(cells[0:1, cell_col:cell_col + 1],
                                         pb[6][0:1, 0:128], axis=AX), (sv, p6f))

    def collective(idx, c_src, c_dst, sv, sgd, scc):
        wait("g", sv)
        dma("g", cc_in[idx], cells[0:1, c_src:c_src + 1], sgd)
        wait("g", sgd)
        emit("g", lambda e: e.collective_compute(
            "AllReduce", ALU.max, replica_groups=[list(range(8))],
            ins=[cc_in[idx]], outs=[cc_out[idx]]), scc)
        wait("g", scc)
        dma("g", cells[0:1, c_dst:c_dst + 1], cc_out[idx], sgd)
        wait("v", sgd)

    sc_pre = S("sc_pre")
    for dst, src in ((constv[:, :], cv_i), (onesr[:, :], ones_i),
                     (maskA[:, :], maskA_i), (maskB[:, :], maskB_i),
                     (identf[:, :], ident_i), (identb[:, :], identb_i),
                     (bfm[:, :], bfm_i),
                     (bdrow[:, :], bd_i), (fbias[:, :], fbias_i),
                     (qexp[:, :], qexp_i), (relum[:, :], relum_i),
                     (clips[:, :], clip_i), (bo_row[:, :], bo_i),
                     (wdTb[:, :].rearrange("p (c f) -> p c f", f=16),
                      wdec_i.rearrange("(c p) f -> p c f", p=128))):
        dma("y", dst, src, sc_pre, slow=True)

    ss0, sv0, st0 = S("ss0"), S("sv0"), S("st0")
    emit("s", load_lnexp, ctl=True)
    wait("s", sc_pre)
    emit("s", lambda e: e.activation(cells[0:1, 0:2], clips[0:1, 0:2], AF.Exp,
                                     scale=-1.0), ss0)
    wait("v", ss0)
    sep("v")
    emit("v", lambda e: e.tensor_scalar_add(cells[0:1, 0:2], cells[0:1, 0:2], 1.0))
    sep("v")
    emit("v", lambda e: e.reciprocal(cells[0:1, 0:2], cells[0:1, 0:2]), sv0)
    wait("t", sc_pre)
    for c in range(2):
        emit("t", lambda e, c=c: e.matmul(pb[6 + c][:, :], onesr[0:1, :],
                                          bo_row[0:1, c * 512:(c + 1) * 512],
                                          start=True, stop=True), st0)
    wait("v", st0)
    emit("v", lambda e: e.tensor_copy(borep[:, 0:512], pb[6][:, :]), (sv0, p6f))
    emit("v", lambda e: e.tensor_copy(borep[:, 512:1024], pb[7][:, :]),
         (sv0, p7f))

    def ffn(tag, src_ap, dst_ap, cc_a, cc_b, first, ln_next, have_stats,
            src_gate=None, stats_gate=None, dst2_ap=None, upto="z"):
        Sm = al.mark()
        svA, ssA, stA = S("fsvA"), S("fssA"), S("fstA")

        if not have_stats:
            xt = [al.buf([128, E], F32, f"{tag}xt{i}") for i in range(3)]
            tmpA = al.buf([128, E], F32, f"{tag}tmpA")
            rload = [S(f"frl{i}") for i in range(3)]
            rfree = [S(f"frf{i}") for i in range(3)]
            for tt in range(NTT):
                sl = tt % 3
                wait("y", rfree[sl])
                if src_gate is not None:
                    waitr("y", src_gate.per.get(tt))
                dma("y", xt[sl][:, :], src_ap[tt * 128:(tt + 1) * 128, :],
                    rload[sl])
                wait("v", rload[sl])
                emit("v", lambda e, sl=sl: e.bn_stats(stat2[:, 0:6],
                                                      xt[sl][:, 0:512]))
                emit("v", lambda e, sl=sl: e.bn_stats(stat2[:, 6:12],
                                                      xt[sl][:, 512:1024]))
                sep("v")
                emit("v", lambda e: e.bn_aggr(bnb[:, 0:2], stat2[:, 0:12]))
                sep("v")
                emit("v", lambda e, tt=tt: e.tensor_copy(mu_a[:, tt:tt + 1],
                                                         bnb[:, 0:1]))
                emit("v", lambda e, tt=tt: e.tensor_copy(vv_a[:, tt:tt + 1],
                                                         bnb[:, 1:2]))
                sep("v")
                emit("v", lambda e, sl=sl, tt=tt: e.tensor_scalar(
                    tmpA[:, :], xt[sl][:, :], mu_a[:, tt:tt + 1], None,
                    ALU.subtract, ALU.bypass), rfree[sl])
                sep("v")
                emit("v", lambda e, tt=tt: e.reduce_max(
                    mraw[:, tt:tt + 1], tmpA[:, :], axis=AX,
                    apply_absolute_value=True), svA)
            al.reset(Sm)
        else:
            if stats_gate is not None:
                wait("v", stats_gate)
                wait("s", stats_gate)
        rsqrt_batch(rr_a[:, :], vv_a[:, :], NTT, svA, ssA)
        sep("v")
        emit("v", lambda e: e.tensor_mul(rq1[:, :], mraw[:, :], rr_a[:, :]))
        sep("v")
        emit("v", lambda e: e.reduce_max(stat2[:, 0:1], rq1[:, :], axis=AX), svA)
        cross_max_to_cell(stat2[:, 0:1], C_LMAX, svA, stA)
        sgd, scc = S("fsgd"), S("fscc")
        collective(cc_a, C_LMAX, C_GMAX, svA, sgd, scc)
        emit("v", lambda e: e.tensor_mul(cells[0:1, C_SX:C_SX + 1],
                                         cells[0:1, C_GMAX:C_GMAX + 1],
                                         cells[0:1, C_SIGX:C_SIGX + 1]))
        sep("v")
        emit("v", lambda e: e.tensor_scalar_mul(cells[0:1, C_SX:C_SX + 1],
                                                cells[0:1, C_SX:C_SX + 1],
                                                1.0 / QMAX))
        sep("v")
        emit("v", lambda e: e.reciprocal(cells[0:1, C_ISX:C_ISX + 1],
                                         cells[0:1, C_SX:C_SX + 1]), svA)
        wait("t", svA)
        bcast_cell(C_ISX, R_ISX, svA, stA)
        bcast_cell(C_SX, R_SX, svA, stA)
        if upto == "a":
            return None, None, None

        swq = S("fswq")
        wready = S("fwready")
        sspl = S("fsspl")
        if first:
            wl = [S(f"fwl{i}") for i in range(2)]
            wf = [S(f"fwf{i}") for i in range(2)]
            wnat = [al.buf([128, E], F32, f"{tag}wn{i}") for i in range(2)]
            for mat in range(2):
                w_nat = (wg_n, wu_n)[mat]
                sdst = (sgate, sup)[mat]
                for ich in range(NICH):
                    it = mat * NICH + ich
                    sl = it % 2
                    wait("y", wf[sl])
                    dma("y", wnat[sl][:, :], w_nat[ich * 128:(ich + 1) * 128, :],
                        wl[sl])
                    wait("v", wl[sl])
                    emit("v", lambda e, sl=sl, sdst=sdst, ich=ich: e.reduce_max(
                        sdst[:, ich:ich + 1], wnat[sl][:, :], axis=AX,
                        apply_absolute_value=True), wf[sl])
            sep("v")
            emit("v", lambda e: e.tensor_scalar_mul(sgate[:, :], sgate[:, :],
                                                    1.0 / QMAX))
            emit("v", lambda e: e.tensor_scalar_mul(sup[:, :], sup[:, :],
                                                    1.0 / QMAX))
            wT = [al.buf([128, I], F32, f"{tag}wTt{i}") for i in range(2)]
            irb = al.buf([128, I], F32, f"{tag}irb")
            irow = al.buf([1, I], F32, f"{tag}irow")
            sg_r = svec_sc.rearrange("one (c p) -> p (c one)", p=128)
            wTl = [S(f"fwTl{i}") for i in range(2)]
            wTf = [S(f"fwTf{i}") for i in range(2)]
            stW, svSc, svB = S("fstW"), S("fsvSc"), S("fsvB")
            wtct = 0
            for mat in range(2):
                ssrc = (sgate, sup)[mat]
                wT_in = (wgT_i, wuT_i)[mat]
                mdst = (mgT, muT)[mat]
                wsp = (wspg_sc, wspu_sc)[mat]
                wait("v", swq)
                sep("v")
                emit("v", lambda e, ssrc=ssrc: e.reciprocal(rq1[:, 0:NICH], ssrc[:, :]),
                     svSc)
                wait("y", svSc)
                dma("y", sg_r, rq1[:, 0:NICH], swq, slow=True)
                wait("y", swq)
                dma("y", irow[:, :], svec_sc, swq)
                wait("t", swq)
                for c in range(6):
                    w = min(512, I - c * 512)
                    wait("t", p6f)
                    emit("t", lambda e, c=c, w=w: e.matmul(
                        pb[6][0:128, 0:w], onesr[0:1, :],
                        irow[0:1, c * 512:c * 512 + w], start=True, stop=True),
                        stW)
                    wait("v", stW)
                    emit("v", lambda e, c=c, w=w: e.tensor_copy(
                        irb[:, c * 512:c * 512 + w], pb[6][0:128, 0:w]),
                        (svB, p6f))
                for hch in range(8):
                    sl = wtct % 2
                    wait("y", wTf[sl])
                    wtct += 1
                    dma("y", wT[sl][:, :], wT_in[hch * 128:(hch + 1) * 128, :],
                        wTl[sl])
                    wait("v", wTl[sl])
                    wait("v", svB)
                    emit("v", lambda e, sl=sl: e.tensor_mul(
                        wT[sl][:, :], wT[sl][:, :], irb[:, :]))
                    sep("v")
                    emit("v", lambda e, sl=sl: e.tensor_scalar(
                        wT[sl][:, :], wT[sl][:, :], MAGIC, MAGIC,
                        ALU.add, ALU.subtract))
                    sep("v")
                    emit("v", lambda e, sl=sl, mdst=mdst, hch=hch: e.tensor_copy(
                        mdst[:, hch, :], wT[sl][:, :]), (wTf[sl], wready))
                    wait("g", wready)
                    dma("g", wsp[hch], mdst[:, hch, :], sspl)
            al.reset(Sm)
        else:
            wait("y", sspl, sspl.n)
            if src_gate is not None:
                wait_gate("y", src_gate)
            for hch in range(8):
                dma("y", mgT[:, hch, :], wspg_sc[hch], swq)
                dma("y", muT[:, hch, :], wspu_sc[hch], swq)
        wrec = (wready, wready.n) if first else (swq, swq.n)
        wait("v", svA)
        emit("v", lambda e: e.tensor_scalar(svg[:, :], sgate[:, :],
                                            rsc[:, R_SX:R_SX + 1], None,
                                            ALU.mult, ALU.bypass))
        emit("v", lambda e: e.tensor_scalar(svu[:, :], sup[:, :],
                                            rsc[:, R_SX:R_SX + 1], None,
                                            ALU.mult, ALU.bypass), svA)
        if upto == "w":
            return None, None, None

        xt2 = [al.buf([128, E], F32, f"{tag}x2t{i}") for i in range(3)]
        ntile = [al.buf([128, E], BF16, f"{tag}nt{i}") for i in range(2)]
        ntfree = {}
        xqrec = {}
        tq = al.buf([128, E], F32, f"{tag}tq")
        xq = [al.buf([128, 8, 512], BF16, f"{tag}xq{i}") for i in range(2)]
        slt = [al.buf([128, 512], F32, f"{tag}slt{i}") for i in range(2)]
        hidt = [al.buf([128, 512], F32, f"{tag}hidt{i}") for i in range(3)]
        hmx = al.buf([128, 2], F32, f"{tag}hmx")
        r2load = [S(f"fr2l{i}") for i in range(3)]
        r2free = [S(f"fr2f{i}") for i in range(3)]
        ntr = [S(f"fntr{i}") for i in range(2)]
        xqf = [S(f"fxqf{i}") for i in range(2)]
        pgr = [S(f"fpgr{i}") for i in range(4)]
        pgf = [S(f"fpgf{i}") for i in range(4)]
        s_sil = [S(f"fssil{i}") for i in range(2)]
        sltf = [S(f"fsltf{i}") for i in range(2)]
        hr = [S(f"fhr{i}") for i in range(3)]
        hfree = [S(f"fhfree{i}") for i in range(3)]
        hstrec = {}
        bfree = {}
        sltrec = {}
        mm_done = S("fmm_done")
        emit("v", lambda e: e.memset(hmx[:, 0:1], 0.0), svA)
        emit("s", lambda e: e.preload_act_table(AF.Silu), ctl=True)
        waitr("s", wrec)
        waitr("t", wrec)
        for tc in range(NTC):
            xsl = tc % 2
            wait("y", xqf[xsl])
            for tti in range(4):
                tt = tc * 4 + tti
                sl = tt % 3
                npar = tt % 2
                wait("y", r2free[sl])
                if src_gate is not None:
                    wait_gate("y", src_gate)
                dma("y", xt2[sl][:, :], src_ap[tt * 128:(tt + 1) * 128, :],
                    r2load[sl])
                wait("v", r2load[sl])
                if tt == 0:
                    wait("v", svA)
                waitr("v", ntfree.get(npar))
                emit("v", lambda e, sl=sl, tt=tt: e.tensor_scalar(
                    tq[:, :], xt2[sl][:, :], mu_a[:, tt:tt + 1], rr_a[:, tt:tt + 1],
                    ALU.subtract, ALU.mult), r2free[sl])
                sep("v")
                emit("v", lambda e: e.tensor_scalar(
                    tq[:, :], tq[:, :], rsc[:, R_ISX:R_ISX + 1], MAGIC, ALU.mult, ALU.add))
                sep("v")
                emit("v", lambda e: e.tensor_scalar(
                    tq[:, :], tq[:, :], MAGIC, -(QMAX + 1.0), ALU.subtract, ALU.max))
                sep("v")
                emit("v", lambda e, npar=npar: e.tensor_scalar_min(
                    ntile[npar][:, :], tq[:, :], QMAX), ntr[npar])
                waitr("t", (S("avm"), S("avm").n))
                for d in ([] if _NOTR else range(8)):
                    lastrec = pe_tr(xq[xsl][:, d, tti * 128:(tti + 1) * 128],
                                    ntile[npar][:, d * 128:(d + 1) * 128],
                                    4 + (d % 2), identb[:, :], 128, 128)
                if not _NOTR:
                    ntfree[npar] = (S("atm"), S("atm").n)
                    xqrec[xsl] = lastrec
            waitr("t", xqrec.get(xsl))
            for ich in ([] if _NOMM else range(NICH)):
                bsel = (tc * NICH + ich) % 2
                bg, bu = 2 * bsel, 2 * bsel + 1
                spar = (tc * NICH + ich) % 2
                if _MMONLY:
                    for d in range(8):
                        emit("t", lambda e, d=d, xsl=xsl: e.matmul(
                            pb[0][:, :], mgT[:, d, 0:128],
                            xq[xsl][:, d, :], start=True, stop=True),
                            S("fpgr0") if d == 7 else ())
                    continue
                waitr("t", bfree.get(bg))
                waitr("t", bfree.get(bu))
                for d in range(8):
                    emit("t", lambda e, bg=bg, ich=ich, d=d, xsl=xsl: e.matmul(
                        pb[bg][:, :], mgT[:, d, ich * 128:(ich + 1) * 128],
                        xq[xsl][:, d, :], start=(d == 0), stop=(d == 7)),
                        pgr[bg] if d == 7 else ())
                for d in range(8):
                    fin = (d == 7) and ich == NICH - 1
                    incs = []
                    if d == 7:
                        incs.append(pgr[bu])
                    if fin:
                        incs.append(xqf[xsl])
                        if tc == NTC - 1:
                            incs.append(mm_done)
                    emit("t", lambda e, bu=bu, ich=ich, d=d, xsl=xsl: e.matmul(
                        pb[bu][:, :], muT[:, d, ich * 128:(ich + 1) * 128],
                        xq[xsl][:, d, :], start=(d == 0), stop=(d == 7)), incs)
                if _NOSILU:
                    hsl = (tc * NICH + ich) % 3
                    wait("s", pgr[bg])
                    emit("s", lambda e, bg=bg, hsl=hsl: e.copy(
                        slt[spar][:, :], pb[bg][:, :]), (s_sil[spar], pgf[bg]))
                    wait("v", hfree[hsl])
                    wait("v", pgr[bu])
                    wait("v", s_sil[spar])
                    emit("v", lambda e, bu=bu, hsl=hsl, spar=spar:
                         e.tensor_add(hidt[hsl][:, :], pb[bu][:, :],
                                      slt[spar][:, :]), (pgf[bu], hr[hsl]))
                    wait("s", hr[hsl])
                    dma("s", hid_sc[ich, tc], hidt[hsl][:, :], hfree[hsl])
                    hstrec[tc * NICH + ich] = (hfree[hsl], hfree[hsl].n)
                    continue
                wait("s", pgr[bg])
                waitr("s", sltrec.get(spar))
                emit("s", lambda e, bg=bg, ich=ich, spar=spar: e.activation(
                    slt[spar][:, :], pb[bg][:, :], AF.Silu,
                    scale=svg[:, ich:ich + 1]), (s_sil[spar], pgf[bg]))
                bfree[bg] = (pgf[bg], pgf[bg].n)
                hsl = (tc * NICH + ich) % 3
                wait("v", hfree[hsl])
                wait("v", pgr[bu])
                wait("v", s_sil[spar])
                emit("v", lambda e, bu=bu, ich=ich, hsl=hsl, spar=spar:
                     e.scalar_tensor_tensor(
                         hidt[hsl][:, :], pb[bu][:, :], svu[:, ich:ich + 1],
                         slt[spar][:, :], ALU.mult, ALU.mult),
                     (pgf[bu], sltf[spar]))
                bfree[bu] = (pgf[bu], pgf[bu].n)
                sltrec[spar] = (sltf[spar], sltf[spar].n)
                sep("v")
                emit("v", lambda e, hsl=hsl: e.reduce_max(
                    hmx[:, 1:2], hidt[hsl][:, :], axis=AX,
                    apply_absolute_value=True))
                sep("v")
                emit("v", lambda e: e.tensor_max(hmx[:, 0:1], hmx[:, 0:1],
                                                 hmx[:, 1:2]), hr[hsl])
                wait("s", hr[hsl])
                dma("s", hid_sc[ich, tc], hidt[hsl][:, :], hfree[hsl])
                hstrec[tc * NICH + ich] = (hfree[hsl], hfree[hsl].n)

        svH = S("fsvH")
        cross_max_to_cell(hmx[:, 0:1], C_LMAX, svH, stA)
        collective(cc_b, C_LMAX, C_GMAX, svH, sgd, scc)
        emit("v", lambda e: e.tensor_mul(cells[0:1, C_SH:C_SH + 1],
                                         cells[0:1, C_GMAX:C_GMAX + 1],
                                         cells[0:1, C_SIGD:C_SIGD + 1]))
        sep("v")
        emit("v", lambda e: e.tensor_scalar_mul(cells[0:1, C_SH:C_SH + 1],
                                                cells[0:1, C_SH:C_SH + 1],
                                                1.0 / QMAX))
        sep("v")
        emit("v", lambda e: e.reciprocal(cells[0:1, C_ISH:C_ISH + 1],
                                         cells[0:1, C_SH:C_SH + 1]))
        emit("v", lambda e: e.tensor_scalar_mul(
            cells[0:1, C_SHHALF:C_SHHALF + 1], cells[0:1, C_SH:C_SH + 1], 0.5),
            svH)
        wait("t", svH)
        bcast_cell(C_ISH, R_ISH, svH, stA)
        bcast_cell(C_SHHALF, R_SHH, svH, stA)
        if upto == "b":
            return None, None, None

        Sd = al.mark()
        dready = S("fdready")
        if first:
            swd = al.buf([128, 8], F32, f"{tag}swd")
            Sd2 = al.mark()
            wdn = al.buf([128, I], F32, f"{tag}wdn")
            wdl = S("fwdl")
            wdf = S("fwdf")
            svD = S("fsvD")
            for ech in range(8):
                wait("y", wdf)
                dma("y", wdn[:, :], wd_n[ech * 128:(ech + 1) * 128, :], wdl)
                wait("v", wdl)
                emit("v", lambda e, ech=ech, swd=swd: e.reduce_max(
                    swd[:, ech:ech + 1], wdn[:, :], axis=AX,
                    apply_absolute_value=True), wdf)
            al.reset(Sd2)
            sep("v")
            emit("v", lambda e, swd=swd: e.tensor_scalar_mul(
                swd[:, :], swd[:, :], 1.0 / QMAX))
            svd_r = svec_sc.rearrange("one (c p) -> p (c one)", p=128)
            sep("v")
            emit("v", lambda e, swd=swd: e.reciprocal(rq1[:, 0:8], swd[:, :]),
                 svD)
            irowd = al.buf([1, E], F32, f"{tag}irowd")
            srowd = al.buf([1, E], F32, f"{tag}srowd")
            swq2 = S("fswq2")
            wait("y", svD)
            dma("y", svd_r[:, 0:8], rq1[:, 0:8], swq2, slow=True)
            wait("y", swq2)
            dma("y", irowd[:, :], svec_sc[0:1, 0:E], swq2)
            wait("y", swq2)
            dma("y", svd_r[:, 0:8], swd[:, :], swq2, slow=True)
            wait("y", swq2)
            dma("y", srowd[:, :], svec_sc[0:1, 0:E], swq2)
            irbd = al.buf([128, E], F32, f"{tag}irbd")
            srbd = al.buf([128, E], F32, f"{tag}srbd")
            stW2, svW3 = S("fstW2"), S("fsvW3")
            wait("t", swq2)
            for (row, dstb) in ((irowd, irbd), (srowd, srbd)):
                for c in range(2):
                    wait("t", p6f)
                    emit("t", lambda e, row=row, c=c: e.matmul(
                        pb[6][:, :], onesr[0:1, :],
                        row[0:1, c * 512:(c + 1) * 512], start=True, stop=True),
                        stW2)
                    wait("v", stW2)
                    emit("v", lambda e, dstb=dstb, c=c: e.tensor_copy(
                        dstb[:, c * 512:(c + 1) * 512], pb[6][:, :]),
                        (svW3, p6f))
            wdt = al.buf([128, E], F32, f"{tag}wdt")
            wdtl = S("fwdtl")
            wdtf = S("fwdtf")
            for ich in range(NICH):
                wait("y", wdtf)
                dma("y", wdt[:, :], wdT_i[ich * 128:(ich + 1) * 128, :], wdtl)
                wait("v", wdtl)
                wait("v", svW3)
                if ich == 0:
                    wait("v", mm_done)
                    wait("v", sspl, sspl.n)
                emit("v", lambda e: e.tensor_mul(wdt[:, :], wdt[:, :],
                                                 irbd[:, :]))
                sep("v")
                emit("v", lambda e: e.tensor_scalar(
                    wdt[:, :], wdt[:, :], MAGIC, MAGIC, ALU.add, ALU.subtract))
                sep("v")
                emit("v", lambda e: e.tensor_mul(wdt[:, :], wdt[:, :],
                                                 srbd[:, :]))
                sep("v")
                emit("v", lambda e, ich=ich: e.tensor_copy(
                    mdT[:, ich, :], wdt[:, :]), (wdtf, dready))
                wait("g", dready)
                dma("g", wspd_sc[ich], mdT[:, ich, :], sspl)
            al.reset(Sd)
        else:
            wait("y", mm_done)
            wait("y", sspl, sspl.n)
            sdq2 = S("fsdq2")
            for ich in range(NICH):
                dma("y", mdT[:, ich, :], wspd_sc[ich], sdq2)
        drec = (dready, dready.n) if first else (sdq2, sdq2.n)
        if upto == "d":
            return None, None, None

        al.reset(Sm)
        hin = [al.buf([128, 512], F32, f"{tag}hin{i}") for i in range(3)]
        tqv = al.buf([128, 512], F32, f"{tag}tqv")
        tqg = al.buf([128, 512], F32, f"{tag}tqg")
        nh = [al.buf([128, NICH, 512], BF16, f"{tag}nhp{i}") for i in range(2)]
        lnt = al.buf([128, E], F32, f"{tag}lnt")
        xr = [al.buf([128, E], F32, f"{tag}xr{i}") for i in range(2)]
        x1t = [al.buf([128, E], F32, f"{tag}x1t{i}") for i in range(2)]
        h2l = [S(f"fh2l{i}") for i in range(3)]
        h2f = [S(f"fh2f{i}") for i in range(3)]
        nhr = [S(f"fnhr{i}") for i in range(NICH)]
        nhfree = [S(f"fnhfr{i}") for i in range(2)]
        xrl = [S(f"fxrl{i}") for i in range(2)]
        xrf = [S(f"fxrf{i}") for i in range(2)]
        x1fr = [S(f"fx1fr{i}") for i in range(2)]
        pdr = [S(f"fpdr{i}") for i in range(4)]
        pdf = [S(f"fpdf{i}") for i in range(4)]
        sstore = S("fsstore")
        nhrec = {}
        pdfree = {}
        dst_gate = Gate()
        mmC_done = S("fmmC_done")
        for i in range(3):
            wait("y", hfree[i], hfree[i].n)
        wait("y", mm_done)
        waitr("y", drec)
        for tc in range(NTC):
            par = tc % 2
            gwaited = [False]
            vwaited = [False]
            for ich in range(NICH):
                it = tc * NICH + ich
                sl = it % 3
                wait("y", h2f[sl])
                waitr("y", hstrec.get(it))
                dma("y", hin[sl][:, :], hid_sc[ich, tc], h2l[sl])
                qeng = "v"
                tqx = tqv
                wait(qeng, h2l[sl])
                if it == 0:
                    wait("v", svH)
                if not vwaited[0]:
                    wait(qeng, nhfree[par])
                    vwaited[0] = True
                emit(qeng, lambda e, sl=sl, tqx=tqx: e.tensor_scalar(
                    tqx[:, :], hin[sl][:, :], rsc[:, R_ISH:R_ISH + 1], MAGIC,
                    ALU.mult, ALU.add), h2f[sl])
                sep(qeng)
                emit(qeng, lambda e, tqx=tqx: e.tensor_scalar(
                    tqx[:, :], tqx[:, :], MAGIC, -(QMAX + 1.0), ALU.subtract, ALU.max))
                sep(qeng)
                emit(qeng, lambda e, tqx=tqx, ich=ich, par=par:
                     e.tensor_scalar_min(nh[par][:, ich, :], tqx[:, :], QMAX),
                     nhr[ich])
                nhrec[ich] = (nhr[ich], nhr[ich].n)
            for tti in range(4):
                tt = tc * 4 + tti
                bsel = tt % 2
                bl, br = 2 * bsel, 2 * bsel + 1
                waitr("t", pdfree.get(bl))
                waitr("t", pdfree.get(br))
                if tt == 0:
                    waitr("t", drec)
                for ich in range(NICH):
                    if tti == 0:
                        waitr("t", nhrec.get(ich))
                    for half in range(2):
                        bk = bl if half == 0 else br
                        incs = []
                        if ich == NICH - 1:
                            incs.append(pdr[bk])
                            if tti == 3 and half == 1:
                                incs.append(nhfree[par])
                                if tc == NTC - 1:
                                    incs.append(mmC_done)
                        emit("t", lambda e, bk=bk, ich=ich, tti=tti, half=half,
                             par=par: e.matmul(
                                 pb[bk][:, :],
                                 nh[par][:, ich, tti * 128:(tti + 1) * 128],
                                 mdT[:, ich, half * 512:(half + 1) * 512],
                                 start=(ich == 0), stop=(ich == NICH - 1)),
                             incs)
                xsl = tt % 2
                wait("y", xrf[xsl])
                if src_gate is not None:
                    wait_gate("y", src_gate)
                dma("y", xr[xsl][:, :], src_ap[tt * 128:(tt + 1) * 128, :],
                    xrl[xsl])
                wait("v", xrl[xsl])
                wait("v", pdr[bl])
                wait("v", pdr[br])
                wait("v", x1fr[xsl])
                emit("v", lambda e, bl=bl, xsl=xsl: e.scalar_tensor_tensor(
                    x1t[xsl][:, 0:512], pb[bl][:, :], rsc[:, R_SHH:R_SHH + 1],
                    xr[xsl][:, 0:512], ALU.mult, ALU.add), pdf[bl])
                pdfree[bl] = (pdf[bl], pdf[bl].n)
                emit("v", lambda e, br=br, xsl=xsl: e.scalar_tensor_tensor(
                    x1t[xsl][:, 512:1024], pb[br][:, :], rsc[:, R_SHH:R_SHH + 1],
                    xr[xsl][:, 512:1024], ALU.mult, ALU.add),
                    (pdf[br], xrf[xsl]))
                pdfree[br] = (pdf[br], pdf[br].n)
                if ln_next:
                    sep("v")
                    emit("v", lambda e, xsl=xsl: e.bn_stats(
                        stat2[:, 0:6], x1t[xsl][:, 0:512]))
                    emit("v", lambda e, xsl=xsl: e.bn_stats(
                        stat2[:, 6:12], x1t[xsl][:, 512:1024]))
                    sep("v")
                    emit("v", lambda e: e.bn_aggr(bnb[:, 0:2], stat2[:, 0:12]))
                    sep("v")
                    emit("v", lambda e, tt=tt: e.tensor_copy(
                        mu_a[:, tt:tt + 1], bnb[:, 0:1]))
                    emit("v", lambda e, tt=tt: e.tensor_copy(
                        vv_a[:, tt:tt + 1], bnb[:, 1:2]))
                    sep("v")
                    emit("v", lambda e, xsl=xsl, tt=tt: e.tensor_scalar(
                        lnt[:, :], x1t[xsl][:, :], mu_a[:, tt:tt + 1], None,
                        ALU.subtract, ALU.bypass))
                    sep("v")
                    emit("v", lambda e, tt=tt: e.reduce_max(
                        mraw[:, tt:tt + 1], lnt[:, :], axis=AX,
                        apply_absolute_value=True), sstore)
                    wait("y", sstore)
                else:
                    waitr("y", chain("v"))
                dma("y", dst_ap[tt * 128:(tt + 1) * 128, :], x1t[xsl][:, :],
                    x1fr[xsl])
                if dst2_ap is not None:
                    dma("y", dst2_ap[tt * 128:(tt + 1) * 128, :],
                        x1t[xsl][:, :], x1fr[xsl])
                dst_gate.per[tt] = (x1fr[xsl], x1fr[xsl].n)
        dst_gate.full = [(x1fr[0], x1fr[0].n), (x1fr[1], x1fr[1].n)]
        al.reset(Sm)
        return dst_gate, mmC_done, sstore

    def attn_phase(x1done, mmC_done, stats_gate):
        Sm = al.mark()
        ppfree = {}
        ssfree = {}
        gpfree = {}
        sv2, ss2, st2 = S("asv2"), S("ass2"), S("ast2")
        emit("s", load_lnexp, ctl=True)
        wait("s", stats_gate)
        wait("v", stats_gate)
        rsqrt_batch(rr_a[:, :], vv_a[:, :], NTT, sv2, ss2)

        Smb = al.mark()
        xb = [al.buf([128, E], F32, f"axb{i}") for i in range(2)]
        nb = [al.buf([128, E], BF16, f"anb{i}") for i in range(2)]
        nbfree = {}
        xbl = [S(f"axbl{i}") for i in range(2)]
        xbf_ = [S(f"axbf{i}") for i in range(2)]
        nbr = [S(f"anbr{i}") for i in range(2)]
        xlrec = [None]
        wait("y", mmC_done)
        wait_gate("y", x1done)
        wait("v", S("fsspl"), S("fsspl").n)
        for tt in range(NTT):
            sl = tt % 2
            wait("y", xbf_[sl])
            dma("y", xb[sl][:, :], x1_sc[tt * 128:(tt + 1) * 128, :], xbl[sl])
            wait("v", xbl[sl])
            if tt == 0:
                wait("v", sv2)
            waitr("v", nbfree.get(sl))
            emit("v", lambda e, sl=sl, tt=tt: e.tensor_scalar(
                nb[sl][:, :], xb[sl][:, :], mu_a[:, tt:tt + 1],
                rr_a[:, tt:tt + 1], ALU.subtract, ALU.mult),
                (nbr[sl], xbf_[sl]))
            waitr("t", (S("avm"), S("avm").n))
            for d in range(8):
                xlrec[0] = pe_tr(xln2fm[:, d, tt * 128:(tt + 1) * 128],
                                 nb[sl][:, d * 128:(d + 1) * 128],
                                 4 + (d % 2), identb[:, :], 128, 128)
            nbfree[sl] = (S("atm"), S("atm").n)
        scs = S("ascs")
        dma("y", cosfm[:, :], cos_i, scs)
        dma("y", sinmfm[:, :], sin_i, scs)
        al.reset(Smb)

        waitr("v", xlrec[0])
        for ech in range(8):
            emit("v", lambda e, ech=ech: e.reduce_sum(
                mean8[:, ech:ech + 1], xln2fm[:, ech, :], axis=AX))
        sep("v")
        emit("v", lambda e: e.tensor_scalar_mul(mean8[:, :], mean8[:, :],
                                                1.0 / T), sv2)
        wait("t", sv2)
        wait("t", p6f)
        for ech in range(8):
            emit("t", lambda e, ech=ech: e.matmul(
                pb[6][0:1, 0:16], mean8[:, ech:ech + 1],
                wdTb[:, ech * 16:(ech + 1) * 16],
                start=(ech == 0), stop=(ech == 7)),
                st2 if ech == 7 else ())
        wait("v", st2)
        emit("v", lambda e: e.tensor_add(srow[0:1, 0:16], pb[6][0:1, 0:16],
                                         bdrow[0:1, :]), (sv2, p6f))
        wait("s", sv2)
        emit("s", lambda e: e.activation(srow2[0:1, 0:16], srow[0:1, 0:16],
                                         AF.Exp, scale=-1.0), ss2)
        wait("v", ss2)
        emit("v", lambda e: e.tensor_scalar_add(srow2[0:1, 0:16],
                                                srow2[0:1, 0:16], 1.0))
        sep("v")
        emit("v", lambda e: e.reciprocal(srow[0:1, 16:32], srow2[0:1, 0:16]),
             sv2)
        wait("s", sv2)
        emit("s", lambda e: e.activation(srow2[0:1, 0:16], srow[0:1, 16:32],
                                         AF.Exp, scale=-8.0 * LN2,
                                         bias=constv[0:1, COL_M5LN2:COL_M5LN2 + 1]),
             ss2)
        wait("v", ss2)
        emit("v", lambda e: e.tensor_scalar(srow2[0:1, 0:16], srow2[0:1, 0:16],
                                            -1.0, 1.0, ALU.mult, ALU.add), sv2)
        wait("s", sv2)
        emit("s", lambda e: e.activation(ldrow[0:1, :], srow2[0:1, 0:16],
                                         AF.Ln), ss2)
        wait("t", ss2)
        wait("t", p6f)
        emit("t", lambda e: e.matmul(pb[6][0:128, 0:16], onesr[0:1, :],
                                     ldrow[0:1, :], start=True, stop=True), st2)
        wait("v", st2)
        emit("v", lambda e: e.tensor_copy(ldb[:, :], pb[6][0:128, 0:16]),
             (sv2, p6f))

        wga = al.buf([128, 8, 128], BF16, "awga")
        gtile = [al.buf([128, 512], BF16, f"agt{i}") for i in range(2)]
        swl = S("aswl")
        gpr = [S(f"agpr{i}") for i in range(2)]
        gpf = [S(f"agpf{i}") for i in range(2)]
        gsr = [S(f"agsr{i}") for i in range(2)]
        gsf = [S(f"agsf{i}") for i in range(2)]
        emit("s", lambda e: e.preload_act_table(AF.Silu), ctl=True)

        def wslice(w_i, pi):
            return w_i.rearrange("(c p) f -> p c f", p=128)[:, :,
                                                            pi * 128:(pi + 1) * 128]

        for pi in range(NP):
            wait("g", swl, swl.n)
            waitr("g", xlrec[0])
            dma("g", wga[:, :, :], wslice(wgaT_i, pi), swl)
            wait("t", swl)
            waitr("t", xlrec[0])
            for tc in range(NTC):
                gb = tc % 2
                waitr("t", gpfree.get(gb))
                for d in range(8):
                    emit("t", lambda e, gb=gb, d=d, tc=tc: e.matmul(
                        pb[gb][:, :], wga[:, d, :],
                        xln2fm[:, d, tc * 512:(tc + 1) * 512],
                        start=(d == 0), stop=(d == 7)),
                        gpr[gb] if d == 7 else ())
                wait("s", gpr[gb])
                wait("s", gsf[gb])
                emit("s", lambda e, gb=gb, pi=pi: e.activation(
                    gtile[gb][:, :], pb[gb][:, :], AF.Silu,
                    bias=bfm[:, CB_G + pi:CB_G + pi + 1]), (gsr[gb], gpf[gb]))
                gpfree[gb] = (gpf[gb], gpf[gb].n)
                wait("g", gsr[gb])
                dma("g", gate_sc[pi, tc], gtile[gb][:, :], gsf[gb])

        emit("s", load_lnexp, ctl=True)
        Qp = al.buf([128, T], BF16, "aQp")
        Kp = al.buf([128, T], BF16, "aKp")
        Vaug = [al.buf([128, 32 * 80], BF16, f"aVa{i}") for i in range(2)]
        og = al.buf([128, T], F32, "aog")
        attnT = [al.buf([128, 512], BF16, f"aaT{i}") for i in range(2)]
        Dmix = al.buf([128, 2048], BF16, "aDmix")
        gtl = al.buf([128, 512], F32, "agtl")
        fbt = al.buf([128, 28], F32, "afbt")
        qraw = al.buf([128, 512], F32, "aqraw")
        qrot = al.buf([128, 512], F32, "aqrot")
        tr1 = al.buf([128, 512], F32, "atr1")
        vbf = [al.buf([128, 512], BF16, f"avbf{i}") for i in range(2)]
        vbfree = {}
        vrec = [None]
        nmbuf = al.buf([128, 512], F32, "anm")
        tfut = al.buf([128, 512], F32, "atfut")
        ont = al.buf([128, 512], F32, "aont")
        gont = [al.buf([128, 512], BF16, f"agont{i}") for i in range(2)]
        gin = [al.buf([128, 512], BF16, f"agin{i}") for i in range(2)]
        wqb = al.buf([128, 8, 128], BF16, "awqb")
        wkb = al.buf([128, 8, 128], BF16, "awkb")
        wvb = al.buf([128, 8, 128], BF16, "awvb")
        sv_ones = S("asvones")
        for hh in range(2):
            emit("v", lambda e, hh=hh: e.memset(
                Vaug[hh].rearrange("p (k c) -> p k c", c=80)[:, :, 64:65], 1.0),
                sv_ones)

        swqkv = S("aswqkv")
        ppr = [S(f"appr{i}") for i in range(2)]
        ppf = [S(f"appf{i}") for i in range(2)]
        sQf = S("asQf")
        vbr = [S(f"avbr{i}") for i in range(2)]
        ssr = [S(f"assr{i}") for i in range(2)]
        ssf = [S(f"assf{i}") for i in range(2)]
        atr = [S(f"aatr{i}") for i in range(2)]
        atf = [S(f"aatf{i}") for i in range(2)]
        sxm = S("asxm")
        avp = S("aavp")
        avf = S("aavf")
        p4free = S("ap4free")
        p5free = S("ap5free")
        smrg = S("asmrg")
        shead = S("ashead")
        svhead = S("asvhead")
        sgn = S("asgn")
        oggate = S("aoggate")
        ginl = [S(f"aginl{i}") for i in range(2)]
        ginf = [S(f"aginf{i}") for i in range(2)]
        gonr = [S(f"agonr{i}") for i in range(2)]
        gonf = [S(f"agonf{i}") for i in range(2)]

        wait("v", scs)
        for pi in range(NP):
            wait("g", swqkv, swqkv.n)
            dma("g", wqb[:, :, :], wslice(wqT_i, pi), swqkv)
            dma("g", wkb[:, :, :], wslice(wkT_i, pi), swqkv)
            dma("g", wvb[:, :, :], wslice(wvT_i, pi), swqkv)
            wait("t", swqkv)
            for proj in range(3):
                wb_ = (wqb, wkb, wvb)[proj]
                bcol = (CB_Q, CB_K, CB_V)[proj]
                dstfm = (Qp, Kp, None)[proj]
                for tc in range(NTC):
                    pbk = tc % 2
                    waitr("t", ppfree.get(pbk))
                    for d in range(8):
                        emit("t", lambda e, pbk=pbk, d=d, tc=tc, wb_=wb_:
                             e.matmul(pb[pbk][:, :], wb_[:, d, :],
                                      xln2fm[:, d, tc * 512:(tc + 1) * 512],
                                      start=(d == 0), stop=(d == 7)),
                             ppr[pbk] if d == 7 else ())
                    wait("v", ppr[pbk])
                    if proj < 2:
                        emit("v", lambda e, pbk=pbk, pi=pi, bcol=bcol:
                             e.tensor_scalar_add(qraw[:, :], pb[pbk][:, :],
                                                 bfm[:, bcol + pi:bcol + pi + 1]),
                             ppf[pbk])
                        ppfree[pbk] = (ppf[pbk], ppf[pbk].n)
                        mask = [i + 1 if i % 2 == 0 else i - 1 for i in range(32)]
                        sep("v")
                        emit("v", lambda e: e.stream_shuffle(qrot[:, :], qraw[:, :],
                                                             mask=mask))
                        emit("v", lambda e, tc=tc: e.tensor_mul(
                            tr1[:, :], qraw[:, :], cosfm[:, tc * 512:(tc + 1) * 512]))
                        sep("v")
                        emit("v", lambda e, tc=tc: e.tensor_mul(
                            qrot[:, :], qrot[:, :], sinmfm[:, tc * 512:(tc + 1) * 512]))
                        sep("v")
                        emit("v", lambda e, tc=tc, dstfm=dstfm: e.tensor_add(
                            dstfm[:, tc * 512:(tc + 1) * 512], tr1[:, :], qrot[:, :]), sQf)
                    else:
                        vsl = tc % 2
                        waitr("v", vbfree.get(vsl))
                        emit("v", lambda e, pbk=pbk, pi=pi, vsl=vsl:
                             e.tensor_scalar_add(vbf[vsl][:, :], pb[pbk][:, :],
                                                 bfm[:, CB_V + pi:CB_V + pi + 1]),
                             (ppf[pbk], vbr[vsl]))
                        ppfree[pbk] = (ppf[pbk], ppf[pbk].n)
                        waitr("t", (S("avm"), S("avm").n))
                        for blk in range(4):
                            kt = tc * 4 + blk
                            for hh in range(2):
                                idsl = (identb[0:64, 0:64] if hh == 0
                                        else identb[64:128, 64:128])
                                vrec[0] = pe_tr(
                                    Vaug[hh][:, kt * 80:kt * 80 + 64],
                                    vbf[vsl][hh * 64:(hh + 1) * 64,
                                             blk * 128:(blk + 1) * 128],
                                    2 + (blk % 2), idsl, 64, 128)
                        vbfree[vsl] = (S("atm"), S("atm").n)
            for hh in range(2):
                base = 64 * hh
                hcol = pi * 2 + hh
                wait("v", sv2)
                emit("v", lambda e, hcol=hcol: e.tensor_scalar(
                    fbt[:, :], fbias[:, :], ldb[:, hcol:hcol + 1], None,
                    ALU.mult, ALU.bypass), svhead)
                wait("s", svhead)
                wait("s", atf[0], atf[0].n)
                wait("s", atf[1], atf[1].n)
                wait("s", smrg, smrg.n)
                emit("s", lambda e, hcol=hcol: e.activation(
                    Dmix[:, :], relum[:, :], AF.Exp,
                    scale=ldb[:, hcol:hcol + 1]), shead)
                emit("s", lambda e, hcol=hcol: e.activation(
                    gtl[:, :], qexp[:, :], AF.Exp,
                    scale=ldb[:, hcol:hcol + 1]), shead)
                wait("t", sQf, sQf.n)
                waitr("t", vrec[0])
                for qc in range(NTC):
                    lastpast = qc * 4 + 3

                    def emit_av(kt):
                        par = kt % 2
                        is_past = kt <= lastpast
                        dstb = 4 if is_past else 5
                        first_mm = (kt == 0) or (kt == lastpast + 1)
                        stop_mm = (kt == lastpast) or (kt == 31)
                        if first_mm:
                            wait("t", p4free if is_past else p5free)
                        rs, rv = atrec[kt]
                        wait("t", rs, rv)
                        incs = [atf[par]]
                        if stop_mm:
                            incs.append(avp if is_past else avf)
                        emit("t", lambda e, dstb=dstb, hh=hh, kt=kt, par=par,
                             first_mm=first_mm, stop_mm=stop_mm: e.matmul(
                                 pb[dstb][0:65, :],
                                 Vaug[hh][:, kt * 80:kt * 80 + 65],
                                 attnT[par][:, :], start=first_mm,
                                 stop=stop_mm), incs)

                    atrec = {}
                    for kt in range(32):
                        par = kt % 2
                        bank = 2 + par
                        dm = kt * 128 - qc * 512
                        waitr("t", ssfree.get(par))
                        emit("t", lambda e, bank=bank, base=base, kt=kt, qc=qc:
                             e.matmul(pb[bank][:, :],
                                      Kp[base:base + 64, kt * 128:(kt + 1) * 128],
                                      Qp[base:base + 64, qc * 512:(qc + 1) * 512],
                                      start=True, stop=True), ssr[par])
                        wait("s", ssr[par])
                        wait("s", atf[par])
                        if dm <= -128:
                            emit("s", lambda e, bank=bank, par=par: e.activation(
                                attnT[par][:, :], pb[bank][:, :], AF.Exp,
                                scale=0.125), (atr[par], ssf[par]))
                            ssfree[par] = (ssf[par], ssf[par].n)
                            atrec[kt] = (atr[par], atr[par].n)
                        elif dm <= 384:
                            emit("s", lambda e, bank=bank, par=par: e.activation(
                                attnT[par][:, :], pb[bank][:, :], AF.Exp,
                                scale=0.125), (sxm, ssf[par]))
                            ssfree[par] = (ssf[par], ssf[par].n)
                            wait("v", sxm)
                            wait("v", shead)
                            m = dm // 128
                            emit("v", lambda e, par=par, m=m: e.tensor_mul(
                                attnT[par][:, :], attnT[par][:, :],
                                Dmix[:, m * 512:(m + 1) * 512]), S("avm"))
                            atrec[kt] = (S("avm"), S("avm").n)
                        else:
                            m = (dm - 512) // 128
                            wait("s", shead)
                            emit("s", lambda e, bank=bank, par=par, m=m:
                                 e.activation(attnT[par][:, :], pb[bank][:, :],
                                              AF.Exp, scale=0.125,
                                              bias=fbt[:, m:m + 1]),
                                 (atr[par], ssf[par]))
                            ssfree[par] = (ssf[par], ssf[par].n)
                            atrec[kt] = (atr[par], atr[par].n)
                        if kt >= 1:
                            emit_av(kt - 1)
                    emit_av(31)
                    has_fut = lastpast < 31
                    wait("v", avp)
                    if has_fut:
                        wait("v", avf)
                        wait("v", shead)
                        emit("v", lambda e: e.tensor_mul(
                            tfut[0:65, :], pb[5][0:65, :], gtl[0:65, :]),
                            p5free)
                        sep("v")
                        emit("v", lambda e: e.tensor_add(
                            nmbuf[0:65, :], pb[4][0:65, :], tfut[0:65, :]),
                            (smrg, p4free))
                    else:
                        emit("v", lambda e: e.tensor_scalar_add(
                            nmbuf[0:65, :], pb[4][0:65, :], 0.0),
                            (smrg, p4free))
                    sep("v")
                    emit("v", lambda e: e.reciprocal(srow[0:1, :],
                                                     nmbuf[64:65, :]), smrg)
                    wait("t", smrg)
                    wait("t", p7f)
                    emit("t", lambda e: e.matmul(pb[7][0:64, :],
                                                 onesr[0:1, 0:64], srow[0:1, :],
                                                 start=True, stop=True), st2)
                    wait("v", st2)
                    emit("v", lambda e, base=base, qc=qc: e.tensor_mul(
                        og[base:base + 64, qc * 512:(qc + 1) * 512],
                        nmbuf[0:64, :], pb[7][0:64, :]), (oggate, p7f))
            wait("v", oggate, oggate.n)
            for c in range(8):
                emit("v", lambda e, c=c: e.bn_stats(
                    bnb[:, 6 * c:6 * c + 6], og[:, c * 512:(c + 1) * 512]))
            sep("v")
            emit("v", lambda e: e.bn_aggr(stat2[:, 0:2], bnb[:, 0:48]), sgn)
            wait("t", sgn)
            wait("t", p7f)
            emit("t", lambda e: e.matmul(pb[7][0:1, 0:128], stat2[:, 0:1],
                                         identf[:, :], start=True, stop=True))
            emit("t", lambda e: e.matmul(pb[7][0:1, 128:256], stat2[:, 1:2],
                                         identf[:, :], start=True, stop=True),
                 st2)
            wait("v", st2)
            emit("v", lambda e: e.reduce_sum(cells[0:1, C_MUA:C_MUA + 1],
                                             pb[7][0:1, 0:64], axis=AX))
            emit("v", lambda e: e.reduce_sum(cells[0:1, C_MUB:C_MUB + 1],
                                             pb[7][0:1, 64:128], axis=AX))
            emit("v", lambda e: e.reduce_sum(cells[0:1, C_VA:C_VA + 1],
                                             pb[7][0:1, 128:192], axis=AX))
            emit("v", lambda e: e.reduce_sum(cells[0:1, C_VB:C_VB + 1],
                                             pb[7][0:1, 192:256], axis=AX))
            emit("v", lambda e: e.tensor_copy(srow[0:1, 0:128],
                                              pb[7][0:1, 0:128]), p7f)
            sep("v")
            emit("v", lambda e: e.tensor_mul(srow[0:1, 0:128],
                                             srow[0:1, 0:128],
                                             srow[0:1, 0:128]))
            sep("v")
            emit("v", lambda e: e.reduce_sum(cells[0:1, C_T0:C_T0 + 1],
                                             srow[0:1, 0:64], axis=AX))
            emit("v", lambda e: e.reduce_sum(cells[0:1, C_T1:C_T1 + 1],
                                             srow[0:1, 64:128], axis=AX))
            emit("v", lambda e: e.tensor_scalar_mul(cells[0:1, C_MUA:C_MUB + 1],
                                                    cells[0:1, C_MUA:C_MUB + 1],
                                                    1.0 / 64))
            sep("v")
            emit("v", lambda e: e.tensor_add(cells[0:1, C_VA:C_VB + 1],
                                             cells[0:1, C_VA:C_VB + 1],
                                             cells[0:1, C_T0:C_T1 + 1]))
            sep("v")
            emit("v", lambda e: e.tensor_scalar_mul(cells[0:1, C_VA:C_VB + 1],
                                                    cells[0:1, C_VA:C_VB + 1],
                                                    1.0 / 64))
            emit("v", lambda e: e.tensor_mul(cells[0:1, C_T0:C_T1 + 1],
                                             cells[0:1, C_MUA:C_MUB + 1],
                                             cells[0:1, C_MUA:C_MUB + 1]))
            sep("v")
            emit("v", lambda e: e.tensor_sub(cells[0:1, C_VA:C_VB + 1],
                                             cells[0:1, C_VA:C_VB + 1],
                                             cells[0:1, C_T0:C_T1 + 1]), sgn)
            wait("s", sgn)
            rsqrt_row(cells[0:1, C_RA:C_RB + 1], cells[0:1, C_VA:C_VB + 1],
                      srow[0:1, 0:2], srow2[0:1, 0:2], sgn, ss2)
            wait("t", sgn)
            wait("t", p6f)
            emit("t", lambda e: e.matmul(pb[6][0:128, 0:1], maskA[0:1, :],
                                         cells[0:1, C_RA:C_RA + 1],
                                         start=True, stop=False))
            emit("t", lambda e: e.matmul(pb[6][0:128, 0:1], maskB[0:1, :],
                                         cells[0:1, C_RB:C_RB + 1],
                                         start=False, stop=True))
            emit("t", lambda e: e.matmul(pb[6][0:128, 1:2], maskA[0:1, :],
                                         cells[0:1, C_MUA:C_MUA + 1],
                                         start=True, stop=False))
            emit("t", lambda e: e.matmul(pb[6][0:128, 1:2], maskB[0:1, :],
                                         cells[0:1, C_MUB:C_MUB + 1],
                                         start=False, stop=True), st2)
            wait("v", st2)
            emit("v", lambda e: e.tensor_copy(rvec[:, :], pb[6][0:128, 0:1]))
            emit("v", lambda e: e.tensor_copy(mvec[:, :], pb[6][0:128, 1:2]),
                 p6f)
            sep("v")
            emit("v", lambda e, pi=pi: e.tensor_mul(
                Avec[:, :], rvec[:, :], bfm[:, CB_GAM + pi:CB_GAM + pi + 1]))
            sep("v")
            emit("v", lambda e: e.tensor_mul(tvec[:, :], mvec[:, :],
                                             Avec[:, :]))
            sep("v")
            emit("v", lambda e, pi=pi: e.tensor_sub(
                Bvec[:, :], bfm[:, CB_BET + pi:CB_BET + pi + 1], tvec[:, :]),
                sgn)
            for tc in range(NTC):
                gsl = tc % 2
                wait("g", ginf[gsl])
                wait("g", gsf[0], gsf[0].n)
                wait("g", gsf[1], gsf[1].n)
                dma("g", gin[gsl][:, :], gate_sc[pi, tc], ginl[gsl])
                wait("v", ginl[gsl])
                wait("v", sgn, sgn.n)
                wait("v", gonf[gsl])
                emit("v", lambda e, tc=tc: e.tensor_scalar(
                    ont[:, :], og[:, tc * 512:(tc + 1) * 512], Avec[:, :],
                    Bvec[:, :], ALU.mult, ALU.add))
                sep("v")
                emit("v", lambda e, gsl=gsl: e.tensor_mul(
                    gont[gsl][:, :], ont[:, :], gin[gsl][:, :]),
                    (gonr[gsl], ginf[gsl]))
                wait("g", gonr[gsl])
                dma("g", gon_sc[pi, tc], gont[gsl][:, :], gonf[gsl])

        al.reset(Sm)
        gin8 = [al.buf([128, 8, 512], BF16, f"pgin8{i}") for i in range(2)]
        xr1 = [al.buf([128, E], F32, f"pxr1{i}") for i in range(2)]
        x2t = [al.buf([128, E], F32, f"px2t{i}") for i in range(2)]
        lnt5 = al.buf([128, E], F32, "plnt5")
        t5 = al.buf([128, 512], F32, "pt5")
        g8l = [S(f"pg8l{i}") for i in range(2)]
        g8f = [S(f"pg8f{i}") for i in range(2)]
        x1l = [S(f"px1l{i}") for i in range(2)]
        x1f = [S(f"px1f{i}") for i in range(2)]
        x2fr = [S(f"px2fr{i}") for i in range(2)]
        por = [S(f"ppor{i}") for i in range(4)]
        pof = [S(f"ppof{i}") for i in range(4)]
        swo = S("pswo")
        pofree = {}
        p5stat = S("p5stat")
        x2gate = Gate()
        wait("g", sQf, sQf.n)
        waitr("g", vrec[0])
        for pi in range(NP):
            dma("g", woTb[:, pi, :],
                woT_i[pi * 128:(pi + 1) * 128, :], swo)
        for tc in range(NTC):
            gsl = tc % 2
            wait("g", g8f[gsl])
            wait("g", gonf[0], gonf[0].n)
            wait("g", gonf[1], gonf[1].n)
            for pi in range(NP):
                dma("g", gin8[gsl][:, pi, :], gon_sc[pi, tc], g8l[gsl])
            wait("t", g8l[gsl])
            if tc == 0:
                wait("t", swo)
            for tti in range(4):
                tt = tc * 4 + tti
                bsel = tt % 2
                bl, br = 2 * bsel, 2 * bsel + 1
                waitr("t", pofree.get(bl))
                waitr("t", pofree.get(br))
                for pi in range(NP):
                    for half in range(2):
                        bk = bl if half == 0 else br
                        incs = []
                        if pi == NP - 1:
                            incs.append(por[bk])
                            if tti == 3 and half == 1:
                                incs.append(g8f[gsl])
                        emit("t", lambda e, bk=bk, pi=pi, tti=tti, half=half,
                             gsl=gsl: e.matmul(
                                 pb[bk][:, :],
                                 gin8[gsl][:, pi, tti * 128:(tti + 1) * 128],
                                 woTb[:, pi, half * 512:(half + 1) * 512],
                                 start=(pi == 0), stop=(pi == NP - 1)),
                             incs)
                xsl = tt % 2
                wait("y", x1f[xsl])
                dma("y", xr1[xsl][:, :], x1_sc[tt * 128:(tt + 1) * 128, :],
                    x1l[xsl])
                wait("v", x1l[xsl])
                wait("v", por[bl])
                wait("v", por[br])
                wait("v", x2fr[xsl])
                for half, bk in ((0, bl), (1, br)):
                    sep("v")
                    emit("v", lambda e, bk=bk, half=half: e.tensor_add(
                        t5[:, :], pb[bk][:, :],
                        borep[:, half * 512:(half + 1) * 512]), pof[bk])
                    pofree[bk] = (pof[bk], pof[bk].n)
                    sep("v")
                    emit("v", lambda e, xsl=xsl, half=half: e.tensor_add(
                        x2t[xsl][:, half * 512:(half + 1) * 512], t5[:, :],
                        xr1[xsl][:, half * 512:(half + 1) * 512]),
                        (x1f[xsl] if half == 1 else ()))
                sep("v")
                emit("v", lambda e, xsl=xsl: e.bn_stats(stat2[:, 0:6],
                                                        x2t[xsl][:, 0:512]))
                emit("v", lambda e, xsl=xsl: e.bn_stats(stat2[:, 6:12],
                                                        x2t[xsl][:, 512:1024]))
                sep("v")
                emit("v", lambda e: e.bn_aggr(bnb[:, 0:2], stat2[:, 0:12]))
                sep("v")
                emit("v", lambda e, tt=tt: e.tensor_copy(mu_a[:, tt:tt + 1],
                                                         bnb[:, 0:1]))
                emit("v", lambda e, tt=tt: e.tensor_copy(vv_a[:, tt:tt + 1],
                                                         bnb[:, 1:2]))
                sep("v")
                emit("v", lambda e, xsl=xsl, tt=tt: e.tensor_scalar(
                    lnt5[:, :], x2t[xsl][:, :], mu_a[:, tt:tt + 1], None,
                    ALU.subtract, ALU.bypass))
                sep("v")
                emit("v", lambda e, tt=tt: e.reduce_max(
                    mraw[:, tt:tt + 1], lnt5[:, :], axis=AX,
                    apply_absolute_value=True), p5stat)
                wait("y", p5stat)
                dma("y", x2_sc[tt * 128:(tt + 1) * 128, :], x2t[xsl][:, :],
                    x2fr[xsl])
                if stage == 2:
                    dma("y", out_t[tt * 128:(tt + 1) * 128, :], x2t[xsl][:, :],
                        x2fr[xsl])
                x2gate.per[tt] = (x2fr[xsl], x2fr[xsl].n)
        x2gate.full = [(x2fr[0], x2fr[0].n), (x2fr[1], x2fr[1].n)]
        al.reset(Sm)
        return x2gate, p5stat

    upto = "z"
    if isinstance(stage, str) and len(stage) == 2:
        upto = stage[1]
        stage = 1
    x1done, mmC1, sstore1 = ffn(
        "f1", x_in, x1_sc, 0, 1, first=True, ln_next=True, have_stats=False,
        dst2_ap=(out_t if stage == 1 else None), upto=upto)
    if stage == 1:
        if x1done is None:
            for sm in list(sems.values()):
                wait("y", sm)
        else:
            wait_gate("y", x1done)
        if _DUMP:
            dbg_t = nc.dram_tensor("dbg", [128, 128], F32,
                                   kind="ExternalOutput").ap()
            for sm in list(sems.values()):
                wait("y", sm)
            sdbg = S("dbgs")
            dma("y", dbg_t[0:1, 0:9], cells[0:1, 0:9], sdbg)
            dma("y", dbg_t[:, 32:36], rsc[:, 0:4], sdbg)
            dma("y", dbg_t[:, 40:62], svg[:, :], sdbg)
            dma("y", dbg_t[:, 62:84], svu[:, :], sdbg)
            dma("y", dbg_t[:, 84:106], sgate[:, :], sdbg)
            dma("y", dbg_t[:, 106:128], sup[:, :], sdbg)
            wait("y", sdbg)
        P.flush()
        return nc

    x2done, p5stat = attn_phase(x1done, mmC1, sstore1)
    if stage == 2:
        wait_gate("y", x2done)
        P.flush()
        return nc

    outdone, _, _ = ffn(
        "f2", x2_sc, out_t, 2, 3, first=False, ln_next=False, have_stats=True,
        src_gate=x2done, stats_gate=p5stat)
    wait_gate("y", outdone)
    P.flush()
    return nc


def _host_prep(inputs):
    f32 = np.float32

    def C(a):
        return np.ascontiguousarray(np.asarray(a), dtype=f32)

    x = C(inputs["L0_attn_x"])
    wq = C(inputs["L0_attn_wq"])
    wk = C(inputs["L0_attn_wk"])
    wv = C(inputs["L0_attn_wv"])
    wg = C(inputs["L0_attn_wg"])
    wo = C(inputs["L0_attn_wo"])
    wd = C(inputs["L0_attn_wd"])
    gamma = C(inputs["L0_attn_gamma"])
    beta = C(inputs["L0_attn_beta"])
    bq = C(inputs["L0_attn_bq"])
    bk = C(inputs["L0_attn_bk"])
    bv = C(inputs["L0_attn_bv"])
    bg = C(inputs["L0_attn_bg"])
    bo = C(inputs["L0_attn_bo"])
    bd = C(inputs["L0_attn_bd"])
    w_gate = C(inputs["L0_ffn_w_gate"])
    w_up = C(inputs["L0_ffn_w_up"])
    w_down = C(inputs["L0_ffn_w_down"])
    clip_x = float(np.asarray(inputs["L0_ffn_clip_x"]).ravel()[0])
    clip_down = float(np.asarray(inputs["L0_ffn_clip_down"]).ravel()[0])

    def fmcols(vec):
        return vec.reshape(8, 128).T

    bfm = np.concatenate([fmcols(v) for v in (bq, bk, bv, bg, gamma, beta)],
                         axis=1)

    hd = np.arange(128) % 64
    inv = 1.0 / 10000.0 ** (np.arange(0, 32, dtype=np.float64) * 2 / 64)
    freq = inv[hd % 32]
    pos = np.arange(T, dtype=np.float64)
    ang = pos[None, :] * freq[:, None]
    cosfm = np.cos(ang)
    sign = np.where(hd % 2 == 0, -1.0, 1.0)
    sinmfm = np.sin(ang) * sign[:, None]

    i_ = np.arange(128)[:, None]
    j_ = np.arange(512)[None, :]
    relum = np.concatenate(
        [np.maximum(128 * m + i_ - j_, 0).astype(f32) for m in range(4)],
        axis=1)
    qexpc = np.broadcast_to((512.0 - j_).astype(f32), (128, 512)).copy()
    fbias = (np.arange(128)[:, None] + 128.0 * np.arange(28)[None, :])
    onesr = np.ones((1, 128), f32)
    maskA = np.concatenate([np.ones((1, 64)), np.zeros((1, 64))], 1)
    maskB = np.concatenate([np.zeros((1, 64)), np.ones((1, 64))], 1)
    identf = np.eye(128, dtype=f32)
    constv = np.zeros((128, 4), f32)
    constv[:, 0] = EPS
    constv[:, 1] = -5.0 * LN2

    shared = {
        "wg_n": w_gate, "wu_n": w_up,
        "wgT": C(w_gate.T), "wuT": C(w_up.T),
        "wd_n": w_down, "wdT": C(w_down.T),
        "wqT": C(wq.T), "wkT": C(wk.T), "wvT": C(wv.T), "wgaT": C(wg.T),
        "woT": C(wo.T), "wdec": C(wd.T),
        "bfm": C(bfm), "bo_r": bo.reshape(1, E).copy(),
        "bd_r": bd.reshape(1, 16).copy(),
        "clip2": np.array([[clip_x, clip_down]], f32),
        "cosfm": C(cosfm), "sinmfm": C(sinmfm), "relum": relum,
        "qexp": qexpc, "fbias": C(fbias), "onesr": onesr,
        "maskA": C(maskA), "maskB": C(maskB), "identf": identf,
        "identb": identf.astype(ml_dtypes.bfloat16),
        "constv": constv,
    }
    in_maps = []
    for b in range(8):
        m = dict(shared)
        m["x"] = np.ascontiguousarray(x[b])
        in_maps.append(m)
    return in_maps


def kernel(**inputs):
    in_maps = _host_prep(inputs)
    nc = build_program(STAGE)
    res = run_bass_kernel_spmd(nc, in_maps, core_ids=list(range(8)))
    out = np.stack([np.asarray(res.results[i]["out"], dtype=np.float32)
                    for i in range(8)], axis=0)
    return out
